# version 01
# baseline (speedup 1.0000x reference)
import contextlib
import numpy as np
import concourse.bass as bass
import concourse.mybir as mybir
from concourse.bass_utils import run_bass_kernel_spmd

F32 = mybir.dt.float32
F32R = mybir.dt.float32r
BF16 = mybir.dt.bfloat16
AF = mybir.ActivationFunctionType
OP = mybir.AluOpType
AX = mybir.AxisListType.X

R = 8
T = 1024
D = 2048
NCD = 16
NH = 16
S = 8192
NE = 16
KSEL = 512
INV_SQRT_DH = 0.08838834764831845
EXP_SHIFT = -16.0
N_BISECT = 45


class Sched:
    def __init__(self):
        self.items = []
        self.counts = {}

    def emit(self, eng, fn, waits=(), inc=None):
        incs = []
        if inc is not None:
            sem, amt = inc
            self.counts[sem] = self.counts.get(sem, 0) + amt
            incs.append((sem, amt))
        self.items.append((eng, [w for w in waits if w is not None], fn, incs))

    def val(self, sem):
        return self.counts.get(sem, 0)


class _StopEmit(Exception):
    pass


def build_program(debug_taps=(), stop_after=None, debug_dram=False, h1_mode=3, h1_upto=16, qkv_upto=48, qkv_mode=3, heads_upto=16, f_mode=3):
    nc = bass.Bass("TRN2", target_bir_lowering=False, detect_race_conditions=False)
    debug_taps = list(debug_taps)
    dbg_kind = "ExternalOutput" if debug_dram else "Internal"

    x_in = nc.dram_tensor("x_slice", [T, D], F32, kind="ExternalInput").ap()
    wqkv = nc.dram_tensor("w_qkv", [D, 3 * D], F32, kind="ExternalInput").ap()
    wo = nc.dram_tensor("w_o", [D, D], F32, kind="ExternalInput").ap()
    rwd = nc.dram_tensor("router_w", [D, NE], F32, kind="ExternalInput").ap()
    rbd = nc.dram_tensor("router_b", [NE, 1], F32, kind="ExternalInput").ap()
    wed = nc.dram_tensor("expert_w", [D, D], F32, kind="ExternalInput").ap()
    bed = nc.dram_tensor("expert_b", [D], F32, kind="ExternalInput").ap()
    cvd = nc.dram_tensor("c_vec", [D], F32, kind="ExternalInput").ap()
    wmodd = nc.dram_tensor("wmod_slice", [D, 1536], F32, kind="ExternalInput").ap()
    identd = nc.dram_tensor("ident", [128, 128], F32, kind="ExternalInput").ap()
    selBd = nc.dram_tensor("selB", [128, 16], F32, kind="ExternalInput").ap()
    sel2d = nc.dram_tensor("sel2", [16, 128], F32, kind="ExternalInput").ap()
    out_d = nc.dram_tensor("out_slice", [T, D], F32, kind="ExternalOutput").ap()

    kag_in = nc.dram_tensor("kag_in", [2, NH, 128, T], BF16)
    kag_dbg = (nc.dram_tensor("kag_dbg", [2, NH, 128, T], BF16,
               kind="ExternalOutput") if debug_dram else None)
    cs_dbg = (nc.dram_tensor("cs_dbg", [NH, T], F32,
              kind="ExternalOutput") if debug_dram else None)
    cs0_dbg = (nc.dram_tensor("cs0_dbg", [4, 2048], F32,
               kind="ExternalOutput") if debug_dram else None)
    z0_dbg = (nc.dram_tensor("z0_dbg", [128, 16], F32,
              kind="ExternalOutput") if debug_dram else None)
    e0_dbg = (nc.dram_tensor("e0_dbg", [128, S], BF16,
              kind="ExternalOutput") if debug_dram else None)
    z2_dbg = (nc.dram_tensor("z2_dbg", [128, 24], F32,
              kind="ExternalOutput") if debug_dram else None)
    kag_out = nc.dram_tensor("kag_out", [R, 2, NH, 128, T], BF16, addr_space="Shared")
    q_sp = nc.dram_tensor("q_sp", [2, NH, 128, T], BF16, kind=dbg_kind)
    v_sp = nc.dram_tensor("v_sp", [2, NH, 128, T], BF16, kind=dbg_kind)
    val_sp = nc.dram_tensor("val_sp", [2, NH, 128, T], BF16, kind=dbg_kind)
    xT_sp = nc.dram_tensor("xT_sp", [NCD, 128, T], F32, kind=dbg_kind)
    x1_sp = nc.dram_tensor("x1_sp", [NCD, 128, T], F32, kind=dbg_kind)
    cs_rs_in = nc.dram_tensor("cs_rs_in", [R, NH, T], F32)
    cs_rs_out = nc.dram_tensor("cs_rs_out", [NH, T], F32)
    s_ag_in = nc.dram_tensor("s_ag_in", [NE, T], F32)
    s_ag_out = nc.dram_tensor("s_ag_out", [R, NE, T], F32, addr_space="Shared")
    mod_ag_in = nc.dram_tensor("mod_ag_in", [1536], F32)
    mod_ag_out = nc.dram_tensor("mod_ag_out", [R, 1536], F32, addr_space="Shared")

    dbg = {}
    for name, shape in debug_taps:
        dbg[name] = nc.dram_tensor("dbg_" + name, list(shape), F32,
                                   kind="ExternalOutput").ap()

    p = Sched()

    def sb(name, shape, dtype):
        return nc.alloc_sbuf_tensor(name, shape, dtype)

    ident_s = sb("ident_s", [128, 128], F32)
    selB_s = sb("selB_s", [128, 16], F32)
    sel2_s = sb("sel2_s", [16, 128], F32)
    ones_row = sb("ones_row", [1, 128], F32)
    ones_col = sb("ones_col", [128, 1], F32)
    ones16 = sb("ones16", [16, 1], F32)
    cstat = sb("cstat", [128, 1], F32)
    biasC = sb("biasC", [128, 1], F32)
    epsb = sb("epsb", [1, 1], F32)
    c512 = sb("c512", [16, 1], F32)
    half16 = sb("half16", [16, 1], F32)

    XK = sb("XK", [128, 16 * T], F32)
    h1h_t = sb("h1h", [128, 16 * T], BF16)
    h1l_t = sb("h1l", [128, 16 * T], BF16)
    wstg = sb("wstg", [128, D], F32)
    wh_t = sb("wh", [128, 2 * D], BF16)
    wl_t = sb("wl", [128, 2 * D], BF16)
    scr32 = sb("scr32", [128, T], F32)
    scr32b = sb("scr32b", [128, T], F32)
    scrh = sb("scrh", [128, 2 * T], BF16)
    scrl = sb("scrl", [128, 2 * T], BF16)
    qh_t = sb("qh_t", [128, 2 * T], BF16)
    ql_t = sb("ql_t", [128, 2 * T], BF16)
    zpart = sb("zpart", [128, 8], F32)
    zrowv = sb("zrowv", [128, 1], F32)
    zrecip = sb("zrecip", [128, 1], F32)
    zrh_b = sb("zrh_b", [128, 1], BF16)
    zrl_b = sb("zrl_b", [128, 1], BF16)
    zrselH = [sb("zrselH%d" % m, [128, 4], BF16) for m in range(4)]
    zrselL = [sb("zrselL%d" % m, [128, 4], BF16) for m in range(4)]
    modT = sb("modT", [128, 96], F32)
    modp = sb("modp", [128, 96], F32)
    cv_sb = sb("cv_sb", [16, 128], F32)
    cT_silu = sb("cT_silu", [128, 16], F32)
    mod_all = sb("mod_all", [96, 128], F32)
    stat_all = sb("stat_all", [1, 2 * T], F32)
    xstg = sb("xstg", [128, D], F32)
    xstg2 = sb("xstg2", [128, D], F32)
    rwh = sb("rwh", [128, 256], BF16)
    rwl = sb("rwl", [128, 256], BF16)
    rbT = sb("rbT", [16, 1], F32)
    beT = sb("beT", [128, 16], F32)
    smax = sb("smax", [128, 384], F32)
    thr128 = sb("thr128", [128, 1], F32)
    lo16 = sb("lo16", [16, 1], F32)
    hi16 = sb("hi16", [16, 1], F32)
    mid16 = sb("mid16", [16, 1], F32)
    pred16 = sb("pred16", [16, 1], mybir.dt.int32)
    sum16 = sb("sum16", [16, 1], F32)
    rc_t = sb("rc_t", [128, 1], F32)

    psW0 = nc.alloc_psum_tensor("psW0", [128, 1024], F32)
    psW1 = nc.alloc_psum_tensor("psW1", [128, 1024], F32)
    ps0 = nc.alloc_psum_tensor("ps0", [128, 512], F32)
    ps1 = nc.alloc_psum_tensor("ps1", [128, 512], F32)
    ps2 = nc.alloc_psum_tensor("ps2", [128, 512], F32)
    ps3 = nc.alloc_psum_tensor("ps3", [128, 512], F32)
    psCS = [ps0, ps1, ps2, ps3]

    xT = XK.ap()
    K_bufs = XK.ap().bitcast(BF16)
    vth = XK.ap().bitcast(BF16)
    U2 = XK.ap()
    OSTG4 = h1h_t.ap().bitcast(F32)
    E_t = h1h_t.ap()[:, 0:S]
    wmod_sb = wh_t.ap().bitcast(F32)
    mod_row = stat_all[0:1, 0:1536]
    stat_mu = stat_all[0:1, 0:T]
    stat_rstd = stat_all[0:1, T:2 * T]
    cmp_t = xstg2.ap()[:, 1024:2048]
    masked = xstg2[0:16, 1024:2048]
    S_lay = scrl.ap().bitcast(F32)
    ST_sb = scr32b[0:16, :]
    logT = S_lay[0:16, :]
    sqs = scrh.ap().bitcast(F32)
    rw_sb = wstg[:, 0:256]
    cs_stage = xstg[0:4, 0:2048]
    cs_mv = ql_t.ap().bitcast(F32)[0:1, :]
    stat_t2 = cs_mv
    w_sb = cs_mv

    def kslice(par, half):
        off = (par * 2 + half) * S
        return K_bufs[:, off:off + S]

    def dma(qeng, sem, out, in_, waits=()):
        p.emit(qeng, lambda e, o=out, i=in_: e.dma_start(out=o, in_=i),
               waits=waits, inc=(sem, 16))

    def act(fn, waits=(), inc=False):
        p.emit('scalar', fn, waits=waits, inc=(('ac', 1) if inc else None))

    def dve(fn, waits=(), inc=False):
        p.emit('vector', fn, waits=waits, inc=(('dv', 1) if inc else None))

    def pe(fn, waits=(), inc=False):
        p.emit('tensor', fn, waits=waits, inc=(('pe', 1) if inc else None))

    def gp(fn, waits=(), inc=None):
        p.emit('gpsimd', fn, waits=waits, inc=inc)

    def mcol(base, c):
        return modp[:, base + c: base + c + 1]

    PH_ORDER = ['0', 'A1', 'A2', 'A3', 'A', 'B', 'C', 'D', 'E', 'F']

    def stop(ph):
        if (stop_after is not None
                and PH_ORDER.index(ph) > PH_ORDER.index(stop_after)):
            raise _StopEmit()

    try:
        dve(lambda e: e.memset(ones_row[:], 1.0))
        dve(lambda e: e.memset(ones_col[:], 1.0))
        dve(lambda e: e.memset(ones16[:], 1.0))
        dve(lambda e: e.memset(biasC[:], EXP_SHIFT))
        dve(lambda e: e.memset(epsb[:], 1e-5))
        dve(lambda e: e.memset(c512[:], float(KSEL)))
        dve(lambda e: e.memset(half16[:], 0.5))
        for m in range(4):
            dve(lambda e, m=m: e.memset(zrselH[m][:], 0.0))
            dve(lambda e, m=m: e.memset(zrselL[m][:], 0.0), inc=(m == 3))
        dv_consts = p.val('dv')

        dma('sync', 'wA', ident_s[:], identd)
        dma('sync', 'wA', selB_s[:], selBd)
        dma('sync', 'wA', sel2_s[:], sel2d)
        dma('sync', 'wA', cv_sb[:], cvd.rearrange("(a b) -> a b", b=128))
        dma('sync', 'wA', rbT[:], rbd)
        wA_consts = p.val('wA')

        pe(lambda e: e.transpose(psW0[:, 0:16], cv_sb[:], ident_s[0:16, 0:16]),
           waits=[('wA', wA_consts)], inc=True)
        pe_cT = p.val('pe')
        act(lambda e: e.activation(out=cT_silu[:], in_=psW0[:, 0:16], func=AF.Silu),
            waits=[('pe', pe_cT)], inc=True)
        ac_silu = p.val('ac')
        dma('sync', 'wA', cv_sb[:], bed.rearrange("(a b) -> a b", b=128),
            waits=[('pe', pe_cT)])
        wA_be = p.val('wA')
        pe(lambda e: e.transpose(ps3[:, 0:16], cv_sb[:], ident_s[0:16, 0:16]),
           waits=[('wA', wA_be)], inc=True)
        pe_beT = p.val('pe')
        act(lambda e: e.activation(out=beT[:], in_=ps3[:, 0:16], func=AF.Copy),
            waits=[('pe', pe_beT)], inc=True)
        ac_beT = p.val('ac')

        for dc in range(NCD):
            dma('sync', 'wA', wmod_sb[:, 0:1536],
                wmodd[dc * 128:(dc + 1) * 128, :],
                waits=([('pe', p.val('pe'))] if dc >= 1 else []))
            wA_mod = p.val('wA')
            dve(lambda e, dc=dc: e.tensor_copy(cstat[:], cT_silu[:, dc:dc + 1]),
                waits=[('ac', ac_silu)] + ([('pe', p.val('pe'))] if dc > 0 else []),
                inc=True)
            dv_c = p.val('dv')
            for j in range(3):
                mpst = (psW1[0:1, 0:512], psW1[0:1, 512:1024], ps2[0:1, 0:512])[j]
                pe(lambda e, j=j, dc=dc, mpst=mpst: e.matmul(
                    mpst, cstat[:],
                    wmod_sb[:, j * 512:(j + 1) * 512],
                    start=(dc == 0), stop=(dc == NCD - 1)),
                   waits=[('wA', wA_mod), ('dv', dv_c)], inc=(j == 2))
        pe_mod = p.val('pe')
        for j in range(3):
            mpst = (psW1[0:1, 0:512], psW1[0:1, 512:1024], ps2[0:1, 0:512])[j]
            act(lambda e, j=j, mpst=mpst: e.activation(
                out=mod_row[:, j * 512:(j + 1) * 512], in_=mpst,
                func=AF.Copy), waits=([('pe', pe_mod)] if j == 0 else []), inc=(j == 2))
        ac_modrow = p.val('ac')
        dma('gpsimd', 'wG', mod_ag_in.ap(), mod_row, waits=[('ac', ac_modrow)])
        wG_mod = p.val('wG')
        gp(lambda e: e.collective_compute(
            "AllGather", OP.bypass, replica_groups=[list(range(R))],
            ins=[mod_ag_in.ap().opt()], outs=[mod_ag_out.ap().opt()]),
           waits=[('wG', wG_mod)], inc=('cc', 1))
        cc_mod = p.val('cc')
        dma('sync', 'wA', mod_all[:],
            mod_ag_out.ap().rearrange("r (a b) -> (r a) b", b=128),
            waits=[('cc', cc_mod)])
        wA_modall = p.val('wA')
        pe(lambda e: e.transpose(psW0[:, 0:96], mod_all[:], ident_s[0:96, 0:96]),
           waits=[('wA', wA_modall), ('ac', ac_silu)], inc=True)
        pe_modT = p.val('pe')
        act(lambda e: e.activation(out=modT[:], in_=psW0[:, 0:96], func=AF.Copy),
            waits=[('pe', pe_modT)], inc=True)
        ac_modT = p.val('ac')
        dve(lambda e: e.tensor_copy(modp[:, 0:16], modT[:, 0:16]),
            waits=[('ac', ac_modT)])
        dve(lambda e: e.tensor_scalar(modp[:, 16:32], modT[:, 16:32], 1.0, None,
                                      op0=OP.add))
        dve(lambda e: e.tensor_copy(modp[:, 48:64], modT[:, 48:64]))
        dve(lambda e: e.tensor_scalar(modp[:, 64:80], modT[:, 64:80], 1.0, None,
                                      op0=OP.add))
        for base in (32, 80):
            dve(lambda e, s=base: e.tensor_scalar(
                scr32[:, 32:48], modT[:, s:s + 16], -1.0, None, op0=OP.mult),
                inc=True)
            dve(lambda e, s=base: e.tensor_tensor(
                scr32[:, 0:16], modT[:, s:s + 16], scr32[:, 32:48], op=OP.max),
                waits=[('dv', p.val('dv'))], inc=True)
            dve(lambda e: e.tensor_scalar(
                scr32[:, 0:16], scr32[:, 0:16], 0.25, None, op0=OP.max),
                waits=[('dv', p.val('dv'))], inc=True)
            dve(lambda e, s=base: e.tensor_scalar(
                scr32[:, 16:32], modT[:, s:s + 16], 0.0, None, op0=OP.is_ge),
                inc=True)
            dve(lambda e: e.tensor_scalar(
                scr32[:, 16:32], scr32[:, 16:32], 2.0, -1.0, op0=OP.mult, op1=OP.add),
                waits=[('dv', p.val('dv'))], inc=True)
            dve(lambda e, d=base: e.tensor_tensor(
                modp[:, d:d + 16], scr32[:, 0:16], scr32[:, 16:32], op=OP.mult),
                waits=[('dv', p.val('dv'))], inc=True)
        dv_modp = p.val('dv')

        stop('A')
        tr_evac_ac = {}
        for tc in range(8):
            stg = xstg if tc % 2 == 0 else xstg2
            dma('sync', 'wA', stg[:], x_in[tc * 128:(tc + 1) * 128, :],
                waits=([('pe', p.val('pe'))] if tc >= 2 else []))
            wA_x = p.val('wA')
            for dc in range(NCD):
                i = tc * NCD + dc
                pb = psW0 if i % 2 == 0 else psW1
                wts = [('wA', wA_x)]
                if i == 0:
                    wts.append(('ac', ac_modT))
                elif i == 1:
                    wts.append(('ac', ac_modrow))
                elif (i - 2) in tr_evac_ac:
                    wts.append(('ac', tr_evac_ac[i - 2]))
                pe(lambda e, stg=stg, dc=dc, pb=pb: e.transpose(
                    pb[:, 0:128], stg[:, dc * 128:(dc + 1) * 128], ident_s[:]),
                   waits=wts, inc=True)
                pe_tr = p.val('pe')
                act(lambda e, dc=dc, tc=tc, pb=pb: e.activation(
                    out=xT[:, dc * T + tc * 128: dc * T + (tc + 1) * 128],
                    in_=pb[:, 0:128], func=AF.Copy),
                    waits=[('pe', pe_tr)], inc=True)
                tr_evac_ac[i] = p.val('ac')
        ac_xT = p.val('ac')

        for dc in range(NCD):
            dma('gpsimd', 'wG', xT_sp[dc], xT[:, dc * T:(dc + 1) * T],
                waits=([('ac', ac_xT)] if dc == 0 else []))
        wG_xT = p.val('wG')
        stop('A1')

        pa_stat_pe = 0
        for dc in range(NCD):
            xc = xT[:, dc * T:(dc + 1) * T]
            act(lambda e, xc=xc: e.activation(out=sqs[:], in_=xc, func=AF.Square),
                waits=([('pe', pa_stat_pe)] if pa_stat_pe else []), inc=True)
            ac_sq = p.val('ac')
            for thf in range(2):
                pe(lambda e, xc=xc, thf=thf, dc=dc: e.matmul(
                    (ps2 if thf == 0 else ps3)[0:1, :], ones_col[:],
                    xc[:, thf * 512:(thf + 1) * 512],
                    start=(dc == 0), stop=(dc == NCD - 1)),
                   waits=([('ac', ac_xT)] if dc == 0 and thf == 0 else []))
                pe(lambda e, thf=thf, dc=dc: e.matmul(
                    psW1[0:1, thf * 512:(thf + 1) * 512], ones_col[:],
                    sqs[:, thf * 512:(thf + 1) * 512],
                    start=(dc == 0), stop=(dc == NCD - 1)),
                   waits=[('ac', ac_sq)], inc=True)
            pa_stat_pe = p.val('pe')

        def ln_finish():
            pe_st = p.val('pe')
            act(lambda e: e.activation(out=stat_mu[:, 0:512], in_=ps2[0:1, :],
                                       func=AF.Copy, scale=1.0 / D),
                waits=[('pe', pe_st)])
            act(lambda e: e.activation(out=stat_mu[:, 512:1024], in_=ps3[0:1, :],
                                       func=AF.Copy, scale=1.0 / D))
            act(lambda e: e.activation(out=stat_t2[:, 0:512], in_=psW1[0:1, 0:512],
                                       func=AF.Copy, scale=1.0 / D))
            act(lambda e: e.activation(out=stat_t2[:, 512:1024], in_=psW1[0:1, 512:1024],
                                       func=AF.Copy, scale=1.0 / D), inc=True)
            ac_st = p.val('ac')
            dve(lambda e: e.tensor_tensor(stat_rstd[:], stat_mu[:], stat_mu[:],
                                          op=OP.mult), waits=[('ac', ac_st)])
            dve(lambda e: e.tensor_tensor(stat_t2[:], stat_t2[:], stat_rstd[:],
                                          op=OP.subtract), inc=True)
            dv_v = p.val('dv')
            act(lambda e: e.activation(out=stat_t2[:], in_=stat_t2[:], func=AF.Sqrt,
                                       bias=epsb[:]), waits=[('dv', dv_v)], inc=True)
            ac_sd = p.val('ac')
            dve(lambda e: e.reciprocal(stat_rstd[:], stat_t2[:]),
                waits=[('ac', ac_sd)], inc=True)
            dve(lambda e: e.tensor_tensor(stat_mu[:], stat_mu[:], stat_rstd[:],
                                          op=OP.mult), waits=[('dv', p.val('dv'))],
                inc=True)
            dv_r = p.val('dv')
            for tc in range(8):
                pe(lambda e, tc=tc: e.matmul(
                    psW0[:, tc * 128:(tc + 1) * 128], ones_row[:],
                    stat_rstd[0:1, tc * 128:(tc + 1) * 128], start=True, stop=True),
                   waits=([('dv', dv_r)] if tc == 0 else []))
                pe(lambda e, tc=tc: e.matmul(
                    psW1[:, tc * 128:(tc + 1) * 128], ones_row[:],
                    stat_mu[0:1, tc * 128:(tc + 1) * 128], start=True, stop=True),
                   inc=(tc == 7))
            return p.val('pe')

        stop('A2')
        pe_bc1 = ln_finish()
        stop('A3')

        h1f_ac = {}
        for dc in range(min(NCD, h1_upto)):
            buf = scr32b if dc % 2 == 0 else sqs
            wts = [('pe', pe_bc1)]
            if dc - 2 >= 0:
                wts.append(('ac', h1f_ac[dc - 2]))
            for hf in range(2):
                dve(lambda e, dc=dc, hf=hf: e.tensor_tensor(
                    scr32[:, hf * 512:(hf + 1) * 512],
                    xT[:, dc * T + hf * 512: dc * T + hf * 512 + 512],
                    psW0[:, hf * 512:(hf + 1) * 512], op=OP.mult),
                    waits=(wts if hf == 0 else []))
                dve(lambda e, hf=hf: e.tensor_tensor(
                    scr32[:, hf * 512:(hf + 1) * 512], scr32[:, hf * 512:(hf + 1) * 512],
                    psW1[:, hf * 512:(hf + 1) * 512], op=OP.subtract))
            if h1_mode >= 1:
                dve(lambda e, dc=dc, buf=buf: e.tensor_scalar(
                    buf[:, 0:T], scr32[:], mcol(16, dc), mcol(0, dc),
                    op0=OP.mult, op1=OP.add), waits=[('dv', dv_modp)], inc=True)
            else:
                dve(lambda e, dc=dc, buf=buf: e.tensor_copy(buf[:, 0:T], scr32[:]),
                    waits=[('dv', dv_modp)], inc=True)
            dv_h1 = p.val('dv')
            if h1_mode >= 2:
                act(lambda e, dc=dc, buf=buf: e.activation(
                    out=h1h_t[:, dc * T:(dc + 1) * T], in_=buf[:, 0:T], func=AF.Copy),
                    waits=[('dv', dv_h1)], inc=True)
            else:
                act(lambda e: e.copy(zrowv[:], zrecip[:]), waits=[('dv', dv_h1)],
                    inc=True)
            h1f_ac[dc] = p.val('ac')
            if h1_mode >= 3:
                dve(lambda e, dc=dc, buf=buf: e.tensor_tensor(
                    h1l_t[:, dc * T:(dc + 1) * T], buf[:, 0:T],
                    h1h_t[:, dc * T:(dc + 1) * T], op=OP.subtract),
                    waits=[('ac', h1f_ac[dc])], inc=True)
            else:
                dve(lambda e: e.tensor_copy(zrowv[:], zrecip[:]),
                    waits=[('ac', h1f_ac[dc])], inc=True)
        dv_h1_done = p.val('dv')

        stop('B')
        def stream_weight(src_col_ap, b, act_extra=(), pe_gate=None):
            dma('sync', 'wA', wstg[:],
                src_col_ap.rearrange("(a b) c -> b a c", b=128),
                waits=[('dv', stream_weight.stg_dv)])
            wA_w = p.val('wA')
            gw = [('pe', pe_gate)] if pe_gate else []
            dve(lambda e, b=b: e.tensor_copy(wh_t[:, b * D:(b + 1) * D], wstg[:]),
                waits=[('wA', wA_w)] + list(act_extra) + gw, inc=True)
            dve(lambda e, b=b: e.tensor_tensor(
                wl_t[:, b * D:(b + 1) * D], wstg[:], wh_t[:, b * D:(b + 1) * D],
                op=OP.subtract), inc=True)
            stream_weight.stg_dv = p.val('dv')
            return p.val('dv')

        stream_weight.stg_dv = 0

        def mm3(pst, b, dc, mov_h, mov_l, first, last, wts):
            whap = wh_t[:, b * D + dc * 128: b * D + (dc + 1) * 128]
            wlap = wl_t[:, b * D + dc * 128: b * D + (dc + 1) * 128]
            pe(lambda e, w=whap, m=mov_h, pst=pst, st=first: e.matmul(
                pst, w, m, start=st, stop=False), waits=(wts if first else []))
            pe(lambda e, w=whap, m=mov_l, pst=pst: e.matmul(
                pst, w, m, start=False, stop=False))
            pe(lambda e, w=wlap, m=mov_h, pst=pst, sp=last: e.matmul(
                pst, w, m, start=False, stop=sp), inc=last)

        oc_order = list(range(16, 32)) + list(range(32, 48)) + list(range(0, 16))
        qkv_s32_dv = {}
        qkv_ps = [psW0.ap()[:, 0:512], psW0.ap()[:, 512:1024],
                  psW1.ap()[:, 0:512], psW1.ap()[:, 512:1024]]
        last_evac = {}
        spill_wG = {}
        kag_cc = None

        qkv_tile_pe = {}
        for oi, oc in enumerate(oc_order[:qkv_upto]):
            b = oi % 2
            dv_w = stream_weight(wqkv[:, oc * 128:(oc + 1) * 128], b,
                                 act_extra=([('pe', pe_mod)] if oi < 2 else []),
                                 pe_gate=qkv_tile_pe.get(oi - 2))
            for th in range(2):
                pidx = (oi * 2 + th) % 4
                pst = qkv_ps[pidx]
                wts = [('dv', dv_w)]
                if pidx in last_evac:
                    wts.append(last_evac[pidx])
                if oi == 0 and th == 0:
                    wts.append(('dv', dv_h1_done))
                for dc in range(NCD):
                    mm3(pst, b, dc,
                        h1h_t[:, dc * T + th * 512: dc * T + th * 512 + 512],
                        h1l_t[:, dc * T + th * 512: dc * T + th * 512 + 512],
                        dc == 0, dc == NCD - 1, wts)
                pe_q = p.val('pe')
                qkv_tile_pe[oi] = pe_q
                sh = scrh[:, b * T + th * 512: b * T + th * 512 + 512]
                sl = scrl[:, b * T + th * 512: b * T + th * 512 + 512]
                s32 = scr32.ap()[:, th * 512: th * 512 + 512]
                awts = [('pe', pe_q)]
                if (oi * 2 + th - 2) in qkv_s32_dv:
                    awts.append(('dv', qkv_s32_dv[oi * 2 + th - 2]))
                act(lambda e, s32=s32, pst=pst: e.activation(
                    out=s32, in_=pst, func=AF.Copy),
                    waits=awts, inc=True)
                last_evac[pidx] = ('ac', p.val('ac'))
                ac_e = p.val('ac')
                dwts = [('ac', ac_e)]
                if oi - 2 >= 0:
                    dwts.append(('wG', spill_wG[oi - 2]))
                dve(lambda e, sh=sh, s32=s32: e.tensor_copy(sh, s32),
                    waits=dwts, inc=True)
                dve(lambda e, sl=sl, s32=s32, sh=sh: e.tensor_tensor(
                    sl, s32, sh, op=OP.subtract), inc=True)
                qkv_s32_dv[oi * 2 + th] = p.val('dv')
            dv_oc = p.val('dv')
            if oc >= 32:
                dsth, dstl = v_sp[0, oc - 32], v_sp[1, oc - 32]
            elif oc >= 16:
                dsth, dstl = kag_in[0, oc - 16], kag_in[1, oc - 16]
            else:
                dsth, dstl = q_sp[0, oc], q_sp[1, oc]
            dma('gpsimd', 'wG', dsth, scrh[:, b * T:(b + 1) * T],
                waits=[('dv', dv_oc)])
            dma('gpsimd', 'wG', dstl, scrl[:, b * T:(b + 1) * T])
            spill_wG[oi] = p.val('wG')
            if oi == 15:
                gp(lambda e: e.collective_compute(
                    "AllGather", OP.bypass, replica_groups=[list(range(R))],
                    ins=[kag_in.ap().opt()], outs=[kag_out.ap().opt()]),
                   waits=[('wG', spill_wG[15])], inc=('cc', 1))
                kag_cc = p.val('cc')

        stop('C')
        exp_ac = {}
        head_end_pe = {}
        csel_done = {}
        cs_dma_wG = {}
        last_colsum_pe = 0
        qkv_evac_gates = list(last_evac.values())

        for h in range(min(NH, heads_upto)):
            par = h % 2
            kwts = [('cc', kag_cc)]
            if h >= 2:
                kwts.append(('pe', head_end_pe[h - 2]))
            for r in range(R):
                dma('scalar', 'wV', kslice(par, 0)[:, r * T:(r + 1) * T],
                    kag_out[r, 0, h], waits=(kwts if r == 0 else []))
                dma('scalar', 'wV', kslice(par, 1)[:, r * T:(r + 1) * T],
                    kag_out[r, 1, h])
            dma('scalar', 'wV', qh_t[:, par * T:(par + 1) * T], q_sp[0, h],
                waits=[('wG', spill_wG[32 + h])])
            dma('scalar', 'wV', ql_t[:, par * T:(par + 1) * T], q_sp[1, h])
            kv = p.val('wV')

            for pnl in range(8):
                qh_ap = qh_t[:, par * T + pnl * 128: par * T + pnl * 128 + 128]
                ql_ap = ql_t[:, par * T + pnl * 128: par * T + pnl * 128 + 128]
                for g in range(8):
                    pw = psW0 if g % 2 == 0 else psW1
                    wts = [('wV', kv)]
                    if h == 0 and pnl == 0 and g < 2:
                        wts.extend(qkv_evac_gates)
                    if g >= 2:
                        key = (h, pnl, g - 2)
                    elif pnl > 0:
                        key = (h, pnl - 1, g + 6)
                    else:
                        key = (h - 1, 7, g + 6)
                    if key in exp_ac:
                        wts.append(('ac', exp_ac[key]))
                    kh0 = kslice(par, 0)[:, g * 1024: g * 1024 + 512]
                    kh1 = kslice(par, 0)[:, g * 1024 + 512: g * 1024 + 1024]
                    kl0 = kslice(par, 1)[:, g * 1024: g * 1024 + 512]
                    kl1 = kslice(par, 1)[:, g * 1024 + 512: g * 1024 + 1024]
                    pe(lambda e, q=qh_ap, k=kh0, pw=pw: e.matmul(
                        pw[:, 0:512], q, k, start=True, stop=False), waits=wts)
                    pe(lambda e, q=qh_ap, k=kh1, pw=pw: e.matmul(
                        pw[:, 512:1024], q, k, start=True, stop=False))
                    pe(lambda e, q=qh_ap, k=kl0, pw=pw: e.matmul(
                        pw[:, 0:512], q, k, start=False, stop=False))
                    pe(lambda e, q=qh_ap, k=kl1, pw=pw: e.matmul(
                        pw[:, 512:1024], q, k, start=False, stop=False))
                    pe(lambda e, q=ql_ap, k=kh0, pw=pw: e.matmul(
                        pw[:, 0:512], q, k, start=False, stop=True))
                    pe(lambda e, q=ql_ap, k=kh1, pw=pw: e.matmul(
                        pw[:, 512:1024], q, k, start=False, stop=True), inc=True)
                    pe_sc = p.val('pe')
                    ewts = [('pe', pe_sc)]
                    if g == 0 and last_colsum_pe:
                        ewts.append(('pe', last_colsum_pe))
                    act(lambda e, g=g, pw=pw: e.activation(
                        out=E_t[:, g * 1024:(g + 1) * 1024], in_=pw[:],
                        func=AF.Exp, scale=INV_SQRT_DH, bias=biasC[:],
                        accum_out=zpart[:, g:g + 1]), waits=ewts, inc=True)
                    exp_ac[(h, pnl, g)] = p.val('ac')
                act(lambda e: e.copy(smax[:, 216:217], zpart[:, 0:1]), inc=True)
                ac_exp = p.val('ac')
                if debug_dram and h == 0 and pnl == 0:
                    dma('gpsimd', 'wG', e0_dbg.ap(), E_t[:, 0:S],
                        waits=[('ac', p.val('ac'))])
                dve(lambda e: e.tensor_reduce(zrowv[:], zpart[:], axis=AX, op=OP.add),
                    waits=[('ac', ac_exp), ('dv', dv_consts),
                           (('pe', last_colsum_pe) if last_colsum_pe else None)],
                    inc=True)
                dve(lambda e: e.reciprocal(zrecip[:], zrowv[:]),
                    waits=[('dv', p.val('dv'))], inc=True)
                if debug_dram and h == 0:
                    dve(lambda e, pnl=pnl: e.tensor_copy(
                        smax[:, 200 + pnl: 201 + pnl], zrowv[:]))
                    dve(lambda e, pnl=pnl: e.tensor_copy(
                        smax[:, 208 + pnl: 209 + pnl], zrecip[:]))
                dve(lambda e: e.tensor_copy(zrh_b[:], zrecip[:]),
                    waits=[('dv', p.val('dv'))], inc=True)
                dve(lambda e: e.tensor_tensor(zrl_b[:], zrecip[:], zrh_b[:],
                                              op=OP.subtract),
                    waits=[('dv', p.val('dv'))], inc=True)
                dve(lambda e: e.engine_nop(), waits=[('dv', p.val('dv'))])
                if debug_dram and h == 0:
                    dve(lambda e, pnl=pnl: e.tensor_copy(
                        smax[:, 230 + pnl: 231 + pnl], zrh_b[:]))
                    dve(lambda e, pnl=pnl: e.tensor_copy(
                        smax[:, 240 + pnl: 241 + pnl], zrl_b[:]))
                for m in range(4):
                    dve(lambda e, m=m: e.tensor_copy(zrselH[m][:, m:m + 1], zrh_b[:]))
                    dve(lambda e, m=m: e.tensor_copy(zrselL[m][:, m:m + 1], zrl_b[:]),
                        inc=(m == 3))
                dv_zr = p.val('dv')
                for j in range(16):
                    wts = [('dv', dv_zr)]
                    if pnl == 0 and (j // 4) in csel_done:
                        wts.append(csel_done[j // 4])
                    pe(lambda e, j=j, pnl=pnl: e.matmul(
                        psCS[j // 4][0:4, :], zrselH[j % 4][:],
                        E_t[:, j * 512:(j + 1) * 512],
                        start=(pnl == 0 and j % 4 == 0), stop=False), waits=wts)
                    pe(lambda e, j=j, pnl=pnl: e.matmul(
                        psCS[j // 4][0:4, :], zrselL[j % 4][:],
                        E_t[:, j * 512:(j + 1) * 512],
                        start=False, stop=(pnl == 7 and j % 4 == 3)),
                       inc=(j == 15))
                last_colsum_pe = p.val('pe')
            head_end_pe[h] = last_colsum_pe
            for bb in range(4):
                ewts = [('pe', last_colsum_pe)] if bb == 0 else []
                if bb == 0 and h - 1 in cs_dma_wG:
                    ewts.append(('wG', cs_dma_wG[h - 1]))
                act(lambda e, bb=bb: e.activation(
                    out=cs_stage[:, bb * 512:(bb + 1) * 512],
                    in_=psCS[bb][0:4, :], func=AF.Copy), waits=ewts, inc=True)
                csel_done[bb] = ('ac', p.val('ac'))
            ac_cse = p.val('ac')
            if debug_dram and h == 0:
                dma('gpsimd', 'wG', cs0_dbg.ap(), cs_stage[:, :],
                    waits=[('ac', ac_cse)])
            for j in range(16):
                csrc = cs_stage[j % 4: j % 4 + 1,
                                512 * (j // 4): 512 * (j // 4) + 512]
                dst = cs_rs_in[j // 2, h, 512 * (j % 2): 512 * (j % 2) + 512]
                dma('gpsimd', 'wG', dst, csrc,
                    waits=([('ac', ac_cse)] if j == 0 else []))
            cs_dma_wG[h] = p.val('wG')
        wG_cs = p.val('wG')

        gp(lambda e: e.collective_compute(
            "ReduceScatter", OP.add, replica_groups=[list(range(R))],
            ins=[cs_rs_in.ap().opt()], outs=[cs_rs_out.ap().opt()]),
           waits=[('wG', wG_cs)], inc=('cc', 1))
        cc_cs = p.val('cc')

        if debug_dram:
            dma('gpsimd', 'wG', z2_dbg.ap(), smax[:, 224:248],
                waits=[('dv', p.val('dv'))])
            dma('gpsimd', 'wG', z0_dbg.ap(), smax[:, 200:216],
                waits=[('dv', p.val('dv'))])
            dma('gpsimd', 'wG', xstg2[0:16, 1024:2048], cs_rs_out.ap(),
                waits=[('cc', cc_cs)])
            dma('gpsimd', 'wG', cs_dbg.ap(), xstg2[0:16, 1024:2048],
                waits=[('wG', p.val('wG'))])

        stop('D')
        val_spill_wG = {}
        pe_csb_prev = None
        dv_val_prev = None
        for h in range(NH):
            par = h % 2
            vwts = [('pe', head_end_pe[max(head_end_pe)])] if h == 0 else []
            if h >= 2:
                vwts.append(('wG', val_spill_wG[h - 2]))
            dma('scalar', 'wV', scrh[:, par * T:(par + 1) * T], v_sp[0, h],
                waits=vwts)
            dma('scalar', 'wV', scrl[:, par * T:(par + 1) * T], v_sp[1, h])
            wV_v = p.val('wV')
            mvwts = [('cc', cc_cs)]
            if pe_csb_prev is not None:
                mvwts.append(('pe', pe_csb_prev))
            dma('gpsimd', 'wG', cs_mv, cs_rs_out[h:h + 1, :], waits=mvwts)
            wG_mv = p.val('wG')
            for thf in range(2):
                bwts = [('wG', wG_mv)]
                if dv_val_prev is not None:
                    bwts.append(('dv', dv_val_prev))
                pe(lambda e, thf=thf: e.matmul(
                    psW0[:, thf * 512:(thf + 1) * 512], ones_row[:],
                    cs_mv[:, thf * 512:(thf + 1) * 512], start=True, stop=True),
                   waits=(bwts if thf == 0 else []), inc=(thf == 1))
            pe_csb = p.val('pe')
            pe_csb_prev = pe_csb
            dve(lambda e, par=par: e.tensor_tensor(
                scr32[:], scrh[:, par * T:(par + 1) * T],
                scrl[:, par * T:(par + 1) * T], op=OP.add), waits=[('wV', wV_v)])
            dve(lambda e: e.tensor_tensor(scr32[:, 0:512], scr32[:, 0:512],
                                          psW0[:, 0:512], op=OP.mult),
                waits=[('pe', pe_csb)])
            dve(lambda e: e.tensor_tensor(scr32[:, 512:1024], scr32[:, 512:1024],
                                          psW0[:, 512:1024], op=OP.mult), inc=True)
            dv_val = p.val('dv')
            dv_val_prev = dv_val
            act(lambda e, par=par: e.activation(
                out=scrh[:, par * T:(par + 1) * T], in_=scr32[:], func=AF.Copy),
                waits=[('dv', dv_val)], inc=True)
            ac_vh = p.val('ac')
            dve(lambda e, par=par: e.tensor_tensor(
                scrl[:, par * T:(par + 1) * T], scr32[:],
                scrh[:, par * T:(par + 1) * T], op=OP.subtract),
                waits=[('ac', ac_vh)], inc=True)
            dv_vl = p.val('dv')
            dma('gpsimd', 'wG', val_sp[0, h], scrh[:, par * T:(par + 1) * T],
                waits=[('dv', dv_vl)])
            dma('gpsimd', 'wG', val_sp[1, h], scrl[:, par * T:(par + 1) * T])
            val_spill_wG[h] = p.val('wG')
        wG_val = p.val('wG')

        op_evac = {}
        op_tile_pe = {}
        x1s_dv = {}
        x1_spill = {}
        x1_sq_ac = {}
        x1_stat_pe = {}
        for th in range(2):
            for dh in range(NH):
                dma('scalar', 'wV', vth[:, dh * 512: dh * 512 + 512],
                    val_sp[0, dh][:, th * 512:(th + 1) * 512],
                    waits=([('wG', wG_val)] if (th == 0 and dh == 0) else []))
                dma('scalar', 'wV', vth[:, 16384 + dh * 512: 16384 + dh * 512 + 512],
                    val_sp[1, dh][:, th * 512:(th + 1) * 512])
            wV_vth = p.val('wV')
            for oc in range(NCD):
                i = th * NCD + oc
                b = i % 2
                dv_w = stream_weight(wo[:, oc * 128:(oc + 1) * 128], b,
                                     pe_gate=op_tile_pe.get(i - 2))
                pidx = i % 2
                pst = (ps0 if pidx == 0 else ps1).ap()
                buf = scr32b if i % 2 == 0 else scr32
                wts = [('dv', dv_w), ('wV', wV_vth)]
                if pidx in op_evac:
                    wts.append(op_evac[pidx])
                for dh in range(NCD):
                    mm3(pst, b, dh,
                        vth[:, dh * 512: dh * 512 + 512],
                        vth[:, 16384 + dh * 512: 16384 + dh * 512 + 512],
                        dh == 0, dh == NCD - 1, wts)
                pe_op = p.val('pe')
                op_tile_pe[i] = pe_op
                xs = xstg if i % 2 == 0 else xstg2
                xwts = [('wG', wG_xT), ('wG', wG_cs)] if i == 0 else []
                if i - 2 in x1s_dv:
                    xwts.append(('dv', x1s_dv[i - 2]))
                dma('sync', 'wA', xs[:, 0:512], xT_sp[oc][:, th * 512:(th + 1) * 512],
                    waits=xwts)
                wA_x1 = p.val('wA')
                swts = [('pe', pe_op), ('wA', wA_x1)]
                if i - 2 in x1_spill:
                    swts += [('wG', x1_spill[i - 2]), ('ac', x1_sq_ac[i - 2]),
                             ('pe', x1_stat_pe[i - 2])]
                dve(lambda e, pst=pst, oc=oc, xs=xs, buf=buf: e.scalar_tensor_tensor(
                    buf[:, 0:512], pst, mcol(32, oc), xs[:, 0:512],
                    op0=OP.mult, op1=OP.add), waits=swts, inc=True)
                dv_x1 = p.val('dv')
                x1s_dv[i] = dv_x1
                op_evac[pidx] = ('dv', dv_x1)
                dma('gpsimd', 'wG', x1_sp[oc][:, th * 512:(th + 1) * 512],
                    buf[:, 0:512], waits=[('dv', dv_x1)])
                x1_spill[i] = p.val('wG')
                aqw = [('dv', dv_x1)]
                if i - 1 in x1_stat_pe:
                    aqw.append(('pe', x1_stat_pe[i - 1]))
                act(lambda e, buf=buf: e.activation(out=sqs[:, 0:512], in_=buf[:, 0:512],
                                                    func=AF.Square), waits=aqw, inc=True)
                ac_sq = p.val('ac')
                x1_sq_ac[i] = ac_sq
                pe(lambda e, th=th, oc=oc, buf=buf: e.matmul(
                    (ps2 if th == 0 else ps3)[0:1, :], ones_col[:], buf[:, 0:512],
                    start=(oc == 0), stop=(oc == NCD - 1)), waits=[('dv', dv_x1)])
                pe(lambda e, th=th, oc=oc: e.matmul(
                    psW1[0:1, th * 512:(th + 1) * 512], ones_col[:], sqs[:, 0:512],
                    start=(oc == 0), stop=(oc == NCD - 1)),
                   waits=[('ac', ac_sq)], inc=True)
                x1_stat_pe[i] = p.val('pe')
        wG_x1 = p.val('wG')

        pe_bc2 = ln_finish()

        stop('E')
        h2f_ac = {}
        h2s_dv = {}
        for dc in range(NCD):
            xs = xstg if dc % 2 == 0 else xstg2
            xwts = [('wG', wG_x1)] if dc == 0 else []
            if dc - 2 in h2s_dv:
                xwts.append(('dv', h2s_dv[dc - 2]))
            dma('sync', 'wA', xs[:, 0:T], x1_sp[dc], waits=xwts)
            wA_h2 = p.val('wA')
            buf = scr32b if dc % 2 == 0 else sqs
            wts = [('pe', pe_bc2), ('wA', wA_h2)]
            if dc - 2 >= 0:
                wts.append(('ac', h2f_ac[dc - 2]))
            for hf in range(2):
                dve(lambda e, xs=xs, hf=hf: e.tensor_tensor(
                    scr32[:, hf * 512:(hf + 1) * 512],
                    xs[:, hf * 512: hf * 512 + 512],
                    psW0[:, hf * 512:(hf + 1) * 512], op=OP.mult),
                    waits=(wts if hf == 0 else []))
                dve(lambda e, hf=hf: e.tensor_tensor(
                    scr32[:, hf * 512:(hf + 1) * 512], scr32[:, hf * 512:(hf + 1) * 512],
                    psW1[:, hf * 512:(hf + 1) * 512], op=OP.subtract))
            dve(lambda e, dc=dc, buf=buf: e.tensor_scalar(
                buf[:, 0:T], scr32[:], mcol(64, dc), mcol(48, dc),
                op0=OP.mult, op1=OP.add), inc=True)
            dv_h2 = p.val('dv')
            h2s_dv[dc] = dv_h2
            act(lambda e, dc=dc, buf=buf: e.activation(
                out=h1h_t[:, dc * T:(dc + 1) * T], in_=buf[:, 0:T], func=AF.Copy),
                waits=[('dv', dv_h2)], inc=True)
            h2f_ac[dc] = p.val('ac')
            dve(lambda e, dc=dc, buf=buf: e.tensor_tensor(
                h1l_t[:, dc * T:(dc + 1) * T], buf[:, 0:T],
                h1h_t[:, dc * T:(dc + 1) * T], op=OP.subtract),
                waits=[('ac', h2f_ac[dc])], inc=True)
        dv_h2_done = p.val('dv')

        dma('sync', 'wA', rw_sb, rwd.rearrange("(a b) c -> b a c", b=128),
            waits=[('dv', stream_weight.stg_dv)])
        wA_rw = p.val('wA')
        act(lambda e: e.activation(out=rwh[:], in_=rw_sb, func=AF.Copy),
            waits=[('wA', wA_rw)], inc=True)
        ac_rwh = p.val('ac')
        dve(lambda e: e.tensor_tensor(rwl[:], rw_sb, rwh[:], op=OP.subtract),
            waits=[('ac', ac_rwh)], inc=True)
        dv_rw = p.val('dv')
        stream_weight.stg_dv = dv_rw

        for th in range(2):
            pst = psW0.ap()[0:16, th * 512:(th + 1) * 512]
            wts = [('dv', max(dv_rw, dv_h2_done))]
            for dc in range(NCD):
                rh = rwh[:, dc * 16:(dc + 1) * 16]
                rl = rwl[:, dc * 16:(dc + 1) * 16]
                mh = h1h_t[:, dc * T + th * 512: dc * T + th * 512 + 512]
                ml = h1l_t[:, dc * T + th * 512: dc * T + th * 512 + 512]
                pe(lambda e, rh=rh, m=mh, pst=pst, st=(dc == 0): e.matmul(
                    pst, rh, m, start=st, stop=False),
                   waits=(wts if dc == 0 else []))
                pe(lambda e, rh=rh, m=ml, pst=pst: e.matmul(
                    pst, rh, m, start=False, stop=False))
                pe(lambda e, rl=rl, m=mh, pst=pst, sp=(dc == NCD - 1): e.matmul(
                    pst, rl, m, start=False, stop=sp), inc=(dc == NCD - 1))
        pe_log = p.val('pe')
        act(lambda e: e.activation(out=logT[:], in_=psW0[0:16, :], func=AF.Copy),
            waits=[('pe', pe_log)], inc=True)
        ac_log0 = p.val('ac')
        dve(lambda e: e.tensor_scalar(logT[:], logT[:], rbT[:], None, op0=OP.add),
            waits=[('ac', ac_log0)], inc=True)
        dv_log = p.val('dv')

        sm_evac = {}
        for tc in range(8):
            pb = ps2 if tc % 2 == 0 else ps3
            wts = [('dv', dv_log)]
            if tc - 2 in sm_evac:
                wts.append(('ac', sm_evac[tc - 2]))
            pe(lambda e, tc=tc, pb=pb: e.transpose(
                pb[:, 0:16], logT[0:16, tc * 128:(tc + 1) * 128], ident_s[0:16, 0:16]),
               waits=wts, inc=True)
            pe_ltr = p.val('pe')
            dve(lambda e, tc=tc, pb=pb: e.tensor_reduce(
                smax[:, tc * 3: tc * 3 + 1], pb[:, 0:16], axis=AX, op=OP.max,
                negate=True), waits=[('pe', pe_ltr)], inc=True)
            dv_mx = p.val('dv')
            act(lambda e, tc=tc, pb=pb: e.activation(
                out=smax[:, 128 + tc * 16: 128 + tc * 16 + 16], in_=pb[:, 0:16],
                func=AF.Exp, bias=smax[:, tc * 3: tc * 3 + 1],
                accum_out=smax[:, tc * 3 + 1: tc * 3 + 2]),
                waits=[('dv', dv_mx)], inc=True)
            ac_ex = p.val('ac')
            dve(lambda e, tc=tc: e.reciprocal(smax[:, tc * 3 + 2: tc * 3 + 3],
                                              smax[:, tc * 3 + 1: tc * 3 + 2]),
                waits=[('ac', ac_ex)], inc=True)
            dve(lambda e, tc=tc: e.tensor_scalar(
                smax[:, 128 + tc * 16: 128 + tc * 16 + 16],
                smax[:, 128 + tc * 16: 128 + tc * 16 + 16],
                smax[:, tc * 3 + 2: tc * 3 + 3], None, op0=OP.mult),
                waits=[('dv', p.val('dv'))], inc=True)
            dv_sm = p.val('dv')
            pe(lambda e, tc=tc: e.transpose(
                psW1[0:16, tc * 128:(tc + 1) * 128],
                smax[:, 128 + tc * 16: 128 + tc * 16 + 16], ident_s[:]),
               waits=[('dv', dv_sm)], inc=True)
            pe_str = p.val('pe')
            act(lambda e, tc=tc: e.activation(
                out=ST_sb[0:16, tc * 128:(tc + 1) * 128],
                in_=psW1[0:16, tc * 128:(tc + 1) * 128], func=AF.Copy),
                waits=[('pe', pe_str)], inc=True)
            sm_evac[tc] = p.val('ac')
        ac_ST = p.val('ac')
        dma('gpsimd', 'wG', s_ag_in.ap(), ST_sb[0:16, :], waits=[('ac', ac_ST)])
        wG_S = p.val('wG')
        gp(lambda e: e.collective_compute(
            "AllGather", OP.bypass, replica_groups=[list(range(R))],
            ins=[s_ag_in.ap().opt()], outs=[s_ag_out.ap().opt()]),
           waits=[('wG', wG_S)], inc=('cc', 1))
        cc_S = p.val('cc')
        for r in range(R):
            dma('scalar', 'wV', S_lay[r * 16:(r + 1) * 16, :], s_ag_out[r],
                waits=([('cc', cc_S)] if r == 0 else []))
        wV_slay = p.val('wV')

        dve(lambda e: e.memset(lo16[:], 0.0), waits=[('wV', wV_slay)])
        dve(lambda e: e.memset(hi16[:], 1.0))
        dve(lambda e: e.memset(mid16[:], 0.5))
        dve(lambda e: e.memset(thr128[:], 0.5), inc=True)

        stop('F')
        bis_rc = {}
        bis_mid = {}
        bis_cnt_pe = {}
        bis_bc_pe = {}

        def bisect_dve_cmp(i):
            wts = [('wV', wV_slay)] if i == 0 else [('pe', bis_bc_pe[i - 1])]
            dve(lambda e: e.tensor_scalar(cmp_t[:], S_lay[:], thr128[:], None,
                                          op0=OP.is_ge), waits=wts)
            dve(lambda e: e.tensor_reduce(rc_t[:], cmp_t[:], axis=AX, op=OP.add),
                inc=True)
            bis_rc[i] = p.val('dv')

        def bisect_dve_upd(i):
            dve(lambda e: e.tensor_scalar(pred16[:], ps1[0:16, 0:1], c512[:], None,
                                          op0=OP.is_ge), waits=[('pe', bis_cnt_pe[i])],
                inc=True)
            dve(lambda e: e.select(lo16[:], pred16[:], mid16[:], lo16[:]),
                waits=[('dv', p.val('dv'))], inc=True)
            dve(lambda e: e.select(hi16[:], pred16[:], hi16[:], mid16[:]),
                inc=True)
            dve(lambda e: e.tensor_tensor(sum16[:], lo16[:], hi16[:], op=OP.add),
                waits=[('dv', p.val('dv'))], inc=True)
            dve(lambda e: e.tensor_scalar(mid16[:], sum16[:], half16[:], None,
                                          op0=OP.mult), waits=[('dv', p.val('dv'))],
                inc=True)
            bis_mid[i] = p.val('dv')

        def bisect_dve_thr(i):
            dve(lambda e: e.tensor_copy(thr128[:], ps3[:, 0:1]),
                waits=[('pe', bis_bc_pe[i])], inc=True)
            dve(lambda e: e.engine_nop(), waits=[('dv', p.val('dv'))], inc=True)

        if f_mode >= 1:
            bisect_dve_cmp(0)
        bstate = {'i': 0, 'stage': 0}

        def maybe_bisect_pe():
            i = bstate['i']
            if i >= N_BISECT:
                return
            if bstate['stage'] == 0:
                if bis_rc.get(i) is None:
                    return
                pe(lambda e: e.matmul(ps1[0:16, 0:1], selB_s[:], rc_t[:],
                                      start=True, stop=True),
                   waits=[('dv', bis_rc[i])], inc=True)
                bis_cnt_pe[i] = p.val('pe')
                bisect_dve_upd(i)
                bstate['stage'] = 1
            else:
                pe(lambda e: e.matmul(ps3[:, 0:1], sel2_s[:], mid16[:],
                                      start=True, stop=True),
                   waits=[('dv', bis_mid[i])], inc=True)
                bis_bc_pe[i] = p.val('pe')
                bisect_dve_thr(i)
                bstate['stage'] = 0
                bstate['i'] += 1
                if bstate['i'] < N_BISECT:
                    bisect_dve_cmp(bstate['i'])

        exp_evac = {}
        ex_tile_pe = {}
        for th in range(2):
            for oc in range(NCD):
                i = th * NCD + oc
                b = i % 2
                dv_w = stream_weight(wed[:, oc * 128:(oc + 1) * 128], b,
                                     pe_gate=ex_tile_pe.get(i - 2))
                pidx = i % 2
                pst = (psW0 if pidx == 0 else psW1).ap()[:, 0:512]
                wts = [('dv', max(dv_w, dv_h2_done))]
                if pidx in exp_evac:
                    wts.append(exp_evac[pidx])
                for dc in range(NCD):
                    mm3(pst, b, dc,
                        h1h_t[:, dc * T + th * 512: dc * T + th * 512 + 512],
                        h1l_t[:, dc * T + th * 512: dc * T + th * 512 + 512],
                        dc == 0, dc == NCD - 1, wts)
                    if f_mode >= 1:
                        maybe_bisect_pe()
                pe_ex = p.val('pe')
                ex_tile_pe[i] = pe_ex
                if f_mode == -1:
                    act(lambda e: e.activation(out=scr32b[:, 0:512],
                                               in_=scr32b[:, 0:512], func=AF.Copy),
                        waits=[('pe', pe_ex)], inc=True)
                elif f_mode == -2:
                    act(lambda e, pst=pst: e.activation(
                        out=scr32b[:, 0:512], in_=pst, func=AF.Copy),
                        waits=[('pe', pe_ex)], inc=True)
                elif f_mode == -3:
                    act(lambda e, oc=oc, th=th: e.activation(
                        out=U2[:, oc * T + th * 512: oc * T + th * 512 + 512],
                        in_=scr32b[:, 0:512], func=AF.Copy),
                        waits=[('pe', pe_ex)], inc=True)
                else:
                    act(lambda e, oc=oc, th=th, pst=pst: e.activation(
                        out=U2[:, oc * T + th * 512: oc * T + th * 512 + 512],
                        in_=pst, func=AF.Copy), waits=[('pe', pe_ex)], inc=True)
                exp_evac[pidx] = ('ac', p.val('ac'))
        ac_U2 = p.val('ac')
        if f_mode >= 1:
            while bstate['i'] < N_BISECT:
                maybe_bisect_pe()
        dv_bis = p.val('dv')
        if f_mode < 2:
            raise _StopEmit()

        if "thr" in dbg:
            dma('gpsimd', 'wG', dbg["thr"], lo16[:], waits=[('dv', dv_bis)])

        dve(lambda e: e.scalar_tensor_tensor(
            masked[:], ST_sb[0:16, :], lo16[:], ST_sb[0:16, :],
            op0=OP.is_ge, op1=OP.mult),
            waits=[('dv', dv_bis), ('pe', bis_bc_pe[N_BISECT - 1])], inc=True)
        dv_msk = p.val('dv')
        for thf in range(2):
            wps = ps2 if thf == 0 else ps3
            pe(lambda e, thf=thf, wps=wps: e.matmul(
                wps[0:1, :], ones16[:], masked[0:16, thf * 512:(thf + 1) * 512],
                start=True, stop=True),
               waits=([('dv', dv_msk)] if thf == 0 else []), inc=True)
            pe_w = p.val('pe')
            act(lambda e, thf=thf, wps=wps: e.activation(
                out=w_sb[:, thf * 512:(thf + 1) * 512], in_=wps[0:1, :],
                func=AF.Copy), waits=[('pe', pe_w)], inc=True)
        ac_w = p.val('ac')
        for thf in range(2):
            pe(lambda e, thf=thf: e.matmul(
                (ps0 if thf == 0 else ps1)[:, :], ones_row[:],
                w_sb[0:1, thf * 512:(thf + 1) * 512], start=True, stop=True),
               waits=([('ac', ac_w)] if thf == 0 else []), inc=(thf == 1))
        pe_wbc = p.val('pe')
        wbc_sb = xstg2.ap()[:, 1024:2048]
        dve(lambda e: e.tensor_copy(wbc_sb[:, 0:512], ps0[:]),
            waits=[('pe', pe_wbc)])
        dve(lambda e: e.tensor_copy(wbc_sb[:, 512:1024], ps1[:]), inc=True)
        dv_wbc = p.val('dv')
        if "w" in dbg:
            dma('gpsimd', 'wG', dbg["w"], w_sb[0:1, :], waits=[('ac', ac_w)])
        if "msk" in dbg:
            dma('gpsimd', 'wG', dbg["msk"], masked[0:16, :], waits=[('dv', dv_msk)])
        if f_mode < 3:
            raise _StopEmit()

        ep_tr_ac = {}
        ep_sub_ac = {}
        ep_dv = {}
        for th in range(2):
            for oc in range(NCD):
                i = th * NCD + oc
                xs = xstg if i % 2 == 0 else xstg2
                buf = scr32 if i % 2 == 0 else scr32b
                xwts = []
                if i - 2 in ep_dv:
                    xwts.append(('dv', ep_dv[i - 2]))
                dma('sync', 'wA', xs[:, 0:512], x1_sp[oc][:, th * 512:(th + 1) * 512],
                    waits=xwts)
                wA_x1b = p.val('wA')
                wbc = wbc_sb[:, th * 512:(th + 1) * 512]
                ewts = [('dv', dv_wbc), ('ac', ac_U2), ('wA', wA_x1b)]
                if i - 2 in ep_tr_ac:
                    ewts.append(('ac', ep_tr_ac[i - 2]))
                dve(lambda e, oc=oc, th=th, wbc=wbc, buf=buf: e.scalar_tensor_tensor(
                    buf[:, 0:512], U2[:, oc * T + th * 512: oc * T + th * 512 + 512],
                    beT[:, oc:oc + 1], wbc, op0=OP.add, op1=OP.mult),
                    waits=ewts)
                dve(lambda e, oc=oc, xs=xs, buf=buf: e.scalar_tensor_tensor(
                    buf[:, 0:512], buf[:, 0:512], mcol(80, oc), xs[:, 0:512],
                    op0=OP.mult, op1=OP.add), inc=True)
                dv_out = p.val('dv')
                ep_dv[i] = dv_out
                for sub in range(4):
                    si = i * 4 + sub
                    pb = ps2 if si % 2 == 0 else ps3
                    twts = [('dv', dv_out)] if sub == 0 else []
                    if si - 2 in ep_sub_ac:
                        twts.append(('ac', ep_sub_ac[si - 2]))
                    pe(lambda e, sub=sub, pb=pb, buf=buf: e.transpose(
                        pb[:, 0:128], buf[:, sub * 128:(sub + 1) * 128], ident_s[:]),
                       waits=twts, inc=True)
                    pe_tr = p.val('pe')
                    awts = [('pe', pe_tr)]
                    if th == 1 and oc == 0 and sub == 0:
                        awts.append(('wOUT', th0_out_wOUT))
                    act(lambda e, sub=sub, oc=oc, pb=pb: e.activation(
                        out=OSTG4[:, sub * D + oc * 128: sub * D + (oc + 1) * 128],
                        in_=pb[:, 0:128], func=AF.Copy),
                        waits=awts, inc=True)
                    ep_sub_ac[si] = p.val('ac')
                ep_tr_ac[i] = p.val('ac')
            ac_ep = p.val('ac')
            for sub in range(4):
                dma('gpsimd', 'wOUT',
                    out_d[th * 512 + sub * 128: th * 512 + (sub + 1) * 128, :],
                    OSTG4[:, sub * D:(sub + 1) * D],
                    waits=([('ac', ac_ep)] if sub == 0 else []))
            th0_out_wOUT = p.val('wOUT')
        wOUT_done = p.val('wOUT')

    except _StopEmit:
        pass

    sem_names = sorted(p.counts.keys())
    with contextlib.ExitStack() as stack:
        sems = {s_: stack.enter_context(nc.semaphore(s_)) for s_ in sem_names}
        block = stack.enter_context(nc.Block())

        def runner_for(name):
            def runner(eng):
                for (en, waits, fn, incs) in p.items:
                    if en != name:
                        continue
                    for (s_, v_) in waits:
                        if v_ > 0:
                            eng.wait_ge(sems[s_], v_)
                    inst = fn(eng)
                    for (s_, amt) in incs:
                        inst.then_inc(sems[s_], amt)
                if name == 'gpsimd':
                    for s_ in ('wA', 'wV', 'wG', 'wOUT'):
                        if s_ in sems and p.val(s_) > 0:
                            eng.wait_ge(sems[s_], p.val(s_))
            return runner

        block.tensor(runner_for('tensor'))
        block.vector(runner_for('vector'))
        block.scalar(runner_for('scalar'))
        block.sync(runner_for('sync'))
        block.gpsimd(runner_for('gpsimd'))

    return nc, dbg


def _host_constants():
    ident = np.eye(128, dtype=np.float32)
    selB = np.zeros((128, 16), dtype=np.float32)
    for pp in range(128):
        selB[pp, pp % 16] = 1.0
    sel2 = np.zeros((16, 128), dtype=np.float32)
    for pp in range(128):
        sel2[pp % 16, pp] = 1.0
    return ident, selB, sel2


def make_in_maps(inputs):
    x = np.ascontiguousarray(np.asarray(inputs["L0_attn_x"], dtype=np.float32)[0])
    wqkv = np.ascontiguousarray(inputs["L0_attn_w_qkv"], dtype=np.float32)
    wo_ = np.ascontiguousarray(inputs["L0_attn_w_o"], dtype=np.float32)
    rw = np.ascontiguousarray(inputs["L0_ffn_router_w"], dtype=np.float32)
    rb = np.ascontiguousarray(
        np.asarray(inputs["L0_ffn_router_b"], dtype=np.float32).reshape(NE, 1))
    we = np.ascontiguousarray(inputs["L0_ffn_expert_w"], dtype=np.float32)
    be = np.ascontiguousarray(
        np.asarray(inputs["L0_ffn_expert_b"], dtype=np.float32).reshape(-1))
    cv = np.ascontiguousarray(np.asarray(inputs["L0_c"], dtype=np.float32).reshape(-1))
    wmod = np.ascontiguousarray(inputs["L0_wmod"], dtype=np.float32)
    ident, selB, sel2 = _host_constants()
    maps = []
    for r in range(R):
        maps.append({
            "x_slice": x[r * T:(r + 1) * T],
            "w_qkv": wqkv, "w_o": wo_,
            "router_w": rw, "router_b": rb,
            "expert_w": we, "expert_b": be,
            "c_vec": cv,
            "wmod_slice": np.ascontiguousarray(wmod[:, r * 1536:(r + 1) * 1536]),
            "ident": ident, "selB": selB, "sel2": sel2,
        })
    return maps


def kernel(L0_attn_x, L0_attn_w_qkv, L0_attn_w_o, L0_ffn_router_w, L0_ffn_router_b,
           L0_ffn_expert_w, L0_ffn_expert_b, L0_c, L0_wmod):
    inputs = dict(L0_attn_x=L0_attn_x, L0_attn_w_qkv=L0_attn_w_qkv,
                  L0_attn_w_o=L0_attn_w_o, L0_ffn_router_w=L0_ffn_router_w,
                  L0_ffn_router_b=L0_ffn_router_b, L0_ffn_expert_w=L0_ffn_expert_w,
                  L0_ffn_expert_b=L0_ffn_expert_b, L0_c=L0_c, L0_wmod=L0_wmod)
    nc, _ = build_program()
    res = run_bass_kernel_spmd(nc, make_in_maps(inputs), core_ids=list(range(R)))
    out = np.concatenate([res.results[r]["out_slice"] for r in range(R)], axis=0)
    return np.ascontiguousarray(out[None]).astype(np.float32, copy=False)


# version 02
# speedup vs baseline: 1.0924x; 1.0924x over previous
import contextlib
import numpy as np
import concourse.bass as bass
import concourse.mybir as mybir
from concourse.bass_utils import run_bass_kernel_spmd

F32 = mybir.dt.float32
F32R = mybir.dt.float32r
BF16 = mybir.dt.bfloat16
AF = mybir.ActivationFunctionType
OP = mybir.AluOpType
AX = mybir.AxisListType.X

R = 8
T = 1024
D = 2048
NCD = 16
NH = 16
S = 8192
NE = 16
KSEL = 512
INV_SQRT_DH = 0.08838834764831845
EXP_SHIFT = -16.0
N_BISECT = 45


class Sched:
    def __init__(self):
        self.items = []
        self.counts = {}

    def emit(self, eng, fn, waits=(), inc=None):
        incs = []
        if inc is not None:
            sem, amt = inc
            self.counts[sem] = self.counts.get(sem, 0) + amt
            incs.append((sem, amt))
        self.items.append((eng, [w for w in waits if w is not None], fn, incs))

    def val(self, sem):
        return self.counts.get(sem, 0)


class _StopEmit(Exception):
    pass


def build_program(debug_taps=(), stop_after=None, debug_dram=False, h1_mode=3, h1_upto=16, qkv_upto=48, qkv_mode=3, heads_upto=16, f_mode=3):
    nc = bass.Bass("TRN2", target_bir_lowering=False, detect_race_conditions=False)
    debug_taps = list(debug_taps)
    dbg_kind = "ExternalOutput" if debug_dram else "Internal"

    x_in = nc.dram_tensor("x_slice", [T, D], F32, kind="ExternalInput").ap()
    wqkv = nc.dram_tensor("w_qkv", [D, 3 * D], F32, kind="ExternalInput").ap()
    wo = nc.dram_tensor("w_o", [D, D], F32, kind="ExternalInput").ap()
    rwd = nc.dram_tensor("router_w", [D, NE], F32, kind="ExternalInput").ap()
    rbd = nc.dram_tensor("router_b", [NE, 1], F32, kind="ExternalInput").ap()
    wed = nc.dram_tensor("expert_w", [D, D], F32, kind="ExternalInput").ap()
    bed = nc.dram_tensor("expert_b", [D], F32, kind="ExternalInput").ap()
    cvd = nc.dram_tensor("c_vec", [D], F32, kind="ExternalInput").ap()
    wmodd = nc.dram_tensor("wmod_slice", [D, 1536], F32, kind="ExternalInput").ap()
    identd = nc.dram_tensor("ident", [128, 128], F32, kind="ExternalInput").ap()
    selBd = nc.dram_tensor("selB", [128, 16], F32, kind="ExternalInput").ap()
    sel2d = nc.dram_tensor("sel2", [16, 128], F32, kind="ExternalInput").ap()
    out_d = nc.dram_tensor("out_slice", [T, D], F32, kind="ExternalOutput").ap()

    kag_in = nc.dram_tensor("kag_in", [2, NH, 128, T], BF16)
    kag_dbg = (nc.dram_tensor("kag_dbg", [2, NH, 128, T], BF16,
               kind="ExternalOutput") if debug_dram else None)
    cs_dbg = (nc.dram_tensor("cs_dbg", [NH, T], F32,
              kind="ExternalOutput") if debug_dram else None)
    cs0_dbg = (nc.dram_tensor("cs0_dbg", [4, 2048], F32,
               kind="ExternalOutput") if debug_dram else None)
    z0_dbg = (nc.dram_tensor("z0_dbg", [128, 16], F32,
              kind="ExternalOutput") if debug_dram else None)
    e0_dbg = (nc.dram_tensor("e0_dbg", [128, S], BF16,
              kind="ExternalOutput") if debug_dram else None)
    z2_dbg = (nc.dram_tensor("z2_dbg", [128, 24], F32,
              kind="ExternalOutput") if debug_dram else None)
    kag_out = nc.dram_tensor("kag_out", [R, 2, NH, 128, T], BF16, addr_space="Shared")
    q_sp = nc.dram_tensor("q_sp", [2, NH, 128, T], BF16, kind=dbg_kind)
    v_sp = nc.dram_tensor("v_sp", [2, NH, 128, T], BF16, kind=dbg_kind)
    val_sp = nc.dram_tensor("val_sp", [2, NH, 128, T], BF16, kind=dbg_kind)
    xT_sp = nc.dram_tensor("xT_sp", [NCD, 128, T], F32, kind=dbg_kind)
    x1_sp = nc.dram_tensor("x1_sp", [NCD, 128, T], F32, kind=dbg_kind)
    cs_rs_in = nc.dram_tensor("cs_rs_in", [R, NH, T], F32)
    cs_rs_out = nc.dram_tensor("cs_rs_out", [NH, T], F32)
    s_ag_in = nc.dram_tensor("s_ag_in", [NE, T], F32)
    s_ag_out = nc.dram_tensor("s_ag_out", [R, NE, T], F32, addr_space="Shared")
    mod_ag_in = nc.dram_tensor("mod_ag_in", [1536], F32)
    mod_ag_out = nc.dram_tensor("mod_ag_out", [R, 1536], F32, addr_space="Shared")

    dbg = {}
    for name, shape in debug_taps:
        dbg[name] = nc.dram_tensor("dbg_" + name, list(shape), F32,
                                   kind="ExternalOutput").ap()

    p = Sched()

    def sb(name, shape, dtype):
        return nc.alloc_sbuf_tensor(name, shape, dtype)

    ident_s = sb("ident_s", [128, 128], F32)
    selB_s = sb("selB_s", [128, 16], F32)
    sel2_s = sb("sel2_s", [16, 128], F32)
    ones_row = sb("ones_row", [1, 128], F32)
    ones_col = sb("ones_col", [128, 1], F32)
    ones16 = sb("ones16", [16, 1], F32)
    cstat = sb("cstat", [128, 1], F32)
    biasC = sb("biasC", [128, 1], F32)
    epsb = sb("epsb", [1, 1], F32)
    c512 = sb("c512", [16, 1], F32)
    half16 = sb("half16", [16, 1], F32)

    XK = sb("XK", [128, 16 * T], F32)
    h1h_t = sb("h1h", [128, 16 * T], BF16)
    h1l_t = sb("h1l", [128, 16 * T], BF16)
    wstg = sb("wstg", [128, D], F32)
    wh_t = sb("wh", [128, 2 * D], BF16)
    wl_t = sb("wl", [128, 2 * D], BF16)
    scr32 = sb("scr32", [128, T], F32)
    scr32b = sb("scr32b", [128, T], F32)
    scrh = sb("scrh", [128, 2 * T], BF16)
    scrl = sb("scrl", [128, 2 * T], BF16)
    qh_t = sb("qh_t", [128, 2 * T], BF16)
    ql_t = sb("ql_t", [128, 2 * T], BF16)
    zpart = sb("zpart", [128, 8], F32)
    zrowv = sb("zrowv", [128, 1], F32)
    zrecip = sb("zrecip", [128, 1], F32)
    zrh_b = sb("zrh_b", [128, 1], BF16)
    zrl_b = sb("zrl_b", [128, 1], BF16)
    zrselH = [sb("zrselH%d" % m, [128, 4], BF16) for m in range(4)]
    zrselL = [sb("zrselL%d" % m, [128, 4], BF16) for m in range(4)]
    modT = sb("modT", [128, 96], F32)
    modp = sb("modp", [128, 96], F32)
    cv_sb = sb("cv_sb", [16, 128], F32)
    cT_silu = sb("cT_silu", [128, 16], F32)
    mod_all = sb("mod_all", [96, 128], F32)
    stat_all = sb("stat_all", [1, 2 * T], F32)
    xstg = sb("xstg", [128, D], F32)
    xstg2 = sb("xstg2", [128, D], F32)
    rwh = sb("rwh", [128, 256], BF16)
    rwl = sb("rwl", [128, 256], BF16)
    rbT = sb("rbT", [16, 1], F32)
    beT = sb("beT", [128, 16], F32)
    smax = sb("smax", [128, 384], F32)
    thr128 = sb("thr128", [128, 1], F32)
    lo16 = sb("lo16", [16, 1], F32)
    hi16 = sb("hi16", [16, 1], F32)
    mid16 = sb("mid16", [16, 1], F32)
    pred16 = sb("pred16", [16, 1], mybir.dt.int32)
    sum16 = sb("sum16", [16, 1], F32)
    rc_t = sb("rc_t", [128, 1], F32)

    psW0 = nc.alloc_psum_tensor("psW0", [128, 1024], F32)
    psW1 = nc.alloc_psum_tensor("psW1", [128, 1024], F32)
    ps0 = nc.alloc_psum_tensor("ps0", [128, 512], F32)
    ps1 = nc.alloc_psum_tensor("ps1", [128, 512], F32)
    ps2 = nc.alloc_psum_tensor("ps2", [128, 512], F32)
    ps3 = nc.alloc_psum_tensor("ps3", [128, 512], F32)
    psCS = [ps0, ps1, ps2, ps3]

    xT = XK.ap()
    K_bufs = XK.ap().bitcast(BF16)
    vth = XK.ap().bitcast(BF16)
    U2 = XK.ap()
    OSTG4 = h1h_t.ap().bitcast(F32)
    E_t = h1h_t.ap()[:, 0:S]
    wmod_sb = wh_t.ap().bitcast(F32)
    mod_row = stat_all[0:1, 0:1536]
    stat_mu = stat_all[0:1, 0:T]
    stat_rstd = stat_all[0:1, T:2 * T]
    cmp_t = xstg2.ap()[:, 1024:2048]
    masked = xstg2[0:16, 1024:2048]
    S_lay = scrl.ap().bitcast(F32)
    ST_sb = scr32b[0:16, :]
    logT = S_lay[0:16, :]
    sqs = scrh.ap().bitcast(F32)
    rw_sb = wstg[:, 0:256]
    cs_stage = xstg[0:4, 0:2048]
    cs_mv = ql_t.ap().bitcast(F32)[0:1, :]
    stat_t2 = cs_mv
    w_sb = cs_mv

    def kslice(par, half):
        off = (par * 2 + half) * S
        return K_bufs[:, off:off + S]

    def dma(qeng, sem, out, in_, waits=()):
        p.emit(qeng, lambda e, o=out, i=in_: e.dma_start(out=o, in_=i),
               waits=waits, inc=(sem, 16))

    def act(fn, waits=(), inc=False):
        p.emit('scalar', fn, waits=waits, inc=(('ac', 1) if inc else None))

    def dve(fn, waits=(), inc=False):
        p.emit('vector', fn, waits=waits, inc=(('dv', 1) if inc else None))

    def pe(fn, waits=(), inc=False):
        p.emit('tensor', fn, waits=waits, inc=(('pe', 1) if inc else None))

    def gp(fn, waits=(), inc=None):
        p.emit('gpsimd', fn, waits=waits, inc=inc)

    def mcol(base, c):
        return modp[:, base + c: base + c + 1]

    PH_ORDER = ['0', 'A1', 'A2', 'A3', 'A', 'B', 'C', 'D', 'E', 'F']

    def stop(ph):
        if (stop_after is not None
                and PH_ORDER.index(ph) > PH_ORDER.index(stop_after)):
            raise _StopEmit()

    try:
        dve(lambda e: e.memset(ones_row[:], 1.0))
        dve(lambda e: e.memset(ones_col[:], 1.0))
        dve(lambda e: e.memset(ones16[:], 1.0))
        dve(lambda e: e.memset(biasC[:], EXP_SHIFT))
        dve(lambda e: e.memset(epsb[:], 1e-5))
        dve(lambda e: e.memset(c512[:], float(KSEL)))
        dve(lambda e: e.memset(half16[:], 0.5))
        for m in range(4):
            dve(lambda e, m=m: e.memset(zrselH[m][:], 0.0))
            dve(lambda e, m=m: e.memset(zrselL[m][:], 0.0), inc=(m == 3))
        dv_consts = p.val('dv')

        dma('sync', 'wA', ident_s[:], identd)
        dma('sync', 'wA', selB_s[:], selBd)
        dma('sync', 'wA', sel2_s[:], sel2d)
        dma('sync', 'wA', cv_sb[:], cvd.rearrange("(a b) -> a b", b=128))
        dma('sync', 'wA', rbT[:], rbd)
        wA_consts = p.val('wA')

        pe(lambda e: e.transpose(psW0[:, 0:16], cv_sb[:], ident_s[0:16, 0:16]),
           waits=[('wA', wA_consts)], inc=True)
        pe_cT = p.val('pe')
        act(lambda e: e.activation(out=cT_silu[:], in_=psW0[:, 0:16], func=AF.Silu),
            waits=[('pe', pe_cT)], inc=True)
        ac_silu = p.val('ac')
        dma('sync', 'wA', cv_sb[:], bed.rearrange("(a b) -> a b", b=128),
            waits=[('pe', pe_cT)])
        wA_be = p.val('wA')
        pe(lambda e: e.transpose(ps3[:, 0:16], cv_sb[:], ident_s[0:16, 0:16]),
           waits=[('wA', wA_be)], inc=True)
        pe_beT = p.val('pe')
        act(lambda e: e.activation(out=beT[:], in_=ps3[:, 0:16], func=AF.Copy),
            waits=[('pe', pe_beT)], inc=True)
        ac_beT = p.val('ac')

        for dc in range(NCD):
            dma('sync', 'wA', wmod_sb[:, 0:1536],
                wmodd[dc * 128:(dc + 1) * 128, :],
                waits=([('pe', p.val('pe'))] if dc >= 1 else []))
            wA_mod = p.val('wA')
            dve(lambda e, dc=dc: e.tensor_copy(cstat[:], cT_silu[:, dc:dc + 1]),
                waits=[('ac', ac_silu)] + ([('pe', p.val('pe'))] if dc > 0 else []),
                inc=True)
            dv_c = p.val('dv')
            for j in range(3):
                mpst = (psW1[0:1, 0:512], psW1[0:1, 512:1024], ps2[0:1, 0:512])[j]
                pe(lambda e, j=j, dc=dc, mpst=mpst: e.matmul(
                    mpst, cstat[:],
                    wmod_sb[:, j * 512:(j + 1) * 512],
                    start=(dc == 0), stop=(dc == NCD - 1)),
                   waits=[('wA', wA_mod), ('dv', dv_c)], inc=(j == 2))
        pe_mod = p.val('pe')
        for j in range(3):
            mpst = (psW1[0:1, 0:512], psW1[0:1, 512:1024], ps2[0:1, 0:512])[j]
            act(lambda e, j=j, mpst=mpst: e.activation(
                out=mod_row[:, j * 512:(j + 1) * 512], in_=mpst,
                func=AF.Copy), waits=([('pe', pe_mod)] if j == 0 else []), inc=(j == 2))
        ac_modrow = p.val('ac')
        dma('gpsimd', 'wG', mod_ag_in.ap(), mod_row, waits=[('ac', ac_modrow)])
        wG_mod = p.val('wG')
        gp(lambda e: e.collective_compute(
            "AllGather", OP.bypass, replica_groups=[list(range(R))],
            ins=[mod_ag_in.ap().opt()], outs=[mod_ag_out.ap().opt()]),
           waits=[('wG', wG_mod)], inc=('cc', 1))
        cc_mod = p.val('cc')
        dma('sync', 'wA', mod_all[:],
            mod_ag_out.ap().rearrange("r (a b) -> (r a) b", b=128),
            waits=[('cc', cc_mod)])
        wA_modall = p.val('wA')
        pe(lambda e: e.transpose(psW0[:, 0:96], mod_all[:], ident_s[0:96, 0:96]),
           waits=[('wA', wA_modall), ('ac', ac_silu)], inc=True)
        pe_modT = p.val('pe')
        act(lambda e: e.activation(out=modT[:], in_=psW0[:, 0:96], func=AF.Copy),
            waits=[('pe', pe_modT)], inc=True)
        ac_modT = p.val('ac')
        dve(lambda e: e.tensor_copy(modp[:, 0:16], modT[:, 0:16]),
            waits=[('ac', ac_modT)])
        dve(lambda e: e.tensor_scalar(modp[:, 16:32], modT[:, 16:32], 1.0, None,
                                      op0=OP.add))
        dve(lambda e: e.tensor_copy(modp[:, 48:64], modT[:, 48:64]))
        dve(lambda e: e.tensor_scalar(modp[:, 64:80], modT[:, 64:80], 1.0, None,
                                      op0=OP.add))
        for base in (32, 80):
            dve(lambda e, s=base: e.tensor_scalar(
                scr32[:, 32:48], modT[:, s:s + 16], -1.0, None, op0=OP.mult),
                inc=True)
            dve(lambda e, s=base: e.tensor_tensor(
                scr32[:, 0:16], modT[:, s:s + 16], scr32[:, 32:48], op=OP.max),
                waits=[('dv', p.val('dv'))], inc=True)
            dve(lambda e: e.tensor_scalar(
                scr32[:, 0:16], scr32[:, 0:16], 0.25, None, op0=OP.max),
                waits=[('dv', p.val('dv'))], inc=True)
            dve(lambda e, s=base: e.tensor_scalar(
                scr32[:, 16:32], modT[:, s:s + 16], 0.0, None, op0=OP.is_ge),
                inc=True)
            dve(lambda e: e.tensor_scalar(
                scr32[:, 16:32], scr32[:, 16:32], 2.0, -1.0, op0=OP.mult, op1=OP.add),
                waits=[('dv', p.val('dv'))], inc=True)
            dve(lambda e, d=base: e.tensor_tensor(
                modp[:, d:d + 16], scr32[:, 0:16], scr32[:, 16:32], op=OP.mult),
                waits=[('dv', p.val('dv'))], inc=True)
        dv_modp = p.val('dv')

        stop('A')
        tr_evac_ac = {}
        for tc in range(8):
            stg = xstg if tc % 2 == 0 else xstg2
            dma('sync', 'wA', stg[:], x_in[tc * 128:(tc + 1) * 128, :],
                waits=([('pe', p.val('pe'))] if tc >= 2 else []))
            wA_x = p.val('wA')
            for dc in range(NCD):
                i = tc * NCD + dc
                pb = psW0 if i % 2 == 0 else psW1
                wts = [('wA', wA_x)]
                if i == 0:
                    wts.append(('ac', ac_modT))
                elif i == 1:
                    wts.append(('ac', ac_modrow))
                elif (i - 2) in tr_evac_ac:
                    wts.append(('ac', tr_evac_ac[i - 2]))
                pe(lambda e, stg=stg, dc=dc, pb=pb: e.transpose(
                    pb[:, 0:128], stg[:, dc * 128:(dc + 1) * 128], ident_s[:]),
                   waits=wts, inc=True)
                pe_tr = p.val('pe')
                act(lambda e, dc=dc, tc=tc, pb=pb: e.activation(
                    out=xT[:, dc * T + tc * 128: dc * T + (tc + 1) * 128],
                    in_=pb[:, 0:128], func=AF.Copy),
                    waits=[('pe', pe_tr)], inc=True)
                tr_evac_ac[i] = p.val('ac')
        ac_xT = p.val('ac')

        for dc in range(NCD):
            dma('gpsimd', 'wG', xT_sp[dc], xT[:, dc * T:(dc + 1) * T],
                waits=([('ac', ac_xT)] if dc == 0 else []))
        wG_xT = p.val('wG')
        stop('A1')

        pa_stat_pe = 0
        for dc in range(NCD):
            xc = xT[:, dc * T:(dc + 1) * T]
            act(lambda e, xc=xc: e.activation(out=sqs[:], in_=xc, func=AF.Square),
                waits=([('pe', pa_stat_pe)] if pa_stat_pe else []), inc=True)
            ac_sq = p.val('ac')
            for thf in range(2):
                pe(lambda e, xc=xc, thf=thf, dc=dc: e.matmul(
                    (ps2 if thf == 0 else ps3)[0:1, :], ones_col[:],
                    xc[:, thf * 512:(thf + 1) * 512],
                    start=(dc == 0), stop=(dc == NCD - 1)),
                   waits=([('ac', ac_xT)] if dc == 0 and thf == 0 else []))
                pe(lambda e, thf=thf, dc=dc: e.matmul(
                    psW1[0:1, thf * 512:(thf + 1) * 512], ones_col[:],
                    sqs[:, thf * 512:(thf + 1) * 512],
                    start=(dc == 0), stop=(dc == NCD - 1)),
                   waits=[('ac', ac_sq)], inc=True)
            pa_stat_pe = p.val('pe')

        def ln_finish():
            pe_st = p.val('pe')
            act(lambda e: e.activation(out=stat_mu[:, 0:512], in_=ps2[0:1, :],
                                       func=AF.Copy, scale=1.0 / D),
                waits=[('pe', pe_st)])
            act(lambda e: e.activation(out=stat_mu[:, 512:1024], in_=ps3[0:1, :],
                                       func=AF.Copy, scale=1.0 / D))
            act(lambda e: e.activation(out=stat_t2[:, 0:512], in_=psW1[0:1, 0:512],
                                       func=AF.Copy, scale=1.0 / D))
            act(lambda e: e.activation(out=stat_t2[:, 512:1024], in_=psW1[0:1, 512:1024],
                                       func=AF.Copy, scale=1.0 / D), inc=True)
            ac_st = p.val('ac')
            dve(lambda e: e.tensor_tensor(stat_rstd[:], stat_mu[:], stat_mu[:],
                                          op=OP.mult), waits=[('ac', ac_st)])
            dve(lambda e: e.tensor_tensor(stat_t2[:], stat_t2[:], stat_rstd[:],
                                          op=OP.subtract), inc=True)
            dv_v = p.val('dv')
            act(lambda e: e.activation(out=stat_t2[:], in_=stat_t2[:], func=AF.Sqrt,
                                       bias=epsb[:]), waits=[('dv', dv_v)], inc=True)
            ac_sd = p.val('ac')
            dve(lambda e: e.reciprocal(stat_rstd[:], stat_t2[:]),
                waits=[('ac', ac_sd)], inc=True)
            dve(lambda e: e.tensor_tensor(stat_mu[:], stat_mu[:], stat_rstd[:],
                                          op=OP.mult), waits=[('dv', p.val('dv'))],
                inc=True)
            dv_r = p.val('dv')
            for tc in range(8):
                pe(lambda e, tc=tc: e.matmul(
                    psW0[:, tc * 128:(tc + 1) * 128], ones_row[:],
                    stat_rstd[0:1, tc * 128:(tc + 1) * 128], start=True, stop=True),
                   waits=([('dv', dv_r)] if tc == 0 else []))
                pe(lambda e, tc=tc: e.matmul(
                    psW1[:, tc * 128:(tc + 1) * 128], ones_row[:],
                    stat_mu[0:1, tc * 128:(tc + 1) * 128], start=True, stop=True),
                   inc=(tc == 7))
            return p.val('pe')

        stop('A2')
        pe_bc1 = ln_finish()
        stop('A3')

        h1f_ac = {}
        for dc in range(min(NCD, h1_upto)):
            buf = scr32b if dc % 2 == 0 else sqs
            wts = [('pe', pe_bc1)]
            if dc - 2 >= 0:
                wts.append(('ac', h1f_ac[dc - 2]))
            for hf in range(2):
                dve(lambda e, dc=dc, hf=hf: e.tensor_tensor(
                    scr32[:, hf * 512:(hf + 1) * 512],
                    xT[:, dc * T + hf * 512: dc * T + hf * 512 + 512],
                    psW0[:, hf * 512:(hf + 1) * 512], op=OP.mult),
                    waits=(wts if hf == 0 else []))
                dve(lambda e, hf=hf: e.tensor_tensor(
                    scr32[:, hf * 512:(hf + 1) * 512], scr32[:, hf * 512:(hf + 1) * 512],
                    psW1[:, hf * 512:(hf + 1) * 512], op=OP.subtract))
            if h1_mode >= 1:
                dve(lambda e, dc=dc, buf=buf: e.tensor_scalar(
                    buf[:, 0:T], scr32[:], mcol(16, dc), mcol(0, dc),
                    op0=OP.mult, op1=OP.add), waits=[('dv', dv_modp)], inc=True)
            else:
                dve(lambda e, dc=dc, buf=buf: e.tensor_copy(buf[:, 0:T], scr32[:]),
                    waits=[('dv', dv_modp)], inc=True)
            dv_h1 = p.val('dv')
            if h1_mode >= 2:
                act(lambda e, dc=dc, buf=buf: e.activation(
                    out=h1h_t[:, dc * T:(dc + 1) * T], in_=buf[:, 0:T], func=AF.Copy),
                    waits=[('dv', dv_h1)], inc=True)
            else:
                act(lambda e: e.copy(zrowv[:], zrecip[:]), waits=[('dv', dv_h1)],
                    inc=True)
            h1f_ac[dc] = p.val('ac')
            if h1_mode >= 3:
                dve(lambda e, dc=dc, buf=buf: e.tensor_tensor(
                    h1l_t[:, dc * T:(dc + 1) * T], buf[:, 0:T],
                    h1h_t[:, dc * T:(dc + 1) * T], op=OP.subtract),
                    waits=[('ac', h1f_ac[dc])], inc=True)
            else:
                dve(lambda e: e.tensor_copy(zrowv[:], zrecip[:]),
                    waits=[('ac', h1f_ac[dc])], inc=True)
        dv_h1_done = p.val('dv')

        stop('B')
        def stream_weight(src_col_ap, b, act_extra=(), pe_gate=None):
            dma('sync', 'wA', wstg[:],
                src_col_ap.rearrange("(a b) c -> b a c", b=128),
                waits=[('dv', stream_weight.stg_dv)])
            wA_w = p.val('wA')
            gw = [('pe', pe_gate)] if pe_gate else []
            dve(lambda e, b=b: e.tensor_copy(wh_t[:, b * D:(b + 1) * D], wstg[:]),
                waits=[('wA', wA_w)] + list(act_extra) + gw, inc=True)
            dve(lambda e, b=b: e.tensor_tensor(
                wl_t[:, b * D:(b + 1) * D], wstg[:], wh_t[:, b * D:(b + 1) * D],
                op=OP.subtract), inc=True)
            stream_weight.stg_dv = p.val('dv')
            return p.val('dv')

        stream_weight.stg_dv = 0

        def mm3(pst, b, dc, mov_h, mov_l, first, last, wts):
            whap = wh_t[:, b * D + dc * 128: b * D + (dc + 1) * 128]
            wlap = wl_t[:, b * D + dc * 128: b * D + (dc + 1) * 128]
            pe(lambda e, w=whap, m=mov_h, pst=pst, st=first: e.matmul(
                pst, w, m, start=st, stop=False), waits=(wts if first else []))
            pe(lambda e, w=whap, m=mov_l, pst=pst: e.matmul(
                pst, w, m, start=False, stop=False))
            pe(lambda e, w=wlap, m=mov_h, pst=pst, sp=last: e.matmul(
                pst, w, m, start=False, stop=sp), inc=last)

        oc_order = list(range(16, 32)) + list(range(32, 48)) + list(range(0, 16))
        qkv_s32_dv = {}
        qkv_ps = [psW0.ap()[:, 0:512], psW0.ap()[:, 512:1024],
                  psW1.ap()[:, 0:512], psW1.ap()[:, 512:1024]]
        last_evac = {}
        spill_wG = {}
        kag_cc = None

        qkv_tile_pe = {}
        for oi, oc in enumerate(oc_order[:qkv_upto]):
            b = oi % 2
            dv_w = stream_weight(wqkv[:, oc * 128:(oc + 1) * 128], b,
                                 act_extra=([('pe', pe_mod)] if oi < 2 else []),
                                 pe_gate=qkv_tile_pe.get(oi - 2))
            for th in range(2):
                pidx = (oi * 2 + th) % 4
                pst = qkv_ps[pidx]
                wts = [('dv', dv_w)]
                if pidx in last_evac:
                    wts.append(last_evac[pidx])
                if oi == 0 and th == 0:
                    wts.append(('dv', dv_h1_done))
                for dc in range(NCD):
                    mm3(pst, b, dc,
                        h1h_t[:, dc * T + th * 512: dc * T + th * 512 + 512],
                        h1l_t[:, dc * T + th * 512: dc * T + th * 512 + 512],
                        dc == 0, dc == NCD - 1, wts)
                pe_q = p.val('pe')
                qkv_tile_pe[oi] = pe_q
                sh = scrh[:, b * T + th * 512: b * T + th * 512 + 512]
                sl = scrl[:, b * T + th * 512: b * T + th * 512 + 512]
                s32 = scr32.ap()[:, th * 512: th * 512 + 512]
                awts = [('pe', pe_q)]
                if (oi * 2 + th - 2) in qkv_s32_dv:
                    awts.append(('dv', qkv_s32_dv[oi * 2 + th - 2]))
                act(lambda e, s32=s32, pst=pst: e.activation(
                    out=s32, in_=pst, func=AF.Copy),
                    waits=awts, inc=True)
                last_evac[pidx] = ('ac', p.val('ac'))
                ac_e = p.val('ac')
                dwts = [('ac', ac_e)]
                if oi - 2 >= 0:
                    dwts.append(('wG', spill_wG[oi - 2]))
                dve(lambda e, sh=sh, s32=s32: e.tensor_copy(sh, s32),
                    waits=dwts, inc=True)
                dve(lambda e, sl=sl, s32=s32, sh=sh: e.tensor_tensor(
                    sl, s32, sh, op=OP.subtract), inc=True)
                qkv_s32_dv[oi * 2 + th] = p.val('dv')
            dv_oc = p.val('dv')
            if oc >= 32:
                dsth, dstl = v_sp[0, oc - 32], v_sp[1, oc - 32]
            elif oc >= 16:
                dsth, dstl = kag_in[0, oc - 16], kag_in[1, oc - 16]
            else:
                dsth, dstl = q_sp[0, oc], q_sp[1, oc]
            dma('gpsimd', 'wG', dsth, scrh[:, b * T:(b + 1) * T],
                waits=[('dv', dv_oc)])
            dma('gpsimd', 'wG', dstl, scrl[:, b * T:(b + 1) * T])
            spill_wG[oi] = p.val('wG')
            if oi == 15:
                gp(lambda e: e.collective_compute(
                    "AllGather", OP.bypass, replica_groups=[list(range(R))],
                    ins=[kag_in.ap().opt()], outs=[kag_out.ap().opt()]),
                   waits=[('wG', spill_wG[15])], inc=('cc', 1))
                kag_cc = p.val('cc')

        stop('C')
        exp_ac = {}
        pending_colsum = [None]
        head_end_pe = {}
        csel_done = {}
        cs_dma_wG = {}
        last_colsum_pe = 0
        qkv_evac_gates = list(last_evac.values())

        for h in range(min(NH, heads_upto)):
            par = h % 2
            kwts = [('cc', kag_cc)]
            if h >= 2:
                kwts.append(('pe', head_end_pe[h - 2]))
            for r in range(R):
                dma('scalar', 'wV', kslice(par, 0)[:, r * T:(r + 1) * T],
                    kag_out[r, 0, h], waits=(kwts if r == 0 else []))
                dma('scalar', 'wV', kslice(par, 1)[:, r * T:(r + 1) * T],
                    kag_out[r, 1, h])
            dma('scalar', 'wV', qh_t[:, par * T:(par + 1) * T], q_sp[0, h],
                waits=[('wG', spill_wG[32 + h])])
            dma('scalar', 'wV', ql_t[:, par * T:(par + 1) * T], q_sp[1, h])
            kv = p.val('wV')

            for pnl in range(8):
                qh_ap = qh_t[:, par * T + pnl * 128: par * T + pnl * 128 + 128]
                ql_ap = ql_t[:, par * T + pnl * 128: par * T + pnl * 128 + 128]
                for g in range(8):
                    pw = psW0 if g % 2 == 0 else psW1
                    wts = [('wV', kv)]
                    if h == 0 and pnl == 0 and g < 2:
                        wts.extend(qkv_evac_gates)
                    if g >= 2:
                        key = (h, pnl, g - 2)
                    elif pnl > 0:
                        key = (h, pnl - 1, g + 6)
                    else:
                        key = (h - 1, 7, g + 6)
                    if key in exp_ac:
                        wts.append(('ac', exp_ac[key]))
                    kh0 = kslice(par, 0)[:, g * 1024: g * 1024 + 512]
                    kh1 = kslice(par, 0)[:, g * 1024 + 512: g * 1024 + 1024]
                    kl0 = kslice(par, 1)[:, g * 1024: g * 1024 + 512]
                    kl1 = kslice(par, 1)[:, g * 1024 + 512: g * 1024 + 1024]
                    pe(lambda e, q=qh_ap, k=kh0, pw=pw: e.matmul(
                        pw[:, 0:512], q, k, start=True, stop=False), waits=wts)
                    pe(lambda e, q=qh_ap, k=kh1, pw=pw: e.matmul(
                        pw[:, 512:1024], q, k, start=True, stop=False))
                    pe(lambda e, q=qh_ap, k=kl0, pw=pw: e.matmul(
                        pw[:, 0:512], q, k, start=False, stop=False))
                    pe(lambda e, q=qh_ap, k=kl1, pw=pw: e.matmul(
                        pw[:, 512:1024], q, k, start=False, stop=False))
                    pe(lambda e, q=ql_ap, k=kh0, pw=pw: e.matmul(
                        pw[:, 0:512], q, k, start=False, stop=True))
                    pe(lambda e, q=ql_ap, k=kh1, pw=pw: e.matmul(
                        pw[:, 512:1024], q, k, start=False, stop=True), inc=True)
                    pe_sc = p.val('pe')
                    ewts = [('pe', pe_sc)]
                    if g == 0 and last_colsum_pe:
                        ewts.append(('pe', last_colsum_pe))
                    eoff = ((h * 8 + pnl) % 2) * S
                    act(lambda e, g=g, pw=pw, eoff=eoff: e.activation(
                        out=h1h_t[:, eoff + g * 1024: eoff + (g + 1) * 1024],
                        in_=pw[:], func=AF.Exp, scale=INV_SQRT_DH, bias=biasC[:],
                        accum_out=zpart[:, g:g + 1]), waits=ewts, inc=True)
                    exp_ac[(h, pnl, g)] = p.val('ac')
                act(lambda e: e.copy(smax[:, 216:217], zpart[:, 0:1]), inc=True)
                ac_exp = p.val('ac')
                if pending_colsum[0] is not None:
                    pending_colsum[0]()
                    pending_colsum[0] = None
                    last_colsum_pe = p.val('pe')
                if debug_dram and h == 0 and pnl == 0:
                    dma('gpsimd', 'wG', e0_dbg.ap(), E_t[:, 0:S],
                        waits=[('ac', p.val('ac'))])
                dve(lambda e: e.tensor_reduce(zrowv[:], zpart[:], axis=AX, op=OP.add),
                    waits=[('ac', ac_exp), ('dv', dv_consts),
                           (('pe', last_colsum_pe) if last_colsum_pe else None)],
                    inc=True)
                dve(lambda e: e.reciprocal(zrecip[:], zrowv[:]),
                    waits=[('dv', p.val('dv'))], inc=True)
                if debug_dram and h == 0:
                    dve(lambda e, pnl=pnl: e.tensor_copy(
                        smax[:, 200 + pnl: 201 + pnl], zrowv[:]))
                    dve(lambda e, pnl=pnl: e.tensor_copy(
                        smax[:, 208 + pnl: 209 + pnl], zrecip[:]))
                dve(lambda e: e.tensor_copy(zrh_b[:], zrecip[:]),
                    waits=[('dv', p.val('dv'))], inc=True)
                dve(lambda e: e.tensor_tensor(zrl_b[:], zrecip[:], zrh_b[:],
                                              op=OP.subtract),
                    waits=[('dv', p.val('dv'))], inc=True)
                dve(lambda e: e.engine_nop(), waits=[('dv', p.val('dv'))])
                if debug_dram and h == 0:
                    dve(lambda e, pnl=pnl: e.tensor_copy(
                        smax[:, 230 + pnl: 231 + pnl], zrh_b[:]))
                    dve(lambda e, pnl=pnl: e.tensor_copy(
                        smax[:, 240 + pnl: 241 + pnl], zrl_b[:]))
                for m in range(4):
                    dve(lambda e, m=m: e.tensor_copy(zrselH[m][:, m:m + 1], zrh_b[:]))
                    dve(lambda e, m=m: e.tensor_copy(zrselL[m][:, m:m + 1], zrl_b[:]),
                        inc=(m == 3))
                dv_zr = p.val('dv')
                eoff2 = ((h * 8 + pnl) % 2) * S
                def emit_colsum(pnl=pnl, dv_zr=dv_zr, csgate=dict(csel_done),
                                eoff=eoff2):
                    for j in range(16):
                        wts = [('dv', dv_zr)]
                        if pnl == 0 and (j // 4) in csgate:
                            wts.append(csgate[j // 4])
                        pe(lambda e, j=j, pnl=pnl, eoff=eoff: e.matmul(
                            psCS[j // 4][0:4, :], zrselH[j % 4][:],
                            h1h_t[:, eoff + j * 512: eoff + (j + 1) * 512],
                            start=(pnl == 0 and j % 4 == 0), stop=False),
                           waits=wts)
                        pe(lambda e, j=j, pnl=pnl, eoff=eoff: e.matmul(
                            psCS[j // 4][0:4, :], zrselL[j % 4][:],
                            h1h_t[:, eoff + j * 512: eoff + (j + 1) * 512],
                            start=False, stop=(pnl == 7 and j % 4 == 3)),
                           inc=(j == 15))
                pending_colsum[0] = emit_colsum
            if pending_colsum[0] is not None:
                pending_colsum[0]()
                pending_colsum[0] = None
                last_colsum_pe = p.val('pe')
            head_end_pe[h] = last_colsum_pe
            for bb in range(4):
                ewts = [('pe', last_colsum_pe)] if bb == 0 else []
                if bb == 0 and h - 1 in cs_dma_wG:
                    ewts.append(('wG', cs_dma_wG[h - 1]))
                act(lambda e, bb=bb: e.activation(
                    out=cs_stage[:, bb * 512:(bb + 1) * 512],
                    in_=psCS[bb][0:4, :], func=AF.Copy), waits=ewts, inc=True)
                csel_done[bb] = ('ac', p.val('ac'))
            ac_cse = p.val('ac')
            if debug_dram and h == 0:
                dma('gpsimd', 'wG', cs0_dbg.ap(), cs_stage[:, :],
                    waits=[('ac', ac_cse)])
            for j in range(16):
                csrc = cs_stage[j % 4: j % 4 + 1,
                                512 * (j // 4): 512 * (j // 4) + 512]
                dst = cs_rs_in[j // 2, h, 512 * (j % 2): 512 * (j % 2) + 512]
                dma('gpsimd', 'wG', dst, csrc,
                    waits=([('ac', ac_cse)] if j == 0 else []))
            cs_dma_wG[h] = p.val('wG')
        wG_cs = p.val('wG')

        gp(lambda e: e.collective_compute(
            "ReduceScatter", OP.add, replica_groups=[list(range(R))],
            ins=[cs_rs_in.ap().opt()], outs=[cs_rs_out.ap().opt()]),
           waits=[('wG', wG_cs)], inc=('cc', 1))
        cc_cs = p.val('cc')

        if debug_dram:
            dma('gpsimd', 'wG', z2_dbg.ap(), smax[:, 224:248],
                waits=[('dv', p.val('dv'))])
            dma('gpsimd', 'wG', z0_dbg.ap(), smax[:, 200:216],
                waits=[('dv', p.val('dv'))])
            dma('gpsimd', 'wG', xstg2[0:16, 1024:2048], cs_rs_out.ap(),
                waits=[('cc', cc_cs)])
            dma('gpsimd', 'wG', cs_dbg.ap(), xstg2[0:16, 1024:2048],
                waits=[('wG', p.val('wG'))])

        stop('D')
        val_spill_wG = {}
        pe_csb_prev = None
        dv_val_prev = None
        for h in range(NH):
            par = h % 2
            vwts = [('pe', head_end_pe[max(head_end_pe)])] if h == 0 else []
            if h >= 2:
                vwts.append(('wG', val_spill_wG[h - 2]))
            dma('scalar', 'wV', scrh[:, par * T:(par + 1) * T], v_sp[0, h],
                waits=vwts)
            dma('scalar', 'wV', scrl[:, par * T:(par + 1) * T], v_sp[1, h])
            wV_v = p.val('wV')
            mvwts = [('cc', cc_cs)]
            if pe_csb_prev is not None:
                mvwts.append(('pe', pe_csb_prev))
            dma('gpsimd', 'wG', cs_mv, cs_rs_out[h:h + 1, :], waits=mvwts)
            wG_mv = p.val('wG')
            for thf in range(2):
                bwts = [('wG', wG_mv)]
                if dv_val_prev is not None:
                    bwts.append(('dv', dv_val_prev))
                pe(lambda e, thf=thf: e.matmul(
                    psW0[:, thf * 512:(thf + 1) * 512], ones_row[:],
                    cs_mv[:, thf * 512:(thf + 1) * 512], start=True, stop=True),
                   waits=(bwts if thf == 0 else []), inc=(thf == 1))
            pe_csb = p.val('pe')
            pe_csb_prev = pe_csb
            dve(lambda e, par=par: e.tensor_tensor(
                scr32[:], scrh[:, par * T:(par + 1) * T],
                scrl[:, par * T:(par + 1) * T], op=OP.add), waits=[('wV', wV_v)])
            dve(lambda e: e.tensor_tensor(scr32[:, 0:512], scr32[:, 0:512],
                                          psW0[:, 0:512], op=OP.mult),
                waits=[('pe', pe_csb)])
            dve(lambda e: e.tensor_tensor(scr32[:, 512:1024], scr32[:, 512:1024],
                                          psW0[:, 512:1024], op=OP.mult), inc=True)
            dv_val = p.val('dv')
            dv_val_prev = dv_val
            act(lambda e, par=par: e.activation(
                out=scrh[:, par * T:(par + 1) * T], in_=scr32[:], func=AF.Copy),
                waits=[('dv', dv_val)], inc=True)
            ac_vh = p.val('ac')
            dve(lambda e, par=par: e.tensor_tensor(
                scrl[:, par * T:(par + 1) * T], scr32[:],
                scrh[:, par * T:(par + 1) * T], op=OP.subtract),
                waits=[('ac', ac_vh)], inc=True)
            dv_vl = p.val('dv')
            dma('gpsimd', 'wG', val_sp[0, h], scrh[:, par * T:(par + 1) * T],
                waits=[('dv', dv_vl)])
            dma('gpsimd', 'wG', val_sp[1, h], scrl[:, par * T:(par + 1) * T])
            val_spill_wG[h] = p.val('wG')
        wG_val = p.val('wG')

        op_evac = {}
        op_tile_pe = {}
        x1s_dv = {}
        x1_spill = {}
        x1_sq_ac = {}
        x1_stat_pe = {}
        for th in range(2):
            for dh in range(NH):
                dma('scalar', 'wV', vth[:, dh * 512: dh * 512 + 512],
                    val_sp[0, dh][:, th * 512:(th + 1) * 512],
                    waits=([('wG', wG_val)] if (th == 0 and dh == 0) else []))
                dma('scalar', 'wV', vth[:, 16384 + dh * 512: 16384 + dh * 512 + 512],
                    val_sp[1, dh][:, th * 512:(th + 1) * 512])
            wV_vth = p.val('wV')
            for oc in range(NCD):
                i = th * NCD + oc
                b = i % 2
                dv_w = stream_weight(wo[:, oc * 128:(oc + 1) * 128], b,
                                     pe_gate=op_tile_pe.get(i - 2))
                pidx = i % 2
                pst = (ps0 if pidx == 0 else ps1).ap()
                buf = scr32b if i % 2 == 0 else scr32
                wts = [('dv', dv_w), ('wV', wV_vth)]
                if pidx in op_evac:
                    wts.append(op_evac[pidx])
                for dh in range(NCD):
                    mm3(pst, b, dh,
                        vth[:, dh * 512: dh * 512 + 512],
                        vth[:, 16384 + dh * 512: 16384 + dh * 512 + 512],
                        dh == 0, dh == NCD - 1, wts)
                pe_op = p.val('pe')
                op_tile_pe[i] = pe_op
                xs = xstg if i % 2 == 0 else xstg2
                xwts = [('wG', wG_xT), ('wG', wG_cs)] if i == 0 else []
                if i - 2 in x1s_dv:
                    xwts.append(('dv', x1s_dv[i - 2]))
                dma('sync', 'wA', xs[:, 0:512], xT_sp[oc][:, th * 512:(th + 1) * 512],
                    waits=xwts)
                wA_x1 = p.val('wA')
                swts = [('pe', pe_op), ('wA', wA_x1)]
                if i - 2 in x1_spill:
                    swts += [('wG', x1_spill[i - 2]), ('ac', x1_sq_ac[i - 2]),
                             ('pe', x1_stat_pe[i - 2])]
                dve(lambda e, pst=pst, oc=oc, xs=xs, buf=buf: e.scalar_tensor_tensor(
                    buf[:, 0:512], pst, mcol(32, oc), xs[:, 0:512],
                    op0=OP.mult, op1=OP.add), waits=swts, inc=True)
                dv_x1 = p.val('dv')
                x1s_dv[i] = dv_x1
                op_evac[pidx] = ('dv', dv_x1)
                dma('gpsimd', 'wG', x1_sp[oc][:, th * 512:(th + 1) * 512],
                    buf[:, 0:512], waits=[('dv', dv_x1)])
                x1_spill[i] = p.val('wG')
                aqw = [('dv', dv_x1)]
                if i - 1 in x1_stat_pe:
                    aqw.append(('pe', x1_stat_pe[i - 1]))
                act(lambda e, buf=buf: e.activation(out=sqs[:, 0:512], in_=buf[:, 0:512],
                                                    func=AF.Square), waits=aqw, inc=True)
                ac_sq = p.val('ac')
                x1_sq_ac[i] = ac_sq
                pe(lambda e, th=th, oc=oc, buf=buf: e.matmul(
                    (ps2 if th == 0 else ps3)[0:1, :], ones_col[:], buf[:, 0:512],
                    start=(oc == 0), stop=(oc == NCD - 1)), waits=[('dv', dv_x1)])
                pe(lambda e, th=th, oc=oc: e.matmul(
                    psW1[0:1, th * 512:(th + 1) * 512], ones_col[:], sqs[:, 0:512],
                    start=(oc == 0), stop=(oc == NCD - 1)),
                   waits=[('ac', ac_sq)], inc=True)
                x1_stat_pe[i] = p.val('pe')
        wG_x1 = p.val('wG')

        pe_bc2 = ln_finish()

        stop('E')
        h2f_ac = {}
        h2s_dv = {}
        for dc in range(NCD):
            xs = xstg if dc % 2 == 0 else xstg2
            xwts = [('wG', wG_x1)] if dc == 0 else []
            if dc - 2 in h2s_dv:
                xwts.append(('dv', h2s_dv[dc - 2]))
            dma('sync', 'wA', xs[:, 0:T], x1_sp[dc], waits=xwts)
            wA_h2 = p.val('wA')
            buf = scr32b if dc % 2 == 0 else sqs
            wts = [('pe', pe_bc2), ('wA', wA_h2)]
            if dc - 2 >= 0:
                wts.append(('ac', h2f_ac[dc - 2]))
            for hf in range(2):
                dve(lambda e, xs=xs, hf=hf: e.tensor_tensor(
                    scr32[:, hf * 512:(hf + 1) * 512],
                    xs[:, hf * 512: hf * 512 + 512],
                    psW0[:, hf * 512:(hf + 1) * 512], op=OP.mult),
                    waits=(wts if hf == 0 else []))
                dve(lambda e, hf=hf: e.tensor_tensor(
                    scr32[:, hf * 512:(hf + 1) * 512], scr32[:, hf * 512:(hf + 1) * 512],
                    psW1[:, hf * 512:(hf + 1) * 512], op=OP.subtract))
            dve(lambda e, dc=dc, buf=buf: e.tensor_scalar(
                buf[:, 0:T], scr32[:], mcol(64, dc), mcol(48, dc),
                op0=OP.mult, op1=OP.add), inc=True)
            dv_h2 = p.val('dv')
            h2s_dv[dc] = dv_h2
            act(lambda e, dc=dc, buf=buf: e.activation(
                out=h1h_t[:, dc * T:(dc + 1) * T], in_=buf[:, 0:T], func=AF.Copy),
                waits=[('dv', dv_h2)], inc=True)
            h2f_ac[dc] = p.val('ac')
            dve(lambda e, dc=dc, buf=buf: e.tensor_tensor(
                h1l_t[:, dc * T:(dc + 1) * T], buf[:, 0:T],
                h1h_t[:, dc * T:(dc + 1) * T], op=OP.subtract),
                waits=[('ac', h2f_ac[dc])], inc=True)
        dv_h2_done = p.val('dv')

        dma('sync', 'wA', rw_sb, rwd.rearrange("(a b) c -> b a c", b=128),
            waits=[('dv', stream_weight.stg_dv)])
        wA_rw = p.val('wA')
        act(lambda e: e.activation(out=rwh[:], in_=rw_sb, func=AF.Copy),
            waits=[('wA', wA_rw)], inc=True)
        ac_rwh = p.val('ac')
        dve(lambda e: e.tensor_tensor(rwl[:], rw_sb, rwh[:], op=OP.subtract),
            waits=[('ac', ac_rwh)], inc=True)
        dv_rw = p.val('dv')
        stream_weight.stg_dv = dv_rw

        for th in range(2):
            pst = psW0.ap()[0:16, th * 512:(th + 1) * 512]
            wts = [('dv', max(dv_rw, dv_h2_done))]
            for dc in range(NCD):
                rh = rwh[:, dc * 16:(dc + 1) * 16]
                rl = rwl[:, dc * 16:(dc + 1) * 16]
                mh = h1h_t[:, dc * T + th * 512: dc * T + th * 512 + 512]
                ml = h1l_t[:, dc * T + th * 512: dc * T + th * 512 + 512]
                pe(lambda e, rh=rh, m=mh, pst=pst, st=(dc == 0): e.matmul(
                    pst, rh, m, start=st, stop=False),
                   waits=(wts if dc == 0 else []))
                pe(lambda e, rh=rh, m=ml, pst=pst: e.matmul(
                    pst, rh, m, start=False, stop=False))
                pe(lambda e, rl=rl, m=mh, pst=pst, sp=(dc == NCD - 1): e.matmul(
                    pst, rl, m, start=False, stop=sp), inc=(dc == NCD - 1))
        pe_log = p.val('pe')
        act(lambda e: e.activation(out=logT[:], in_=psW0[0:16, :], func=AF.Copy),
            waits=[('pe', pe_log)], inc=True)
        ac_log0 = p.val('ac')
        dve(lambda e: e.tensor_scalar(logT[:], logT[:], rbT[:], None, op0=OP.add),
            waits=[('ac', ac_log0)], inc=True)
        dv_log = p.val('dv')

        sm_evac = {}
        for tc in range(8):
            pb = ps2 if tc % 2 == 0 else ps3
            wts = [('dv', dv_log)]
            if tc - 2 in sm_evac:
                wts.append(('ac', sm_evac[tc - 2]))
            pe(lambda e, tc=tc, pb=pb: e.transpose(
                pb[:, 0:16], logT[0:16, tc * 128:(tc + 1) * 128], ident_s[0:16, 0:16]),
               waits=wts, inc=True)
            pe_ltr = p.val('pe')
            dve(lambda e, tc=tc, pb=pb: e.tensor_reduce(
                smax[:, tc * 3: tc * 3 + 1], pb[:, 0:16], axis=AX, op=OP.max,
                negate=True), waits=[('pe', pe_ltr)], inc=True)
            dv_mx = p.val('dv')
            act(lambda e, tc=tc, pb=pb: e.activation(
                out=smax[:, 128 + tc * 16: 128 + tc * 16 + 16], in_=pb[:, 0:16],
                func=AF.Exp, bias=smax[:, tc * 3: tc * 3 + 1],
                accum_out=smax[:, tc * 3 + 1: tc * 3 + 2]),
                waits=[('dv', dv_mx)], inc=True)
            ac_ex = p.val('ac')
            dve(lambda e, tc=tc: e.reciprocal(smax[:, tc * 3 + 2: tc * 3 + 3],
                                              smax[:, tc * 3 + 1: tc * 3 + 2]),
                waits=[('ac', ac_ex)], inc=True)
            dve(lambda e, tc=tc: e.tensor_scalar(
                smax[:, 128 + tc * 16: 128 + tc * 16 + 16],
                smax[:, 128 + tc * 16: 128 + tc * 16 + 16],
                smax[:, tc * 3 + 2: tc * 3 + 3], None, op0=OP.mult),
                waits=[('dv', p.val('dv'))], inc=True)
            dv_sm = p.val('dv')
            pe(lambda e, tc=tc: e.transpose(
                psW1[0:16, tc * 128:(tc + 1) * 128],
                smax[:, 128 + tc * 16: 128 + tc * 16 + 16], ident_s[:]),
               waits=[('dv', dv_sm)], inc=True)
            pe_str = p.val('pe')
            act(lambda e, tc=tc: e.activation(
                out=ST_sb[0:16, tc * 128:(tc + 1) * 128],
                in_=psW1[0:16, tc * 128:(tc + 1) * 128], func=AF.Copy),
                waits=[('pe', pe_str)], inc=True)
            sm_evac[tc] = p.val('ac')
        ac_ST = p.val('ac')
        dma('gpsimd', 'wG', s_ag_in.ap(), ST_sb[0:16, :], waits=[('ac', ac_ST)])
        wG_S = p.val('wG')
        gp(lambda e: e.collective_compute(
            "AllGather", OP.bypass, replica_groups=[list(range(R))],
            ins=[s_ag_in.ap().opt()], outs=[s_ag_out.ap().opt()]),
           waits=[('wG', wG_S)], inc=('cc', 1))
        cc_S = p.val('cc')
        for r in range(R):
            dma('scalar', 'wV', S_lay[r * 16:(r + 1) * 16, :], s_ag_out[r],
                waits=([('cc', cc_S)] if r == 0 else []))
        wV_slay = p.val('wV')

        dve(lambda e: e.memset(lo16[:], 0.0), waits=[('wV', wV_slay)])
        dve(lambda e: e.memset(hi16[:], 1.0))
        dve(lambda e: e.memset(mid16[:], 0.5))
        dve(lambda e: e.memset(thr128[:], 0.5), inc=True)

        stop('F')
        bis_rc = {}
        bis_mid = {}
        bis_cnt_pe = {}
        bis_bc_pe = {}

        def bisect_dve_cmp(i):
            wts = [('wV', wV_slay)] if i == 0 else [('pe', bis_bc_pe[i - 1])]
            dve(lambda e: e.tensor_scalar(cmp_t[:], S_lay[:], thr128[:], None,
                                          op0=OP.is_ge), waits=wts)
            dve(lambda e: e.tensor_reduce(rc_t[:], cmp_t[:], axis=AX, op=OP.add),
                inc=True)
            bis_rc[i] = p.val('dv')

        def bisect_dve_upd(i):
            dve(lambda e: e.tensor_scalar(pred16[:], ps1[0:16, 0:1], c512[:], None,
                                          op0=OP.is_ge), waits=[('pe', bis_cnt_pe[i])],
                inc=True)
            dve(lambda e: e.select(lo16[:], pred16[:], mid16[:], lo16[:]),
                waits=[('dv', p.val('dv'))], inc=True)
            dve(lambda e: e.select(hi16[:], pred16[:], hi16[:], mid16[:]),
                inc=True)
            dve(lambda e: e.tensor_tensor(sum16[:], lo16[:], hi16[:], op=OP.add),
                waits=[('dv', p.val('dv'))], inc=True)
            dve(lambda e: e.tensor_scalar(mid16[:], sum16[:], half16[:], None,
                                          op0=OP.mult), waits=[('dv', p.val('dv'))],
                inc=True)
            bis_mid[i] = p.val('dv')

        def bisect_dve_thr(i):
            dve(lambda e: e.tensor_copy(thr128[:], ps3[:, 0:1]),
                waits=[('pe', bis_bc_pe[i])], inc=True)
            dve(lambda e: e.engine_nop(), waits=[('dv', p.val('dv'))], inc=True)

        if f_mode >= 1:
            bisect_dve_cmp(0)
        bstate = {'i': 0, 'stage': 0}

        def maybe_bisect_pe():
            i = bstate['i']
            if i >= N_BISECT:
                return
            if bstate['stage'] == 0:
                if bis_rc.get(i) is None:
                    return
                pe(lambda e: e.matmul(ps1[0:16, 0:1], selB_s[:], rc_t[:],
                                      start=True, stop=True),
                   waits=[('dv', bis_rc[i])], inc=True)
                bis_cnt_pe[i] = p.val('pe')
                bisect_dve_upd(i)
                bstate['stage'] = 1
            else:
                pe(lambda e: e.matmul(ps3[:, 0:1], sel2_s[:], mid16[:],
                                      start=True, stop=True),
                   waits=[('dv', bis_mid[i])], inc=True)
                bis_bc_pe[i] = p.val('pe')
                bisect_dve_thr(i)
                bstate['stage'] = 0
                bstate['i'] += 1
                if bstate['i'] < N_BISECT:
                    bisect_dve_cmp(bstate['i'])

        exp_evac = {}
        ex_tile_pe = {}
        for th in range(2):
            for oc in range(NCD):
                i = th * NCD + oc
                b = i % 2
                dv_w = stream_weight(wed[:, oc * 128:(oc + 1) * 128], b,
                                     pe_gate=ex_tile_pe.get(i - 2))
                pidx = i % 2
                pst = (psW0 if pidx == 0 else psW1).ap()[:, 0:512]
                wts = [('dv', max(dv_w, dv_h2_done))]
                if pidx in exp_evac:
                    wts.append(exp_evac[pidx])
                for dc in range(NCD):
                    mm3(pst, b, dc,
                        h1h_t[:, dc * T + th * 512: dc * T + th * 512 + 512],
                        h1l_t[:, dc * T + th * 512: dc * T + th * 512 + 512],
                        dc == 0, dc == NCD - 1, wts)
                    if f_mode >= 1:
                        maybe_bisect_pe()
                pe_ex = p.val('pe')
                ex_tile_pe[i] = pe_ex
                if f_mode == -1:
                    act(lambda e: e.activation(out=scr32b[:, 0:512],
                                               in_=scr32b[:, 0:512], func=AF.Copy),
                        waits=[('pe', pe_ex)], inc=True)
                elif f_mode == -2:
                    act(lambda e, pst=pst: e.activation(
                        out=scr32b[:, 0:512], in_=pst, func=AF.Copy),
                        waits=[('pe', pe_ex)], inc=True)
                elif f_mode == -3:
                    act(lambda e, oc=oc, th=th: e.activation(
                        out=U2[:, oc * T + th * 512: oc * T + th * 512 + 512],
                        in_=scr32b[:, 0:512], func=AF.Copy),
                        waits=[('pe', pe_ex)], inc=True)
                else:
                    act(lambda e, oc=oc, th=th, pst=pst: e.activation(
                        out=U2[:, oc * T + th * 512: oc * T + th * 512 + 512],
                        in_=pst, func=AF.Copy), waits=[('pe', pe_ex)], inc=True)
                exp_evac[pidx] = ('ac', p.val('ac'))
        ac_U2 = p.val('ac')
        if f_mode >= 1:
            while bstate['i'] < N_BISECT:
                maybe_bisect_pe()
        dv_bis = p.val('dv')
        if f_mode < 2:
            raise _StopEmit()

        if "thr" in dbg:
            dma('gpsimd', 'wG', dbg["thr"], lo16[:], waits=[('dv', dv_bis)])

        dve(lambda e: e.scalar_tensor_tensor(
            masked[:], ST_sb[0:16, :], lo16[:], ST_sb[0:16, :],
            op0=OP.is_ge, op1=OP.mult),
            waits=[('dv', dv_bis), ('pe', bis_bc_pe[N_BISECT - 1])], inc=True)
        dv_msk = p.val('dv')
        for thf in range(2):
            wps = ps2 if thf == 0 else ps3
            pe(lambda e, thf=thf, wps=wps: e.matmul(
                wps[0:1, :], ones16[:], masked[0:16, thf * 512:(thf + 1) * 512],
                start=True, stop=True),
               waits=([('dv', dv_msk)] if thf == 0 else []), inc=True)
            pe_w = p.val('pe')
            act(lambda e, thf=thf, wps=wps: e.activation(
                out=w_sb[:, thf * 512:(thf + 1) * 512], in_=wps[0:1, :],
                func=AF.Copy), waits=[('pe', pe_w)], inc=True)
        ac_w = p.val('ac')
        for thf in range(2):
            pe(lambda e, thf=thf: e.matmul(
                (ps0 if thf == 0 else ps1)[:, :], ones_row[:],
                w_sb[0:1, thf * 512:(thf + 1) * 512], start=True, stop=True),
               waits=([('ac', ac_w)] if thf == 0 else []), inc=(thf == 1))
        pe_wbc = p.val('pe')
        wbc_sb = xstg2.ap()[:, 1024:2048]
        dve(lambda e: e.tensor_copy(wbc_sb[:, 0:512], ps0[:]),
            waits=[('pe', pe_wbc)])
        dve(lambda e: e.tensor_copy(wbc_sb[:, 512:1024], ps1[:]), inc=True)
        dv_wbc = p.val('dv')
        if "w" in dbg:
            dma('gpsimd', 'wG', dbg["w"], w_sb[0:1, :], waits=[('ac', ac_w)])
        if "msk" in dbg:
            dma('gpsimd', 'wG', dbg["msk"], masked[0:16, :], waits=[('dv', dv_msk)])
        if f_mode < 3:
            raise _StopEmit()

        ep_tr_ac = {}
        ep_sub_ac = {}
        ep_dv = {}
        for th in range(2):
            for oc in range(NCD):
                i = th * NCD + oc
                xs = xstg if i % 2 == 0 else xstg2
                buf = scr32 if i % 2 == 0 else scr32b
                xwts = []
                if i - 2 in ep_dv:
                    xwts.append(('dv', ep_dv[i - 2]))
                dma('sync', 'wA', xs[:, 0:512], x1_sp[oc][:, th * 512:(th + 1) * 512],
                    waits=xwts)
                wA_x1b = p.val('wA')
                wbc = wbc_sb[:, th * 512:(th + 1) * 512]
                ewts = [('dv', dv_wbc), ('ac', ac_U2), ('wA', wA_x1b)]
                if i - 2 in ep_tr_ac:
                    ewts.append(('ac', ep_tr_ac[i - 2]))
                dve(lambda e, oc=oc, th=th, wbc=wbc, buf=buf: e.scalar_tensor_tensor(
                    buf[:, 0:512], U2[:, oc * T + th * 512: oc * T + th * 512 + 512],
                    beT[:, oc:oc + 1], wbc, op0=OP.add, op1=OP.mult),
                    waits=ewts)
                dve(lambda e, oc=oc, xs=xs, buf=buf: e.scalar_tensor_tensor(
                    buf[:, 0:512], buf[:, 0:512], mcol(80, oc), xs[:, 0:512],
                    op0=OP.mult, op1=OP.add), inc=True)
                dv_out = p.val('dv')
                ep_dv[i] = dv_out
                for sub in range(4):
                    si = i * 4 + sub
                    pb = ps2 if si % 2 == 0 else ps3
                    twts = [('dv', dv_out)] if sub == 0 else []
                    if si - 2 in ep_sub_ac:
                        twts.append(('ac', ep_sub_ac[si - 2]))
                    pe(lambda e, sub=sub, pb=pb, buf=buf: e.transpose(
                        pb[:, 0:128], buf[:, sub * 128:(sub + 1) * 128], ident_s[:]),
                       waits=twts, inc=True)
                    pe_tr = p.val('pe')
                    awts = [('pe', pe_tr)]
                    if th == 1 and oc == 0 and sub == 0:
                        awts.append(('wOUT', th0_out_wOUT))
                    act(lambda e, sub=sub, oc=oc, pb=pb: e.activation(
                        out=OSTG4[:, sub * D + oc * 128: sub * D + (oc + 1) * 128],
                        in_=pb[:, 0:128], func=AF.Copy),
                        waits=awts, inc=True)
                    ep_sub_ac[si] = p.val('ac')
                ep_tr_ac[i] = p.val('ac')
            ac_ep = p.val('ac')
            for sub in range(4):
                dma('gpsimd', 'wOUT',
                    out_d[th * 512 + sub * 128: th * 512 + (sub + 1) * 128, :],
                    OSTG4[:, sub * D:(sub + 1) * D],
                    waits=([('ac', ac_ep)] if sub == 0 else []))
            th0_out_wOUT = p.val('wOUT')
        wOUT_done = p.val('wOUT')

    except _StopEmit:
        pass

    sem_names = sorted(p.counts.keys())
    with contextlib.ExitStack() as stack:
        sems = {s_: stack.enter_context(nc.semaphore(s_)) for s_ in sem_names}
        block = stack.enter_context(nc.Block())

        def runner_for(name):
            def runner(eng):
                for (en, waits, fn, incs) in p.items:
                    if en != name:
                        continue
                    for (s_, v_) in waits:
                        if v_ > 0:
                            eng.wait_ge(sems[s_], v_)
                    inst = fn(eng)
                    for (s_, amt) in incs:
                        inst.then_inc(sems[s_], amt)
                if name == 'gpsimd':
                    for s_ in ('wA', 'wV', 'wG', 'wOUT'):
                        if s_ in sems and p.val(s_) > 0:
                            eng.wait_ge(sems[s_], p.val(s_))
            return runner

        block.tensor(runner_for('tensor'))
        block.vector(runner_for('vector'))
        block.scalar(runner_for('scalar'))
        block.sync(runner_for('sync'))
        block.gpsimd(runner_for('gpsimd'))

    return nc, dbg


def _host_constants():
    ident = np.eye(128, dtype=np.float32)
    selB = np.zeros((128, 16), dtype=np.float32)
    for pp in range(128):
        selB[pp, pp % 16] = 1.0
    sel2 = np.zeros((16, 128), dtype=np.float32)
    for pp in range(128):
        sel2[pp % 16, pp] = 1.0
    return ident, selB, sel2


def make_in_maps(inputs):
    x = np.ascontiguousarray(np.asarray(inputs["L0_attn_x"], dtype=np.float32)[0])
    wqkv = np.ascontiguousarray(inputs["L0_attn_w_qkv"], dtype=np.float32)
    wo_ = np.ascontiguousarray(inputs["L0_attn_w_o"], dtype=np.float32)
    rw = np.ascontiguousarray(inputs["L0_ffn_router_w"], dtype=np.float32)
    rb = np.ascontiguousarray(
        np.asarray(inputs["L0_ffn_router_b"], dtype=np.float32).reshape(NE, 1))
    we = np.ascontiguousarray(inputs["L0_ffn_expert_w"], dtype=np.float32)
    be = np.ascontiguousarray(
        np.asarray(inputs["L0_ffn_expert_b"], dtype=np.float32).reshape(-1))
    cv = np.ascontiguousarray(np.asarray(inputs["L0_c"], dtype=np.float32).reshape(-1))
    wmod = np.ascontiguousarray(inputs["L0_wmod"], dtype=np.float32)
    ident, selB, sel2 = _host_constants()
    maps = []
    for r in range(R):
        maps.append({
            "x_slice": x[r * T:(r + 1) * T],
            "w_qkv": wqkv, "w_o": wo_,
            "router_w": rw, "router_b": rb,
            "expert_w": we, "expert_b": be,
            "c_vec": cv,
            "wmod_slice": np.ascontiguousarray(wmod[:, r * 1536:(r + 1) * 1536]),
            "ident": ident, "selB": selB, "sel2": sel2,
        })
    return maps


def kernel(L0_attn_x, L0_attn_w_qkv, L0_attn_w_o, L0_ffn_router_w, L0_ffn_router_b,
           L0_ffn_expert_w, L0_ffn_expert_b, L0_c, L0_wmod):
    inputs = dict(L0_attn_x=L0_attn_x, L0_attn_w_qkv=L0_attn_w_qkv,
                  L0_attn_w_o=L0_attn_w_o, L0_ffn_router_w=L0_ffn_router_w,
                  L0_ffn_router_b=L0_ffn_router_b, L0_ffn_expert_w=L0_ffn_expert_w,
                  L0_ffn_expert_b=L0_ffn_expert_b, L0_c=L0_c, L0_wmod=L0_wmod)
    nc, _ = build_program()
    res = run_bass_kernel_spmd(nc, make_in_maps(inputs), core_ids=list(range(R)))
    out = np.concatenate([res.results[r]["out_slice"] for r in range(R)], axis=0)
    return np.ascontiguousarray(out[None]).astype(np.float32, copy=False)


# version 03 (final)
# speedup vs baseline: 1.1396x; 1.0432x over previous
import contextlib
import numpy as np
import concourse.bass as bass
import concourse.mybir as mybir
from concourse.bass_utils import run_bass_kernel_spmd

F32 = mybir.dt.float32
F32R = mybir.dt.float32r
BF16 = mybir.dt.bfloat16
AF = mybir.ActivationFunctionType
OP = mybir.AluOpType
AX = mybir.AxisListType.X

R = 8
T = 1024
D = 2048
NCD = 16
NH = 16
S = 8192
NE = 16
KSEL = 512
INV_SQRT_DH = 0.08838834764831845
EXP_SHIFT = -16.0
N_BISECT = 45


class Sched:
    def __init__(self):
        self.items = []
        self.counts = {}

    def emit(self, eng, fn, waits=(), inc=None):
        incs = []
        if inc is not None:
            sem, amt = inc
            self.counts[sem] = self.counts.get(sem, 0) + amt
            incs.append((sem, amt))
        self.items.append((eng, [w for w in waits if w is not None], fn, incs))

    def val(self, sem):
        return self.counts.get(sem, 0)


class _StopEmit(Exception):
    pass


def build_program(debug_taps=(), stop_after=None, debug_dram=False, h1_mode=3, h1_upto=16, qkv_upto=48, qkv_mode=3, heads_upto=16, f_mode=3):
    nc = bass.Bass("TRN2", target_bir_lowering=False, detect_race_conditions=False)
    debug_taps = list(debug_taps)
    dbg_kind = "ExternalOutput" if debug_dram else "Internal"

    x_in = nc.dram_tensor("x_slice", [T, D], F32, kind="ExternalInput").ap()
    wqkv = nc.dram_tensor("w_qkv", [D, 3 * D], F32, kind="ExternalInput").ap()
    wo = nc.dram_tensor("w_o", [D, D], F32, kind="ExternalInput").ap()
    rwd = nc.dram_tensor("router_w", [D, NE], F32, kind="ExternalInput").ap()
    rbd = nc.dram_tensor("router_b", [NE, 1], F32, kind="ExternalInput").ap()
    wed = nc.dram_tensor("expert_w", [D, D], F32, kind="ExternalInput").ap()
    bed = nc.dram_tensor("expert_b", [D], F32, kind="ExternalInput").ap()
    cvd = nc.dram_tensor("c_vec", [D], F32, kind="ExternalInput").ap()
    wmodd = nc.dram_tensor("wmod_slice", [D, 1536], F32, kind="ExternalInput").ap()
    identd = nc.dram_tensor("ident", [128, 128], F32, kind="ExternalInput").ap()
    selBd = nc.dram_tensor("selB", [128, 16], F32, kind="ExternalInput").ap()
    sel2d = nc.dram_tensor("sel2", [16, 128], F32, kind="ExternalInput").ap()
    out_d = nc.dram_tensor("out_slice", [T, D], F32, kind="ExternalOutput").ap()

    kag_in = nc.dram_tensor("kag_in", [2, NH, 128, T], BF16)
    kag_dbg = (nc.dram_tensor("kag_dbg", [2, NH, 128, T], BF16,
               kind="ExternalOutput") if debug_dram else None)
    cs_dbg = (nc.dram_tensor("cs_dbg", [NH, T], F32,
              kind="ExternalOutput") if debug_dram else None)
    cs0_dbg = (nc.dram_tensor("cs0_dbg", [4, 2048], F32,
               kind="ExternalOutput") if debug_dram else None)
    z0_dbg = (nc.dram_tensor("z0_dbg", [128, 16], F32,
              kind="ExternalOutput") if debug_dram else None)
    e0_dbg = (nc.dram_tensor("e0_dbg", [128, S], BF16,
              kind="ExternalOutput") if debug_dram else None)
    z2_dbg = (nc.dram_tensor("z2_dbg", [128, 24], F32,
              kind="ExternalOutput") if debug_dram else None)
    kag_out = nc.dram_tensor("kag_out", [R, 2, NH, 128, T], BF16, addr_space="Shared")
    q_sp = nc.dram_tensor("q_sp", [2, NH, 128, T], BF16, kind=dbg_kind)
    v_sp = nc.dram_tensor("v_sp", [2, NH, 128, T], BF16, kind=dbg_kind)
    val_sp = nc.dram_tensor("val_sp", [2, NH, 128, T], BF16, kind=dbg_kind)
    xT_sp = nc.dram_tensor("xT_sp", [NCD, 128, T], F32, kind=dbg_kind)
    x1_sp = nc.dram_tensor("x1_sp", [NCD, 128, T], F32, kind=dbg_kind)
    cs_rs_in = nc.dram_tensor("cs_rs_in", [R, NH, T], F32)
    cs_rs_out = nc.dram_tensor("cs_rs_out", [NH, T], F32)
    s_ag_in = nc.dram_tensor("s_ag_in", [NE, T], F32)
    s_ag_out = nc.dram_tensor("s_ag_out", [R, NE, T], F32, addr_space="Shared")
    mod_ag_in = nc.dram_tensor("mod_ag_in", [1536], F32)
    mod_ag_out = nc.dram_tensor("mod_ag_out", [R, 1536], F32, addr_space="Shared")

    dbg = {}
    for name, shape in debug_taps:
        dbg[name] = nc.dram_tensor("dbg_" + name, list(shape), F32,
                                   kind="ExternalOutput").ap()

    p = Sched()

    def sb(name, shape, dtype):
        return nc.alloc_sbuf_tensor(name, shape, dtype)

    ident_s = sb("ident_s", [128, 128], F32)
    selB_s = sb("selB_s", [128, 16], F32)
    sel2_s = sb("sel2_s", [16, 128], F32)
    ones_row = sb("ones_row", [1, 128], F32)
    ones_col = sb("ones_col", [128, 1], F32)
    ones16 = sb("ones16", [16, 1], F32)
    cstat = sb("cstat", [128, 1], F32)
    biasC = sb("biasC", [128, 1], F32)
    epsb = sb("epsb", [1, 1], F32)
    c512 = sb("c512", [16, 1], F32)
    half16 = sb("half16", [16, 1], F32)

    XK = sb("XK", [128, 16 * T], F32)
    h1h_t = sb("h1h", [128, 16 * T], BF16)
    h1l_t = sb("h1l", [128, 16 * T], BF16)
    wstg = sb("wstg", [128, D], F32)
    wh_t = sb("wh", [128, 2 * D], BF16)
    wl_t = sb("wl", [128, 2 * D], BF16)
    scr32 = sb("scr32", [128, T], F32)
    scr32b = sb("scr32b", [128, T], F32)
    scrh = sb("scrh", [128, 2 * T], BF16)
    scrl = sb("scrl", [128, 2 * T], BF16)
    qh_t = sb("qh_t", [128, 2 * T], BF16)
    ql_t = sb("ql_t", [128, 2 * T], BF16)
    zpart = sb("zpart", [128, 8], F32)
    zrowv = sb("zrowv", [128, 1], F32)
    zrecip = sb("zrecip", [128, 1], F32)
    zrh_b = sb("zrh_b", [128, 1], BF16)
    zrl_b = sb("zrl_b", [128, 1], BF16)
    zrselH = [sb("zrselH%d" % m, [128, 4], BF16) for m in range(4)]
    zrselL = [sb("zrselL%d" % m, [128, 4], BF16) for m in range(4)]
    modT = sb("modT", [128, 96], F32)
    modp = sb("modp", [128, 96], F32)
    cv_sb = sb("cv_sb", [16, 128], F32)
    cT_silu = sb("cT_silu", [128, 16], F32)
    mod_all = sb("mod_all", [96, 128], F32)
    stat_all = sb("stat_all", [1, 2 * T], F32)
    xstg = sb("xstg", [128, D], F32)
    xstg2 = sb("xstg2", [128, D], F32)
    rwh = sb("rwh", [128, 256], BF16)
    rwl = sb("rwl", [128, 256], BF16)
    rbT = sb("rbT", [16, 1], F32)
    beT = sb("beT", [128, 16], F32)
    smax = sb("smax", [128, 384], F32)
    thr128 = sb("thr128", [128, 1], F32)
    lo16 = sb("lo16", [16, 1], F32)
    hi16 = sb("hi16", [16, 1], F32)
    mid16 = sb("mid16", [16, 1], F32)
    pred16 = sb("pred16", [16, 1], mybir.dt.int32)
    sum16 = sb("sum16", [16, 1], F32)
    rc_t = sb("rc_t", [128, 1], F32)

    psW0 = nc.alloc_psum_tensor("psW0", [128, 1024], F32)
    psW1 = nc.alloc_psum_tensor("psW1", [128, 1024], F32)
    ps0 = nc.alloc_psum_tensor("ps0", [128, 512], F32)
    ps1 = nc.alloc_psum_tensor("ps1", [128, 512], F32)
    ps2 = nc.alloc_psum_tensor("ps2", [128, 512], F32)
    ps3 = nc.alloc_psum_tensor("ps3", [128, 512], F32)
    psCS = [ps0, ps1, ps2, ps3]

    xT = XK.ap()
    K_bufs = XK.ap().bitcast(BF16)
    vth = XK.ap().bitcast(BF16)
    U2 = XK.ap()
    OSTG4 = h1h_t.ap().bitcast(F32)
    E_t = h1h_t.ap()[:, 0:S]
    wmod_sb = wh_t.ap().bitcast(F32)
    mod_row = stat_all[0:1, 0:1536]
    stat_mu = stat_all[0:1, 0:T]
    stat_rstd = stat_all[0:1, T:2 * T]
    cmp_t = xstg2.ap()[:, 1024:2048]
    masked = xstg2[0:16, 1024:2048]
    S_lay = scrl.ap().bitcast(F32)
    ST_sb = scr32b[0:16, :]
    logT = S_lay[0:16, :]
    sqs = scrh.ap().bitcast(F32)
    rw_sb = wstg[:, 0:256]
    cs_stage = xstg[0:4, 0:2048]
    cs_mv = ql_t.ap().bitcast(F32)[0:1, :]
    stat_t2 = cs_mv
    w_sb = cs_mv

    def kslice(par, half):
        off = (par * 2 + half) * S
        return K_bufs[:, off:off + S]

    def dma(qeng, sem, out, in_, waits=()):
        p.emit(qeng, lambda e, o=out, i=in_: e.dma_start(out=o, in_=i),
               waits=waits, inc=(sem, 16))

    def act(fn, waits=(), inc=False):
        p.emit('scalar', fn, waits=waits, inc=(('ac', 1) if inc else None))

    def dve(fn, waits=(), inc=False):
        p.emit('vector', fn, waits=waits, inc=(('dv', 1) if inc else None))

    def pe(fn, waits=(), inc=False):
        p.emit('tensor', fn, waits=waits, inc=(('pe', 1) if inc else None))

    def gp(fn, waits=(), inc=None):
        p.emit('gpsimd', fn, waits=waits, inc=inc)

    def mcol(base, c):
        return modp[:, base + c: base + c + 1]

    PH_ORDER = ['0', 'A1', 'A2', 'A3', 'A', 'B', 'C', 'D', 'E', 'F']

    def stop(ph):
        if (stop_after is not None
                and PH_ORDER.index(ph) > PH_ORDER.index(stop_after)):
            raise _StopEmit()

    try:
        dve(lambda e: e.memset(ones_row[:], 1.0))
        dve(lambda e: e.memset(ones_col[:], 1.0))
        dve(lambda e: e.memset(ones16[:], 1.0))
        dve(lambda e: e.memset(biasC[:], EXP_SHIFT))
        dve(lambda e: e.memset(epsb[:], 1e-5))
        dve(lambda e: e.memset(c512[:], float(KSEL)))
        dve(lambda e: e.memset(half16[:], 0.5))
        for m in range(4):
            dve(lambda e, m=m: e.memset(zrselH[m][:], 0.0))
            dve(lambda e, m=m: e.memset(zrselL[m][:], 0.0), inc=(m == 3))
        dv_consts = p.val('dv')

        dma('sync', 'wA', ident_s[:], identd)
        dma('sync', 'wA', selB_s[:], selBd)
        dma('sync', 'wA', sel2_s[:], sel2d)
        dma('sync', 'wA', cv_sb[:], cvd.rearrange("(a b) -> a b", b=128))
        dma('sync', 'wA', rbT[:], rbd)
        wA_consts = p.val('wA')

        pe(lambda e: e.transpose(psW0[:, 0:16], cv_sb[:], ident_s[0:16, 0:16]),
           waits=[('wA', wA_consts)], inc=True)
        pe_cT = p.val('pe')
        act(lambda e: e.activation(out=cT_silu[:], in_=psW0[:, 0:16], func=AF.Silu),
            waits=[('pe', pe_cT)], inc=True)
        ac_silu = p.val('ac')
        dma('sync', 'wA', cv_sb[:], bed.rearrange("(a b) -> a b", b=128),
            waits=[('pe', pe_cT)])
        wA_be = p.val('wA')
        pe(lambda e: e.transpose(ps3[:, 0:16], cv_sb[:], ident_s[0:16, 0:16]),
           waits=[('wA', wA_be)], inc=True)
        pe_beT = p.val('pe')
        act(lambda e: e.activation(out=beT[:], in_=ps3[:, 0:16], func=AF.Copy),
            waits=[('pe', pe_beT)], inc=True)
        ac_beT = p.val('ac')

        for dc in range(NCD):
            dma('sync', 'wA', wmod_sb[:, 0:1536],
                wmodd[dc * 128:(dc + 1) * 128, :],
                waits=([('pe', p.val('pe'))] if dc >= 1 else []))
            wA_mod = p.val('wA')
            dve(lambda e, dc=dc: e.tensor_copy(cstat[:], cT_silu[:, dc:dc + 1]),
                waits=[('ac', ac_silu)] + ([('pe', p.val('pe'))] if dc > 0 else []),
                inc=True)
            dv_c = p.val('dv')
            for j in range(3):
                mpst = (psW1[0:1, 0:512], psW1[0:1, 512:1024], ps2[0:1, 0:512])[j]
                pe(lambda e, j=j, dc=dc, mpst=mpst: e.matmul(
                    mpst, cstat[:],
                    wmod_sb[:, j * 512:(j + 1) * 512],
                    start=(dc == 0), stop=(dc == NCD - 1)),
                   waits=[('wA', wA_mod), ('dv', dv_c)], inc=(j == 2))
        pe_mod = p.val('pe')
        for j in range(3):
            mpst = (psW1[0:1, 0:512], psW1[0:1, 512:1024], ps2[0:1, 0:512])[j]
            act(lambda e, j=j, mpst=mpst: e.activation(
                out=mod_row[:, j * 512:(j + 1) * 512], in_=mpst,
                func=AF.Copy), waits=([('pe', pe_mod)] if j == 0 else []), inc=(j == 2))
        ac_modrow = p.val('ac')
        dma('gpsimd', 'wG', mod_ag_in.ap(), mod_row, waits=[('ac', ac_modrow)])
        wG_mod = p.val('wG')
        gp(lambda e: e.collective_compute(
            "AllGather", OP.bypass, replica_groups=[list(range(R))],
            ins=[mod_ag_in.ap().opt()], outs=[mod_ag_out.ap().opt()]),
           waits=[('wG', wG_mod)], inc=('cc', 1))
        cc_mod = p.val('cc')
        dma('sync', 'wA', mod_all[:],
            mod_ag_out.ap().rearrange("r (a b) -> (r a) b", b=128),
            waits=[('cc', cc_mod)])
        wA_modall = p.val('wA')
        pe(lambda e: e.transpose(psW0[:, 0:96], mod_all[:], ident_s[0:96, 0:96]),
           waits=[('wA', wA_modall), ('ac', ac_silu)], inc=True)
        pe_modT = p.val('pe')
        act(lambda e: e.activation(out=modT[:], in_=psW0[:, 0:96], func=AF.Copy),
            waits=[('pe', pe_modT)], inc=True)
        ac_modT = p.val('ac')
        dve(lambda e: e.tensor_copy(modp[:, 0:16], modT[:, 0:16]),
            waits=[('ac', ac_modT)])
        dve(lambda e: e.tensor_scalar(modp[:, 16:32], modT[:, 16:32], 1.0, None,
                                      op0=OP.add))
        dve(lambda e: e.tensor_copy(modp[:, 48:64], modT[:, 48:64]))
        dve(lambda e: e.tensor_scalar(modp[:, 64:80], modT[:, 64:80], 1.0, None,
                                      op0=OP.add))
        for base in (32, 80):
            dve(lambda e, s=base: e.tensor_scalar(
                scr32[:, 32:48], modT[:, s:s + 16], -1.0, None, op0=OP.mult),
                inc=True)
            dve(lambda e, s=base: e.tensor_tensor(
                scr32[:, 0:16], modT[:, s:s + 16], scr32[:, 32:48], op=OP.max),
                waits=[('dv', p.val('dv'))], inc=True)
            dve(lambda e: e.tensor_scalar(
                scr32[:, 0:16], scr32[:, 0:16], 0.25, None, op0=OP.max),
                waits=[('dv', p.val('dv'))], inc=True)
            dve(lambda e, s=base: e.tensor_scalar(
                scr32[:, 16:32], modT[:, s:s + 16], 0.0, None, op0=OP.is_ge),
                inc=True)
            dve(lambda e: e.tensor_scalar(
                scr32[:, 16:32], scr32[:, 16:32], 2.0, -1.0, op0=OP.mult, op1=OP.add),
                waits=[('dv', p.val('dv'))], inc=True)
            dve(lambda e, d=base: e.tensor_tensor(
                modp[:, d:d + 16], scr32[:, 0:16], scr32[:, 16:32], op=OP.mult),
                waits=[('dv', p.val('dv'))], inc=True)
        dv_modp = p.val('dv')

        stop('A')
        tr_evac_ac = {}
        for tc in range(8):
            stg = xstg if tc % 2 == 0 else xstg2
            dma('sync', 'wA', stg[:], x_in[tc * 128:(tc + 1) * 128, :],
                waits=([('pe', p.val('pe'))] if tc >= 2 else []))
            wA_x = p.val('wA')
            for dc in range(NCD):
                i = tc * NCD + dc
                pb = psW0 if i % 2 == 0 else psW1
                wts = [('wA', wA_x)]
                if i == 0:
                    wts.append(('ac', ac_modT))
                elif i == 1:
                    wts.append(('ac', ac_modrow))
                elif (i - 2) in tr_evac_ac:
                    wts.append(('ac', tr_evac_ac[i - 2]))
                pe(lambda e, stg=stg, dc=dc, pb=pb: e.transpose(
                    pb[:, 0:128], stg[:, dc * 128:(dc + 1) * 128], ident_s[:]),
                   waits=wts, inc=True)
                pe_tr = p.val('pe')
                act(lambda e, dc=dc, tc=tc, pb=pb: e.activation(
                    out=xT[:, dc * T + tc * 128: dc * T + (tc + 1) * 128],
                    in_=pb[:, 0:128], func=AF.Copy),
                    waits=[('pe', pe_tr)], inc=True)
                tr_evac_ac[i] = p.val('ac')
        ac_xT = p.val('ac')

        for dc in range(NCD):
            dma('gpsimd', 'wG', xT_sp[dc], xT[:, dc * T:(dc + 1) * T],
                waits=([('ac', ac_xT)] if dc == 0 else []))
        wG_xT = p.val('wG')
        stop('A1')

        pa_stat_pe = 0
        for dc in range(NCD):
            xc = xT[:, dc * T:(dc + 1) * T]
            act(lambda e, xc=xc: e.activation(out=sqs[:], in_=xc, func=AF.Square),
                waits=([('pe', pa_stat_pe)] if pa_stat_pe else []), inc=True)
            ac_sq = p.val('ac')
            for thf in range(2):
                pe(lambda e, xc=xc, thf=thf, dc=dc: e.matmul(
                    (ps2 if thf == 0 else ps3)[0:1, :], ones_col[:],
                    xc[:, thf * 512:(thf + 1) * 512],
                    start=(dc == 0), stop=(dc == NCD - 1)),
                   waits=([('ac', ac_xT)] if dc == 0 and thf == 0 else []))
                pe(lambda e, thf=thf, dc=dc: e.matmul(
                    psW1[0:1, thf * 512:(thf + 1) * 512], ones_col[:],
                    sqs[:, thf * 512:(thf + 1) * 512],
                    start=(dc == 0), stop=(dc == NCD - 1)),
                   waits=[('ac', ac_sq)], inc=True)
            pa_stat_pe = p.val('pe')

        def ln_finish():
            pe_st = p.val('pe')
            act(lambda e: e.activation(out=stat_mu[:, 0:512], in_=ps2[0:1, :],
                                       func=AF.Copy, scale=1.0 / D),
                waits=[('pe', pe_st)])
            act(lambda e: e.activation(out=stat_mu[:, 512:1024], in_=ps3[0:1, :],
                                       func=AF.Copy, scale=1.0 / D))
            act(lambda e: e.activation(out=stat_t2[:, 0:512], in_=psW1[0:1, 0:512],
                                       func=AF.Copy, scale=1.0 / D))
            act(lambda e: e.activation(out=stat_t2[:, 512:1024], in_=psW1[0:1, 512:1024],
                                       func=AF.Copy, scale=1.0 / D), inc=True)
            ac_st = p.val('ac')
            dve(lambda e: e.tensor_tensor(stat_rstd[:], stat_mu[:], stat_mu[:],
                                          op=OP.mult), waits=[('ac', ac_st)])
            dve(lambda e: e.tensor_tensor(stat_t2[:], stat_t2[:], stat_rstd[:],
                                          op=OP.subtract), inc=True)
            dv_v = p.val('dv')
            act(lambda e: e.activation(out=stat_t2[:], in_=stat_t2[:], func=AF.Sqrt,
                                       bias=epsb[:]), waits=[('dv', dv_v)], inc=True)
            ac_sd = p.val('ac')
            dve(lambda e: e.reciprocal(stat_rstd[:], stat_t2[:]),
                waits=[('ac', ac_sd)], inc=True)
            dve(lambda e: e.tensor_tensor(stat_mu[:], stat_mu[:], stat_rstd[:],
                                          op=OP.mult), waits=[('dv', p.val('dv'))],
                inc=True)
            dv_r = p.val('dv')
            for tc in range(8):
                pe(lambda e, tc=tc: e.matmul(
                    psW0[:, tc * 128:(tc + 1) * 128], ones_row[:],
                    stat_rstd[0:1, tc * 128:(tc + 1) * 128], start=True, stop=True),
                   waits=([('dv', dv_r)] if tc == 0 else []))
                pe(lambda e, tc=tc: e.matmul(
                    psW1[:, tc * 128:(tc + 1) * 128], ones_row[:],
                    stat_mu[0:1, tc * 128:(tc + 1) * 128], start=True, stop=True),
                   inc=(tc == 7))
            return p.val('pe')

        stop('A2')
        pe_bc1 = ln_finish()
        stop('A3')

        h1f_ac = {}
        for dc in range(min(NCD, h1_upto)):
            buf = scr32b if dc % 2 == 0 else sqs
            wts = [('pe', pe_bc1)]
            if dc - 2 >= 0:
                wts.append(('ac', h1f_ac[dc - 2]))
            for hf in range(2):
                dve(lambda e, dc=dc, hf=hf: e.tensor_tensor(
                    scr32[:, hf * 512:(hf + 1) * 512],
                    xT[:, dc * T + hf * 512: dc * T + hf * 512 + 512],
                    psW0[:, hf * 512:(hf + 1) * 512], op=OP.mult),
                    waits=(wts if hf == 0 else []))
                dve(lambda e, hf=hf: e.tensor_tensor(
                    scr32[:, hf * 512:(hf + 1) * 512], scr32[:, hf * 512:(hf + 1) * 512],
                    psW1[:, hf * 512:(hf + 1) * 512], op=OP.subtract))
            if h1_mode >= 1:
                dve(lambda e, dc=dc, buf=buf: e.tensor_scalar(
                    buf[:, 0:T], scr32[:], mcol(16, dc), mcol(0, dc),
                    op0=OP.mult, op1=OP.add), waits=[('dv', dv_modp)], inc=True)
            else:
                dve(lambda e, dc=dc, buf=buf: e.tensor_copy(buf[:, 0:T], scr32[:]),
                    waits=[('dv', dv_modp)], inc=True)
            dv_h1 = p.val('dv')
            if h1_mode >= 2:
                act(lambda e, dc=dc, buf=buf: e.activation(
                    out=h1h_t[:, dc * T:(dc + 1) * T], in_=buf[:, 0:T], func=AF.Copy),
                    waits=[('dv', dv_h1)], inc=True)
            else:
                act(lambda e: e.copy(zrowv[:], zrecip[:]), waits=[('dv', dv_h1)],
                    inc=True)
            h1f_ac[dc] = p.val('ac')
            if h1_mode >= 3:
                dve(lambda e, dc=dc, buf=buf: e.tensor_tensor(
                    h1l_t[:, dc * T:(dc + 1) * T], buf[:, 0:T],
                    h1h_t[:, dc * T:(dc + 1) * T], op=OP.subtract),
                    waits=[('ac', h1f_ac[dc])], inc=True)
            else:
                dve(lambda e: e.tensor_copy(zrowv[:], zrecip[:]),
                    waits=[('ac', h1f_ac[dc])], inc=True)
        dv_h1_done = p.val('dv')

        stop('B')
        def stream_weight(src_col_ap, b, act_extra=(), pe_gate=None):
            dma('sync', 'wA', wstg[:],
                src_col_ap.rearrange("(a b) c -> b a c", b=128),
                waits=[('dv', stream_weight.stg_dv)])
            wA_w = p.val('wA')
            gw = [('pe', pe_gate)] if pe_gate else []
            dve(lambda e, b=b: e.tensor_copy(wh_t[:, b * D:(b + 1) * D], wstg[:]),
                waits=[('wA', wA_w)] + list(act_extra) + gw, inc=True)
            dve(lambda e, b=b: e.tensor_tensor(
                wl_t[:, b * D:(b + 1) * D], wstg[:], wh_t[:, b * D:(b + 1) * D],
                op=OP.subtract), inc=True)
            stream_weight.stg_dv = p.val('dv')
            return p.val('dv')

        stream_weight.stg_dv = 0

        def mm3(pst, b, dc, mov_h, mov_l, first, last, wts):
            whap = wh_t[:, b * D + dc * 128: b * D + (dc + 1) * 128]
            wlap = wl_t[:, b * D + dc * 128: b * D + (dc + 1) * 128]
            pe(lambda e, w=whap, m=mov_h, pst=pst, st=first: e.matmul(
                pst, w, m, start=st, stop=False), waits=(wts if first else []))
            pe(lambda e, w=whap, m=mov_l, pst=pst: e.matmul(
                pst, w, m, start=False, stop=False))
            pe(lambda e, w=wlap, m=mov_h, pst=pst, sp=last: e.matmul(
                pst, w, m, start=False, stop=sp), inc=last)

        oc_order = list(range(16, 32)) + list(range(32, 48)) + list(range(0, 16))
        qkv_s32_dv = {}
        qkv_ps = [psW0.ap()[:, 0:512], psW0.ap()[:, 512:1024],
                  psW1.ap()[:, 0:512], psW1.ap()[:, 512:1024]]
        last_evac = {}
        spill_wG = {}
        kag_cc = None

        qkv_tile_pe = {}
        for oi, oc in enumerate(oc_order[:qkv_upto]):
            b = oi % 2
            dv_w = stream_weight(wqkv[:, oc * 128:(oc + 1) * 128], b,
                                 act_extra=([('pe', pe_mod)] if oi < 2 else []),
                                 pe_gate=qkv_tile_pe.get(oi - 2))
            for th in range(2):
                pidx = (oi * 2 + th) % 4
                pst = qkv_ps[pidx]
                wts = [('dv', dv_w)]
                if pidx in last_evac:
                    wts.append(last_evac[pidx])
                if oi == 0 and th == 0:
                    wts.append(('dv', dv_h1_done))
                for dc in range(NCD):
                    mm3(pst, b, dc,
                        h1h_t[:, dc * T + th * 512: dc * T + th * 512 + 512],
                        h1l_t[:, dc * T + th * 512: dc * T + th * 512 + 512],
                        dc == 0, dc == NCD - 1, wts)
                pe_q = p.val('pe')
                qkv_tile_pe[oi] = pe_q
                sh = scrh[:, b * T + th * 512: b * T + th * 512 + 512]
                sl = scrl[:, b * T + th * 512: b * T + th * 512 + 512]
                s32 = scr32.ap()[:, th * 512: th * 512 + 512]
                awts = [('pe', pe_q)]
                if (oi * 2 + th - 2) in qkv_s32_dv:
                    awts.append(('dv', qkv_s32_dv[oi * 2 + th - 2]))
                act(lambda e, s32=s32, pst=pst: e.activation(
                    out=s32, in_=pst, func=AF.Copy),
                    waits=awts, inc=True)
                last_evac[pidx] = ('ac', p.val('ac'))
                ac_e = p.val('ac')
                dwts = [('ac', ac_e)]
                if oi - 2 >= 0:
                    dwts.append(('wG', spill_wG[oi - 2]))
                dve(lambda e, sh=sh, s32=s32: e.tensor_copy(sh, s32),
                    waits=dwts, inc=True)
                dve(lambda e, sl=sl, s32=s32, sh=sh: e.tensor_tensor(
                    sl, s32, sh, op=OP.subtract), inc=True)
                qkv_s32_dv[oi * 2 + th] = p.val('dv')
            dv_oc = p.val('dv')
            if oc >= 32:
                dsth, dstl = v_sp[0, oc - 32], v_sp[1, oc - 32]
            elif oc >= 16:
                dsth, dstl = kag_in[0, oc - 16], kag_in[1, oc - 16]
            else:
                dsth, dstl = q_sp[0, oc], q_sp[1, oc]
            dma('gpsimd', 'wG', dsth, scrh[:, b * T:(b + 1) * T],
                waits=[('dv', dv_oc)])
            dma('gpsimd', 'wG', dstl, scrl[:, b * T:(b + 1) * T])
            spill_wG[oi] = p.val('wG')
            if oi == 15:
                gp(lambda e: e.collective_compute(
                    "AllGather", OP.bypass, replica_groups=[list(range(R))],
                    ins=[kag_in.ap().opt()], outs=[kag_out.ap().opt()]),
                   waits=[('wG', spill_wG[15])], inc=('cc', 1))
                kag_cc = p.val('cc')

        stop('C')
        exp_ac = {}
        pending_colsum = [None]
        head_end_pe = {}
        csel_done = {}
        cs_dma_wG = {}
        last_colsum_pe = 0
        qkv_evac_gates = list(last_evac.values())

        for h in range(min(NH, heads_upto)):
            par = h % 2
            kwts = [('cc', kag_cc)]
            if h >= 2:
                kwts.append(('pe', head_end_pe[h - 2]))
            for r in range(R):
                dma('sync', 'wV', kslice(par, 0)[:, r * T:(r + 1) * T],
                    kag_out[r, 0, h], waits=(kwts if r == 0 else []))
                dma('sync', 'wV', kslice(par, 1)[:, r * T:(r + 1) * T],
                    kag_out[r, 1, h])
            dma('sync', 'wV', qh_t[:, par * T:(par + 1) * T], q_sp[0, h],
                waits=[('wG', spill_wG[32 + h])])
            dma('sync', 'wV', ql_t[:, par * T:(par + 1) * T], q_sp[1, h])
            kv = p.val('wV')

            for pnl in range(8):
                qh_ap = qh_t[:, par * T + pnl * 128: par * T + pnl * 128 + 128]
                ql_ap = ql_t[:, par * T + pnl * 128: par * T + pnl * 128 + 128]
                for g in range(8):
                    pw = psW0 if g % 2 == 0 else psW1
                    wts = [('wV', kv)]
                    if h == 0 and pnl == 0 and g < 2:
                        wts.extend(qkv_evac_gates)
                    if g >= 2:
                        key = (h, pnl, g - 2)
                    elif pnl > 0:
                        key = (h, pnl - 1, g + 6)
                    else:
                        key = (h - 1, 7, g + 6)
                    if key in exp_ac:
                        wts.append(('ac', exp_ac[key]))
                    kh0 = kslice(par, 0)[:, g * 1024: g * 1024 + 512]
                    kh1 = kslice(par, 0)[:, g * 1024 + 512: g * 1024 + 1024]
                    kl0 = kslice(par, 1)[:, g * 1024: g * 1024 + 512]
                    kl1 = kslice(par, 1)[:, g * 1024 + 512: g * 1024 + 1024]
                    pe(lambda e, q=qh_ap, k=kh0, pw=pw: e.matmul(
                        pw[:, 0:512], q, k, start=True, stop=False), waits=wts)
                    pe(lambda e, q=qh_ap, k=kh1, pw=pw: e.matmul(
                        pw[:, 512:1024], q, k, start=True, stop=False))
                    pe(lambda e, q=qh_ap, k=kl0, pw=pw: e.matmul(
                        pw[:, 0:512], q, k, start=False, stop=False))
                    pe(lambda e, q=qh_ap, k=kl1, pw=pw: e.matmul(
                        pw[:, 512:1024], q, k, start=False, stop=False))
                    pe(lambda e, q=ql_ap, k=kh0, pw=pw: e.matmul(
                        pw[:, 0:512], q, k, start=False, stop=True))
                    pe(lambda e, q=ql_ap, k=kh1, pw=pw: e.matmul(
                        pw[:, 512:1024], q, k, start=False, stop=True), inc=True)
                    pe_sc = p.val('pe')
                    ewts = [('pe', pe_sc)]
                    if g == 0 and last_colsum_pe:
                        ewts.append(('pe', last_colsum_pe))
                    eoff = ((h * 8 + pnl) % 2) * S
                    act(lambda e, g=g, pw=pw, eoff=eoff: e.activation(
                        out=h1h_t[:, eoff + g * 1024: eoff + (g + 1) * 1024],
                        in_=pw[:], func=AF.Exp, scale=INV_SQRT_DH, bias=biasC[:],
                        accum_out=zpart[:, g:g + 1]), waits=ewts, inc=True)
                    exp_ac[(h, pnl, g)] = p.val('ac')
                act(lambda e: e.copy(smax[:, 216:217], zpart[:, 0:1]), inc=True)
                ac_exp = p.val('ac')
                if pending_colsum[0] is not None:
                    pending_colsum[0]()
                    pending_colsum[0] = None
                    last_colsum_pe = p.val('pe')
                if debug_dram and h == 0 and pnl == 0:
                    dma('gpsimd', 'wG', e0_dbg.ap(), E_t[:, 0:S],
                        waits=[('ac', p.val('ac'))])
                dve(lambda e: e.tensor_reduce(zrowv[:], zpart[:], axis=AX, op=OP.add),
                    waits=[('ac', ac_exp), ('dv', dv_consts),
                           (('pe', last_colsum_pe) if last_colsum_pe else None)],
                    inc=True)
                dve(lambda e: e.reciprocal(zrecip[:], zrowv[:]),
                    waits=[('dv', p.val('dv'))], inc=True)
                if debug_dram and h == 0:
                    dve(lambda e, pnl=pnl: e.tensor_copy(
                        smax[:, 200 + pnl: 201 + pnl], zrowv[:]))
                    dve(lambda e, pnl=pnl: e.tensor_copy(
                        smax[:, 208 + pnl: 209 + pnl], zrecip[:]))
                dve(lambda e: e.tensor_copy(zrh_b[:], zrecip[:]),
                    waits=[('dv', p.val('dv'))], inc=True)
                dve(lambda e: e.tensor_tensor(zrl_b[:], zrecip[:], zrh_b[:],
                                              op=OP.subtract),
                    waits=[('dv', p.val('dv'))], inc=True)
                dve(lambda e: e.engine_nop(), waits=[('dv', p.val('dv'))])
                if debug_dram and h == 0:
                    dve(lambda e, pnl=pnl: e.tensor_copy(
                        smax[:, 230 + pnl: 231 + pnl], zrh_b[:]))
                    dve(lambda e, pnl=pnl: e.tensor_copy(
                        smax[:, 240 + pnl: 241 + pnl], zrl_b[:]))
                for m in range(4):
                    dve(lambda e, m=m: e.tensor_copy(zrselH[m][:, m:m + 1], zrh_b[:]))
                    dve(lambda e, m=m: e.tensor_copy(zrselL[m][:, m:m + 1], zrl_b[:]),
                        inc=(m == 3))
                dv_zr = p.val('dv')
                eoff2 = ((h * 8 + pnl) % 2) * S
                def emit_colsum(pnl=pnl, dv_zr=dv_zr, csgate=dict(csel_done),
                                eoff=eoff2):
                    for j in range(16):
                        wts = [('dv', dv_zr)]
                        if pnl == 0 and (j // 4) in csgate:
                            wts.append(csgate[j // 4])
                        pe(lambda e, j=j, pnl=pnl, eoff=eoff: e.matmul(
                            psCS[j // 4][0:4, :], zrselH[j % 4][:],
                            h1h_t[:, eoff + j * 512: eoff + (j + 1) * 512],
                            start=(pnl == 0 and j % 4 == 0), stop=False),
                           waits=wts)
                        pe(lambda e, j=j, pnl=pnl, eoff=eoff: e.matmul(
                            psCS[j // 4][0:4, :], zrselL[j % 4][:],
                            h1h_t[:, eoff + j * 512: eoff + (j + 1) * 512],
                            start=False, stop=(pnl == 7 and j % 4 == 3)),
                           inc=(j == 15))
                pending_colsum[0] = emit_colsum
            if pending_colsum[0] is not None:
                pending_colsum[0]()
                pending_colsum[0] = None
                last_colsum_pe = p.val('pe')
            head_end_pe[h] = last_colsum_pe
            for bb in range(4):
                ewts = [('pe', last_colsum_pe)] if bb == 0 else []
                if bb == 0 and h - 1 in cs_dma_wG:
                    ewts.append(('wG', cs_dma_wG[h - 1]))
                act(lambda e, bb=bb: e.activation(
                    out=cs_stage[:, bb * 512:(bb + 1) * 512],
                    in_=psCS[bb][0:4, :], func=AF.Copy), waits=ewts, inc=True)
                csel_done[bb] = ('ac', p.val('ac'))
            ac_cse = p.val('ac')
            if debug_dram and h == 0:
                dma('gpsimd', 'wG', cs0_dbg.ap(), cs_stage[:, :],
                    waits=[('ac', ac_cse)])
            for j in range(16):
                csrc = cs_stage[j % 4: j % 4 + 1,
                                512 * (j // 4): 512 * (j // 4) + 512]
                dst = cs_rs_in[j // 2, h, 512 * (j % 2): 512 * (j % 2) + 512]
                dma('gpsimd', 'wG', dst, csrc,
                    waits=([('ac', ac_cse)] if j == 0 else []))
            cs_dma_wG[h] = p.val('wG')
        wG_cs = p.val('wG')

        gp(lambda e: e.collective_compute(
            "ReduceScatter", OP.add, replica_groups=[list(range(R))],
            ins=[cs_rs_in.ap().opt()], outs=[cs_rs_out.ap().opt()]),
           waits=[('wG', wG_cs)], inc=('cc', 1))
        cc_cs = p.val('cc')

        if debug_dram:
            dma('gpsimd', 'wG', z2_dbg.ap(), smax[:, 224:248],
                waits=[('dv', p.val('dv'))])
            dma('gpsimd', 'wG', z0_dbg.ap(), smax[:, 200:216],
                waits=[('dv', p.val('dv'))])
            dma('gpsimd', 'wG', xstg2[0:16, 1024:2048], cs_rs_out.ap(),
                waits=[('cc', cc_cs)])
            dma('gpsimd', 'wG', cs_dbg.ap(), xstg2[0:16, 1024:2048],
                waits=[('wG', p.val('wG'))])

        stop('D')
        val_spill_wG = {}
        pe_csb_prev = None
        dv_val_prev = None
        for h in range(NH):
            par = h % 2
            vwts = [('pe', head_end_pe[max(head_end_pe)])] if h == 0 else []
            if h >= 2:
                vwts.append(('wG', val_spill_wG[h - 2]))
            dma('scalar', 'wV', scrh[:, par * T:(par + 1) * T], v_sp[0, h],
                waits=vwts)
            dma('scalar', 'wV', scrl[:, par * T:(par + 1) * T], v_sp[1, h])
            wV_v = p.val('wV')
            mvwts = [('cc', cc_cs)]
            if pe_csb_prev is not None:
                mvwts.append(('pe', pe_csb_prev))
            dma('gpsimd', 'wG', cs_mv, cs_rs_out[h:h + 1, :], waits=mvwts)
            wG_mv = p.val('wG')
            for thf in range(2):
                bwts = [('wG', wG_mv)]
                if dv_val_prev is not None:
                    bwts.append(('dv', dv_val_prev))
                pe(lambda e, thf=thf: e.matmul(
                    psW0[:, thf * 512:(thf + 1) * 512], ones_row[:],
                    cs_mv[:, thf * 512:(thf + 1) * 512], start=True, stop=True),
                   waits=(bwts if thf == 0 else []), inc=(thf == 1))
            pe_csb = p.val('pe')
            pe_csb_prev = pe_csb
            dve(lambda e, par=par: e.tensor_tensor(
                scr32[:], scrh[:, par * T:(par + 1) * T],
                scrl[:, par * T:(par + 1) * T], op=OP.add), waits=[('wV', wV_v)])
            dve(lambda e: e.tensor_tensor(scr32[:, 0:512], scr32[:, 0:512],
                                          psW0[:, 0:512], op=OP.mult),
                waits=[('pe', pe_csb)])
            dve(lambda e: e.tensor_tensor(scr32[:, 512:1024], scr32[:, 512:1024],
                                          psW0[:, 512:1024], op=OP.mult), inc=True)
            dv_val = p.val('dv')
            dv_val_prev = dv_val
            act(lambda e, par=par: e.activation(
                out=scrh[:, par * T:(par + 1) * T], in_=scr32[:], func=AF.Copy),
                waits=[('dv', dv_val)], inc=True)
            ac_vh = p.val('ac')
            dve(lambda e, par=par: e.tensor_tensor(
                scrl[:, par * T:(par + 1) * T], scr32[:],
                scrh[:, par * T:(par + 1) * T], op=OP.subtract),
                waits=[('ac', ac_vh)], inc=True)
            dv_vl = p.val('dv')
            dma('gpsimd', 'wG', val_sp[0, h], scrh[:, par * T:(par + 1) * T],
                waits=[('dv', dv_vl)])
            dma('gpsimd', 'wG', val_sp[1, h], scrl[:, par * T:(par + 1) * T])
            val_spill_wG[h] = p.val('wG')
        wG_val = p.val('wG')

        op_evac = {}
        op_tile_pe = {}
        x1s_dv = {}
        x1_spill = {}
        x1_sq_ac = {}
        x1_stat_pe = {}
        for th in range(2):
            for dh in range(NH):
                dma('scalar', 'wV', vth[:, dh * 512: dh * 512 + 512],
                    val_sp[0, dh][:, th * 512:(th + 1) * 512],
                    waits=([('wG', wG_val)] if (th == 0 and dh == 0) else []))
                dma('scalar', 'wV', vth[:, 16384 + dh * 512: 16384 + dh * 512 + 512],
                    val_sp[1, dh][:, th * 512:(th + 1) * 512])
            wV_vth = p.val('wV')
            for oc in range(NCD):
                i = th * NCD + oc
                b = i % 2
                dv_w = stream_weight(wo[:, oc * 128:(oc + 1) * 128], b,
                                     pe_gate=op_tile_pe.get(i - 2))
                pidx = i % 2
                pst = (ps0 if pidx == 0 else ps1).ap()
                buf = scr32b if i % 2 == 0 else scr32
                wts = [('dv', dv_w), ('wV', wV_vth)]
                if pidx in op_evac:
                    wts.append(op_evac[pidx])
                for dh in range(NCD):
                    mm3(pst, b, dh,
                        vth[:, dh * 512: dh * 512 + 512],
                        vth[:, 16384 + dh * 512: 16384 + dh * 512 + 512],
                        dh == 0, dh == NCD - 1, wts)
                pe_op = p.val('pe')
                op_tile_pe[i] = pe_op
                xs = xstg if i % 2 == 0 else xstg2
                xwts = [('wG', wG_xT), ('wG', wG_cs)] if i == 0 else []
                if i - 2 in x1s_dv:
                    xwts.append(('dv', x1s_dv[i - 2]))
                dma('sync', 'wA', xs[:, 0:512], xT_sp[oc][:, th * 512:(th + 1) * 512],
                    waits=xwts)
                wA_x1 = p.val('wA')
                swts = [('pe', pe_op), ('wA', wA_x1)]
                if i - 2 in x1_spill:
                    swts += [('wG', x1_spill[i - 2]), ('ac', x1_sq_ac[i - 2]),
                             ('pe', x1_stat_pe[i - 2])]
                dve(lambda e, pst=pst, oc=oc, xs=xs, buf=buf: e.scalar_tensor_tensor(
                    buf[:, 0:512], pst, mcol(32, oc), xs[:, 0:512],
                    op0=OP.mult, op1=OP.add), waits=swts, inc=True)
                dv_x1 = p.val('dv')
                x1s_dv[i] = dv_x1
                op_evac[pidx] = ('dv', dv_x1)
                dma('gpsimd', 'wG', x1_sp[oc][:, th * 512:(th + 1) * 512],
                    buf[:, 0:512], waits=[('dv', dv_x1)])
                x1_spill[i] = p.val('wG')
                aqw = [('dv', dv_x1)]
                if i - 1 in x1_stat_pe:
                    aqw.append(('pe', x1_stat_pe[i - 1]))
                act(lambda e, buf=buf: e.activation(out=sqs[:, 0:512], in_=buf[:, 0:512],
                                                    func=AF.Square), waits=aqw, inc=True)
                ac_sq = p.val('ac')
                x1_sq_ac[i] = ac_sq
                pe(lambda e, th=th, oc=oc, buf=buf: e.matmul(
                    (ps2 if th == 0 else ps3)[0:1, :], ones_col[:], buf[:, 0:512],
                    start=(oc == 0), stop=(oc == NCD - 1)), waits=[('dv', dv_x1)])
                pe(lambda e, th=th, oc=oc: e.matmul(
                    psW1[0:1, th * 512:(th + 1) * 512], ones_col[:], sqs[:, 0:512],
                    start=(oc == 0), stop=(oc == NCD - 1)),
                   waits=[('ac', ac_sq)], inc=True)
                x1_stat_pe[i] = p.val('pe')
        wG_x1 = p.val('wG')

        pe_bc2 = ln_finish()

        stop('E')
        h2f_ac = {}
        h2s_dv = {}
        for dc in range(NCD):
            xs = xstg if dc % 2 == 0 else xstg2
            xwts = [('wG', wG_x1)] if dc == 0 else []
            if dc - 2 in h2s_dv:
                xwts.append(('dv', h2s_dv[dc - 2]))
            dma('sync', 'wA', xs[:, 0:T], x1_sp[dc], waits=xwts)
            wA_h2 = p.val('wA')
            buf = scr32b if dc % 2 == 0 else sqs
            wts = [('pe', pe_bc2), ('wA', wA_h2)]
            if dc - 2 >= 0:
                wts.append(('ac', h2f_ac[dc - 2]))
            for hf in range(2):
                dve(lambda e, xs=xs, hf=hf: e.tensor_tensor(
                    scr32[:, hf * 512:(hf + 1) * 512],
                    xs[:, hf * 512: hf * 512 + 512],
                    psW0[:, hf * 512:(hf + 1) * 512], op=OP.mult),
                    waits=(wts if hf == 0 else []))
                dve(lambda e, hf=hf: e.tensor_tensor(
                    scr32[:, hf * 512:(hf + 1) * 512], scr32[:, hf * 512:(hf + 1) * 512],
                    psW1[:, hf * 512:(hf + 1) * 512], op=OP.subtract))
            dve(lambda e, dc=dc, buf=buf: e.tensor_scalar(
                buf[:, 0:T], scr32[:], mcol(64, dc), mcol(48, dc),
                op0=OP.mult, op1=OP.add), inc=True)
            dv_h2 = p.val('dv')
            h2s_dv[dc] = dv_h2
            act(lambda e, dc=dc, buf=buf: e.activation(
                out=h1h_t[:, dc * T:(dc + 1) * T], in_=buf[:, 0:T], func=AF.Copy),
                waits=[('dv', dv_h2)], inc=True)
            h2f_ac[dc] = p.val('ac')
            dve(lambda e, dc=dc, buf=buf: e.tensor_tensor(
                h1l_t[:, dc * T:(dc + 1) * T], buf[:, 0:T],
                h1h_t[:, dc * T:(dc + 1) * T], op=OP.subtract),
                waits=[('ac', h2f_ac[dc])], inc=True)
        dv_h2_done = p.val('dv')

        dma('sync', 'wA', rw_sb, rwd.rearrange("(a b) c -> b a c", b=128),
            waits=[('dv', stream_weight.stg_dv)])
        wA_rw = p.val('wA')
        act(lambda e: e.activation(out=rwh[:], in_=rw_sb, func=AF.Copy),
            waits=[('wA', wA_rw)], inc=True)
        ac_rwh = p.val('ac')
        dve(lambda e: e.tensor_tensor(rwl[:], rw_sb, rwh[:], op=OP.subtract),
            waits=[('ac', ac_rwh)], inc=True)
        dv_rw = p.val('dv')
        stream_weight.stg_dv = dv_rw

        for th in range(2):
            pst = psW0.ap()[0:16, th * 512:(th + 1) * 512]
            wts = [('dv', max(dv_rw, dv_h2_done))]
            for dc in range(NCD):
                rh = rwh[:, dc * 16:(dc + 1) * 16]
                rl = rwl[:, dc * 16:(dc + 1) * 16]
                mh = h1h_t[:, dc * T + th * 512: dc * T + th * 512 + 512]
                ml = h1l_t[:, dc * T + th * 512: dc * T + th * 512 + 512]
                pe(lambda e, rh=rh, m=mh, pst=pst, st=(dc == 0): e.matmul(
                    pst, rh, m, start=st, stop=False),
                   waits=(wts if dc == 0 else []))
                pe(lambda e, rh=rh, m=ml, pst=pst: e.matmul(
                    pst, rh, m, start=False, stop=False))
                pe(lambda e, rl=rl, m=mh, pst=pst, sp=(dc == NCD - 1): e.matmul(
                    pst, rl, m, start=False, stop=sp), inc=(dc == NCD - 1))
        pe_log = p.val('pe')
        act(lambda e: e.activation(out=logT[:], in_=psW0[0:16, :], func=AF.Copy),
            waits=[('pe', pe_log)], inc=True)
        ac_log0 = p.val('ac')
        dve(lambda e: e.tensor_scalar(logT[:], logT[:], rbT[:], None, op0=OP.add),
            waits=[('ac', ac_log0)], inc=True)
        dv_log = p.val('dv')

        sm_evac = {}
        for tc in range(8):
            pb = ps2 if tc % 2 == 0 else ps3
            wts = [('dv', dv_log)]
            if tc - 2 in sm_evac:
                wts.append(('ac', sm_evac[tc - 2]))
            pe(lambda e, tc=tc, pb=pb: e.transpose(
                pb[:, 0:16], logT[0:16, tc * 128:(tc + 1) * 128], ident_s[0:16, 0:16]),
               waits=wts, inc=True)
            pe_ltr = p.val('pe')
            dve(lambda e, tc=tc, pb=pb: e.tensor_reduce(
                smax[:, tc * 3: tc * 3 + 1], pb[:, 0:16], axis=AX, op=OP.max,
                negate=True), waits=[('pe', pe_ltr)], inc=True)
            dv_mx = p.val('dv')
            act(lambda e, tc=tc, pb=pb: e.activation(
                out=smax[:, 128 + tc * 16: 128 + tc * 16 + 16], in_=pb[:, 0:16],
                func=AF.Exp, bias=smax[:, tc * 3: tc * 3 + 1],
                accum_out=smax[:, tc * 3 + 1: tc * 3 + 2]),
                waits=[('dv', dv_mx)], inc=True)
            ac_ex = p.val('ac')
            dve(lambda e, tc=tc: e.reciprocal(smax[:, tc * 3 + 2: tc * 3 + 3],
                                              smax[:, tc * 3 + 1: tc * 3 + 2]),
                waits=[('ac', ac_ex)], inc=True)
            dve(lambda e, tc=tc: e.tensor_scalar(
                smax[:, 128 + tc * 16: 128 + tc * 16 + 16],
                smax[:, 128 + tc * 16: 128 + tc * 16 + 16],
                smax[:, tc * 3 + 2: tc * 3 + 3], None, op0=OP.mult),
                waits=[('dv', p.val('dv'))], inc=True)
            dv_sm = p.val('dv')
            pe(lambda e, tc=tc: e.transpose(
                psW1[0:16, tc * 128:(tc + 1) * 128],
                smax[:, 128 + tc * 16: 128 + tc * 16 + 16], ident_s[:]),
               waits=[('dv', dv_sm)], inc=True)
            pe_str = p.val('pe')
            act(lambda e, tc=tc: e.activation(
                out=ST_sb[0:16, tc * 128:(tc + 1) * 128],
                in_=psW1[0:16, tc * 128:(tc + 1) * 128], func=AF.Copy),
                waits=[('pe', pe_str)], inc=True)
            sm_evac[tc] = p.val('ac')
        ac_ST = p.val('ac')
        dma('gpsimd', 'wG', s_ag_in.ap(), ST_sb[0:16, :], waits=[('ac', ac_ST)])
        wG_S = p.val('wG')
        gp(lambda e: e.collective_compute(
            "AllGather", OP.bypass, replica_groups=[list(range(R))],
            ins=[s_ag_in.ap().opt()], outs=[s_ag_out.ap().opt()]),
           waits=[('wG', wG_S)], inc=('cc', 1))
        cc_S = p.val('cc')
        for r in range(R):
            dma('scalar', 'wV', S_lay[r * 16:(r + 1) * 16, :], s_ag_out[r],
                waits=([('cc', cc_S)] if r == 0 else []))
        wV_slay = p.val('wV')

        dve(lambda e: e.memset(lo16[:], 0.0), waits=[('wV', wV_slay)])
        dve(lambda e: e.memset(hi16[:], 1.0))
        dve(lambda e: e.memset(mid16[:], 0.5))
        dve(lambda e: e.memset(thr128[:], 0.5), inc=True)

        stop('F')
        bis_rc = {}
        bis_mid = {}
        bis_cnt_pe = {}
        bis_bc_pe = {}

        def bisect_dve_cmp(i):
            wts = [('wV', wV_slay)] if i == 0 else [('pe', bis_bc_pe[i - 1])]
            dve(lambda e: e.tensor_scalar(cmp_t[:], S_lay[:], thr128[:], None,
                                          op0=OP.is_ge), waits=wts)
            dve(lambda e: e.tensor_reduce(rc_t[:], cmp_t[:], axis=AX, op=OP.add),
                inc=True)
            bis_rc[i] = p.val('dv')

        def bisect_dve_upd(i):
            dve(lambda e: e.tensor_scalar(pred16[:], ps1[0:16, 0:1], c512[:], None,
                                          op0=OP.is_ge), waits=[('pe', bis_cnt_pe[i])],
                inc=True)
            dve(lambda e: e.select(lo16[:], pred16[:], mid16[:], lo16[:]),
                waits=[('dv', p.val('dv'))], inc=True)
            dve(lambda e: e.select(hi16[:], pred16[:], hi16[:], mid16[:]),
                inc=True)
            dve(lambda e: e.tensor_tensor(sum16[:], lo16[:], hi16[:], op=OP.add),
                waits=[('dv', p.val('dv'))], inc=True)
            dve(lambda e: e.tensor_scalar(mid16[:], sum16[:], half16[:], None,
                                          op0=OP.mult), waits=[('dv', p.val('dv'))],
                inc=True)
            bis_mid[i] = p.val('dv')

        def bisect_dve_thr(i):
            dve(lambda e: e.tensor_copy(thr128[:], ps3[:, 0:1]),
                waits=[('pe', bis_bc_pe[i])], inc=True)
            dve(lambda e: e.engine_nop(), waits=[('dv', p.val('dv'))], inc=True)

        if f_mode >= 1:
            bisect_dve_cmp(0)
        bstate = {'i': 0, 'stage': 0}

        def maybe_bisect_pe():
            i = bstate['i']
            if i >= N_BISECT:
                return
            if bstate['stage'] == 0:
                if bis_rc.get(i) is None:
                    return
                pe(lambda e: e.matmul(ps1[0:16, 0:1], selB_s[:], rc_t[:],
                                      start=True, stop=True),
                   waits=[('dv', bis_rc[i])], inc=True)
                bis_cnt_pe[i] = p.val('pe')
                bisect_dve_upd(i)
                bstate['stage'] = 1
            else:
                pe(lambda e: e.matmul(ps3[:, 0:1], sel2_s[:], mid16[:],
                                      start=True, stop=True),
                   waits=[('dv', bis_mid[i])], inc=True)
                bis_bc_pe[i] = p.val('pe')
                bisect_dve_thr(i)
                bstate['stage'] = 0
                bstate['i'] += 1
                if bstate['i'] < N_BISECT:
                    bisect_dve_cmp(bstate['i'])

        exp_evac = {}
        ex_tile_pe = {}
        for th in range(2):
            for oc in range(NCD):
                i = th * NCD + oc
                b = i % 2
                dv_w = stream_weight(wed[:, oc * 128:(oc + 1) * 128], b,
                                     pe_gate=ex_tile_pe.get(i - 2))
                pidx = i % 2
                pst = (psW0 if pidx == 0 else psW1).ap()[:, 0:512]
                wts = [('dv', max(dv_w, dv_h2_done))]
                if pidx in exp_evac:
                    wts.append(exp_evac[pidx])
                for dc in range(NCD):
                    mm3(pst, b, dc,
                        h1h_t[:, dc * T + th * 512: dc * T + th * 512 + 512],
                        h1l_t[:, dc * T + th * 512: dc * T + th * 512 + 512],
                        dc == 0, dc == NCD - 1, wts)
                    if f_mode >= 1:
                        maybe_bisect_pe()
                pe_ex = p.val('pe')
                ex_tile_pe[i] = pe_ex
                if f_mode == -1:
                    act(lambda e: e.activation(out=scr32b[:, 0:512],
                                               in_=scr32b[:, 0:512], func=AF.Copy),
                        waits=[('pe', pe_ex)], inc=True)
                elif f_mode == -2:
                    act(lambda e, pst=pst: e.activation(
                        out=scr32b[:, 0:512], in_=pst, func=AF.Copy),
                        waits=[('pe', pe_ex)], inc=True)
                elif f_mode == -3:
                    act(lambda e, oc=oc, th=th: e.activation(
                        out=U2[:, oc * T + th * 512: oc * T + th * 512 + 512],
                        in_=scr32b[:, 0:512], func=AF.Copy),
                        waits=[('pe', pe_ex)], inc=True)
                else:
                    act(lambda e, oc=oc, th=th, pst=pst: e.activation(
                        out=U2[:, oc * T + th * 512: oc * T + th * 512 + 512],
                        in_=pst, func=AF.Copy), waits=[('pe', pe_ex)], inc=True)
                exp_evac[pidx] = ('ac', p.val('ac'))
        ac_U2 = p.val('ac')
        if f_mode >= 1:
            while bstate['i'] < N_BISECT:
                maybe_bisect_pe()
        dv_bis = p.val('dv')
        if f_mode < 2:
            raise _StopEmit()

        if "thr" in dbg:
            dma('gpsimd', 'wG', dbg["thr"], lo16[:], waits=[('dv', dv_bis)])

        dve(lambda e: e.scalar_tensor_tensor(
            masked[:], ST_sb[0:16, :], lo16[:], ST_sb[0:16, :],
            op0=OP.is_ge, op1=OP.mult),
            waits=[('dv', dv_bis), ('pe', bis_bc_pe[N_BISECT - 1])], inc=True)
        dv_msk = p.val('dv')
        for thf in range(2):
            wps = ps2 if thf == 0 else ps3
            pe(lambda e, thf=thf, wps=wps: e.matmul(
                wps[0:1, :], ones16[:], masked[0:16, thf * 512:(thf + 1) * 512],
                start=True, stop=True),
               waits=([('dv', dv_msk)] if thf == 0 else []), inc=True)
            pe_w = p.val('pe')
            act(lambda e, thf=thf, wps=wps: e.activation(
                out=w_sb[:, thf * 512:(thf + 1) * 512], in_=wps[0:1, :],
                func=AF.Copy), waits=[('pe', pe_w)], inc=True)
        ac_w = p.val('ac')
        for thf in range(2):
            pe(lambda e, thf=thf: e.matmul(
                (ps0 if thf == 0 else ps1)[:, :], ones_row[:],
                w_sb[0:1, thf * 512:(thf + 1) * 512], start=True, stop=True),
               waits=([('ac', ac_w)] if thf == 0 else []), inc=(thf == 1))
        pe_wbc = p.val('pe')
        wbc_sb = xstg2.ap()[:, 1024:2048]
        dve(lambda e: e.tensor_copy(wbc_sb[:, 0:512], ps0[:]),
            waits=[('pe', pe_wbc)])
        dve(lambda e: e.tensor_copy(wbc_sb[:, 512:1024], ps1[:]), inc=True)
        dv_wbc = p.val('dv')
        if "w" in dbg:
            dma('gpsimd', 'wG', dbg["w"], w_sb[0:1, :], waits=[('ac', ac_w)])
        if "msk" in dbg:
            dma('gpsimd', 'wG', dbg["msk"], masked[0:16, :], waits=[('dv', dv_msk)])
        if f_mode < 3:
            raise _StopEmit()

        ep_tr_ac = {}
        ep_sub_ac = {}
        ep_dv = {}
        for th in range(2):
            for oc in range(NCD):
                i = th * NCD + oc
                xs = xstg if i % 2 == 0 else xstg2
                buf = scr32 if i % 2 == 0 else scr32b
                xwts = []
                if i - 2 in ep_dv:
                    xwts.append(('dv', ep_dv[i - 2]))
                dma('sync', 'wA', xs[:, 0:512], x1_sp[oc][:, th * 512:(th + 1) * 512],
                    waits=xwts)
                wA_x1b = p.val('wA')
                wbc = wbc_sb[:, th * 512:(th + 1) * 512]
                ewts = [('dv', dv_wbc), ('ac', ac_U2), ('wA', wA_x1b)]
                if i - 2 in ep_tr_ac:
                    ewts.append(('ac', ep_tr_ac[i - 2]))
                dve(lambda e, oc=oc, th=th, wbc=wbc, buf=buf: e.scalar_tensor_tensor(
                    buf[:, 0:512], U2[:, oc * T + th * 512: oc * T + th * 512 + 512],
                    beT[:, oc:oc + 1], wbc, op0=OP.add, op1=OP.mult),
                    waits=ewts)
                dve(lambda e, oc=oc, xs=xs, buf=buf: e.scalar_tensor_tensor(
                    buf[:, 0:512], buf[:, 0:512], mcol(80, oc), xs[:, 0:512],
                    op0=OP.mult, op1=OP.add), inc=True)
                dv_out = p.val('dv')
                ep_dv[i] = dv_out
                for sub in range(4):
                    si = i * 4 + sub
                    pb = ps2 if si % 2 == 0 else ps3
                    twts = [('dv', dv_out)] if sub == 0 else []
                    if si - 2 in ep_sub_ac:
                        twts.append(('ac', ep_sub_ac[si - 2]))
                    pe(lambda e, sub=sub, pb=pb, buf=buf: e.transpose(
                        pb[:, 0:128], buf[:, sub * 128:(sub + 1) * 128], ident_s[:]),
                       waits=twts, inc=True)
                    pe_tr = p.val('pe')
                    awts = [('pe', pe_tr)]
                    if th == 1 and oc == 0 and sub == 0:
                        awts.append(('wOUT', th0_out_wOUT))
                    act(lambda e, sub=sub, oc=oc, pb=pb: e.activation(
                        out=OSTG4[:, sub * D + oc * 128: sub * D + (oc + 1) * 128],
                        in_=pb[:, 0:128], func=AF.Copy),
                        waits=awts, inc=True)
                    ep_sub_ac[si] = p.val('ac')
                ep_tr_ac[i] = p.val('ac')
            ac_ep = p.val('ac')
            for sub in range(4):
                dma('gpsimd', 'wOUT',
                    out_d[th * 512 + sub * 128: th * 512 + (sub + 1) * 128, :],
                    OSTG4[:, sub * D:(sub + 1) * D],
                    waits=([('ac', ac_ep)] if sub == 0 else []))
            th0_out_wOUT = p.val('wOUT')
        wOUT_done = p.val('wOUT')

    except _StopEmit:
        pass

    sem_names = sorted(p.counts.keys())
    with contextlib.ExitStack() as stack:
        sems = {s_: stack.enter_context(nc.semaphore(s_)) for s_ in sem_names}
        block = stack.enter_context(nc.Block())

        def runner_for(name):
            def runner(eng):
                for (en, waits, fn, incs) in p.items:
                    if en != name:
                        continue
                    for (s_, v_) in waits:
                        if v_ > 0:
                            eng.wait_ge(sems[s_], v_)
                    inst = fn(eng)
                    for (s_, amt) in incs:
                        inst.then_inc(sems[s_], amt)
                if name == 'gpsimd':
                    for s_ in ('wA', 'wV', 'wG', 'wOUT'):
                        if s_ in sems and p.val(s_) > 0:
                            eng.wait_ge(sems[s_], p.val(s_))
            return runner

        block.tensor(runner_for('tensor'))
        block.vector(runner_for('vector'))
        block.scalar(runner_for('scalar'))
        block.sync(runner_for('sync'))
        block.gpsimd(runner_for('gpsimd'))

    return nc, dbg


def _host_constants():
    ident = np.eye(128, dtype=np.float32)
    selB = np.zeros((128, 16), dtype=np.float32)
    for pp in range(128):
        selB[pp, pp % 16] = 1.0
    sel2 = np.zeros((16, 128), dtype=np.float32)
    for pp in range(128):
        sel2[pp % 16, pp] = 1.0
    return ident, selB, sel2


def make_in_maps(inputs):
    x = np.ascontiguousarray(np.asarray(inputs["L0_attn_x"], dtype=np.float32)[0])
    wqkv = np.ascontiguousarray(inputs["L0_attn_w_qkv"], dtype=np.float32)
    wo_ = np.ascontiguousarray(inputs["L0_attn_w_o"], dtype=np.float32)
    rw = np.ascontiguousarray(inputs["L0_ffn_router_w"], dtype=np.float32)
    rb = np.ascontiguousarray(
        np.asarray(inputs["L0_ffn_router_b"], dtype=np.float32).reshape(NE, 1))
    we = np.ascontiguousarray(inputs["L0_ffn_expert_w"], dtype=np.float32)
    be = np.ascontiguousarray(
        np.asarray(inputs["L0_ffn_expert_b"], dtype=np.float32).reshape(-1))
    cv = np.ascontiguousarray(np.asarray(inputs["L0_c"], dtype=np.float32).reshape(-1))
    wmod = np.ascontiguousarray(inputs["L0_wmod"], dtype=np.float32)
    ident, selB, sel2 = _host_constants()
    maps = []
    for r in range(R):
        maps.append({
            "x_slice": x[r * T:(r + 1) * T],
            "w_qkv": wqkv, "w_o": wo_,
            "router_w": rw, "router_b": rb,
            "expert_w": we, "expert_b": be,
            "c_vec": cv,
            "wmod_slice": np.ascontiguousarray(wmod[:, r * 1536:(r + 1) * 1536]),
            "ident": ident, "selB": selB, "sel2": sel2,
        })
    return maps


def kernel(L0_attn_x, L0_attn_w_qkv, L0_attn_w_o, L0_ffn_router_w, L0_ffn_router_b,
           L0_ffn_expert_w, L0_ffn_expert_b, L0_c, L0_wmod):
    inputs = dict(L0_attn_x=L0_attn_x, L0_attn_w_qkv=L0_attn_w_qkv,
                  L0_attn_w_o=L0_attn_w_o, L0_ffn_router_w=L0_ffn_router_w,
                  L0_ffn_router_b=L0_ffn_router_b, L0_ffn_expert_w=L0_ffn_expert_w,
                  L0_ffn_expert_b=L0_ffn_expert_b, L0_c=L0_c, L0_wmod=L0_wmod)
    nc, _ = build_program()
    res = run_bass_kernel_spmd(nc, make_in_maps(inputs), core_ids=list(range(R)))
    out = np.concatenate([res.results[r]["out_slice"] for r in range(R)], axis=0)
    return np.ascontiguousarray(out[None]).astype(np.float32, copy=False)
